# version 11
# baseline (speedup 1.0000x reference)
import numpy as np
from contextlib import ExitStack

import concourse.bass as bass
import concourse.mybir as mybir
from concourse.bass_utils import run_bass_kernel_spmd

F32 = mybir.dt.float32
F16 = mybir.dt.float16
AF = mybir.ActivationFunctionType
OP = mybir.AluOpType
AX = mybir.AxisListType

NTOK = 4096
T = 512
NT = NTOK // T
D = 1024
DB = D // 128
H = 4096
HF = H // 128
HEADS = 16
HD = 64
FEAT = 256
QC = DB
M_QKV = 3 * DB
EPS_LN = 1e-5
EPS_Z = 1e-8

ENGINES = ("sync", "tensor", "vector", "scalar", "gpsimd")
ALL_PHASES = ("ffn1", "attn", "ffn2")


class Prog:

    def __init__(self, nc, stack):
        self.nc = nc
        self.stack = stack
        self.q = {e: [] for e in ENGINES}
        self.sems = {}
        self.cnt = {}

    def sem(self, *names):
        for n in names:
            self.sems[n] = self.stack.enter_context(self.nc.semaphore(n))
            self.cnt[n] = 0

    def do(self, eng, meth, *args, sem=None, amt=1, **kw):
        s = self.sems[sem] if sem is not None else None
        if sem is not None:
            self.cnt[sem] += amt

        def fn(e, meth=meth, args=args, kw=kw, s=s, amt=amt):
            inst = getattr(e, meth)(*args, **kw)
            if s is not None:
                inst.then_inc(s, amt)

        self.q[eng].append(fn)
        return self.cnt.get(sem)

    def wait(self, eng, sem, val=None):
        v = self.cnt[sem] if val is None else val
        if v <= 0:
            return
        s = self.sems[sem]
        self.q[eng].append(lambda e, s=s, v=v: e.wait_ge(s, v))

    def mark(self, sem):
        return self.cnt[sem]

    def flush_block(self, name=None):
        with self.nc.Block(name) as blk:
            for eng in ENGINES:
                fns = self.q[eng]
                if not fns:
                    continue
                getattr(blk, eng)(lambda e, fns=tuple(fns): [f(e) for f in fns])
        self.q = {e: [] for e in ENGINES}


def _build_nc(phases=ALL_PHASES, debug_out=None, taps=False, taps_full=False):
    nc = bass.Bass("TRN2", target_bir_lowering=False)

    x_in = nc.dram_tensor("x", [NTOK, D], F32, kind="ExternalInput").ap()
    w_qkv = nc.dram_tensor("w_qkv", [D, 3 * D], F32, kind="ExternalInput").ap()
    proj = nc.dram_tensor("proj", [HEADS, HD, FEAT], F32, kind="ExternalInput").ap()
    w_out = nc.dram_tensor("w_out", [D, D], F32, kind="ExternalInput").ap()
    b_out = nc.dram_tensor("b_out", [D], F32, kind="ExternalInput").ap()
    w1 = nc.dram_tensor("w1", [H, D], F32, kind="ExternalInput").ap()
    b1 = nc.dram_tensor("b1", [H], F32, kind="ExternalInput").ap()
    w2 = nc.dram_tensor("w2", [D, H], F32, kind="ExternalInput").ap()
    b2 = nc.dram_tensor("b2", [D], F32, kind="ExternalInput").ap()
    eye32 = nc.dram_tensor("eye32", [128, 128], F32, kind="ExternalInput").ap()
    eye16 = nc.dram_tensor("eye16", [128, 128], F32, kind="ExternalInput").ap()
    ones_in = nc.dram_tensor("ones_in", [128, 2], F32, kind="ExternalInput").ap()
    e2_in = nc.dram_tensor("e2_in", [2, 128], F32, kind="ExternalInput").ap()
    zsel_in = nc.dram_tensor("zsel_in", [2, 128], F32, kind="ExternalInput").ap()

    if debug_out is None:
        y = nc.dram_tensor("y", [NTOK, D], F32, kind="ExternalOutput").ap()
    else:
        y = nc.dram_tensor("y", [128, DB, NTOK], F32, kind="ExternalOutput").ap()
    tap_aps = {}
    if taps:
        for nm, shape in (("t_Xf", [128, DB * T]), ("t_xnh", [128, DB * T]),
                          ("t_xnl", [128, DB * T]), ("t_rows", [1, 4 * T]),
                          ("t_a1", [128, HF]), ("t_b1t", [128, HF]),
                          ("t_sc2", [128, DB]), ("t_bb2", [128, DB]),
                          ("t_s1", [128, 1024]), ("t_s2", [128, 1024])):
            tap_aps[nm] = nc.dram_tensor(nm, shape, F32, kind="ExternalOutput").ap()
        for nm in ("t_hh", "t_hl"):
            tap_aps[nm] = nc.dram_tensor(nm, [128, HF * T], F16,
                                         kind="ExternalOutput").ap()
        tap_aps["t_kv"] = nc.dram_tensor("t_kv", [128, HEADS * FEAT], F32,
                                         kind="ExternalOutput").ap()
        for nm, wd in (("t_kv65h", HEADS * 2 * 65), ("t_kv65l", HEADS * 2 * 65),
                       ("t_ath", DB * T), ("t_atl", DB * T),
                       ("t_qsh", DB * T), ("t_qsl", DB * T)):
            tap_aps[nm] = nc.dram_tensor(nm, [128, wd], F16,
                                         kind="ExternalOutput").ap()
        for nm, wd in (("t_kh", DB * T), ("t_kl", DB * T), ("t_vh", 4 * 1040),
                       ("t_vl", 4 * 1040), ("t_qh2", DB * T), ("t_ql2", DB * T),
                       ("t_keh", 4 * FEAT), ("t_kel", 4 * FEAT)):
            tap_aps[nm] = nc.dram_tensor(nm, [128, wd], F16,
                                         kind="ExternalOutput").ap()

    Xa = nc.dram_tensor("Xa", [128, DB, NTOK], F32).ap()
    Xb = nc.dram_tensor("Xb", [128, DB, NTOK], F32).ap()
    s1T = nc.dram_tensor("s1T", [128, HF, DB, 128], F16).ap()
    s2T = nc.dram_tensor("s2T", [128, DB, HF, 128], F16).ap()
    wqh = nc.dram_tensor("wqh", [128, M_QKV, DB, 128], F16).ap()
    wql = nc.dram_tensor("wql", [128, M_QKV, DB, 128], F16).ap()
    woh = nc.dram_tensor("woh", [128, DB, DB, 128], F16).ap()
    wol = nc.dram_tensor("wol", [128, DB, DB, 128], F16).ap()
    projhl = nc.dram_tensor("projhl", [128, 2, HEADS * FEAT], F16).ap()
    qhh = nc.dram_tensor("qhh", [128, QC, NTOK], F16).ap()
    qhl = nc.dram_tensor("qhl", [128, QC, NTOK], F16).ap()

    with ExitStack() as st:
        def sb(name, shape, dt=F32):
            return st.enter_context(nc.sbuf_tensor(name, shape, dt))

        p = Prog(nc, st)
        p.sem("ld", "wr", "sg", "tr", "ev", "vx", "sx", "gx", "mm",
              "row", "hs", "s2l", "gel", "epi",
              *[f"ldx{c}" for c in range(8)],
              *[f"wrx{c}" for c in range(8)],
              *[f"qwr{c}" for c in range(8)],
              "s2a", "s2b", "wqa", "wqb", "s1a", "s1b")

        eyeS = sb("eyeS", [128, 128], F32)
        eyeH = sb("eyeH", [128, 128], F16)
        ones1 = sb("ones1", [128, 2], F32)
        onesr = sb("onesr", [2, 128], F32)
        a1 = sb("a1", [128, HF], F32)
        b1t = sb("b1t", [128, HF], F32)
        sc2 = sb("sc2", [128, DB], F32)
        bb2 = sb("bb2", [128, DB], F32)
        bo = sb("bo", [128, DB], F32)
        zselS = sb("zselS", [2, 128], F32)
        ones16S = sb("ones16S", [128, 1], F16)
        rows = sb("rows", [1, T * 4], F32)
        psum = st.enter_context(nc.psum_tensor("psum", [128, 6, 512], F32))
        psumh = st.enter_context(nc.psum_tensor("psumh", [128, 2, 1024], F16))

        def pbank(i, p0=0, p1=128, f0=0, f1=512):
            return psum[p0:p1, i, f0:f1]

        def phbank(i, p0=0, p1=128, f0=0, f1=128):
            return psumh[p0:p1, i, f0:f1]

        p.do("sync", "dma_start", sem="ld", amt=16, out=eyeS[:], in_=eye32)
        p.do("sync", "dma_start", sem="ld", amt=16, out=ones1[:], in_=ones_in)
        p.do("sync", "dma_start", sem="ld", amt=16, out=onesr[:], in_=e2_in)
        p.do("gpsimd", "dma_start", sem="wr", amt=16, out=eyeH[:], in_=eye16)
        p.do("sync", "dma_start", sem="ld", amt=16, out=zselS[:], in_=zsel_in)
        p.do("gpsimd", "dma_start", sem="wr", amt=16, out=ones16S[:],
             in_=ones_in[:, 0:1], allow_slow_non_contiguous=True)

        with ExitStack() as st0:
            def sb0(name, shape, dt=F32):
                return st0.enter_context(nc.sbuf_tensor(name, shape, dt))

            wld = [sb0(f"wld{i}", [128, H], F32) for i in range(2)]
            sgn = [sb0(f"sgn{i}", [128, H], F16) for i in range(2)]
            stg = sb0("stg", [128, HF * 128], F16)
            hi_t = sb0("hi_t", [128, 4 * D], F16)
            lo_t = sb0("lo_t", [128, 4 * D], F16)
            araw = sb0("araw", [128, HF], F32)
            brow = sb0("brow", [HF, 128], F32)

            p.wait("vector", "ld")
            p.wait("scalar", "ld")
            p.wait("tensor", "ld")
            p.wait("tensor", "wr")

            def prep_sign(n_jb, n_db, w_ap, scr_ap, a_tile):
                cons = {}
                bank_ev = {}
                wr_marks = {}
                for jb in range(n_jb):
                    buf = jb % 2
                    if jb >= 2:
                        vxm, sgm, trm = cons[jb - 2]
                        p.wait("sync", "vx", vxm)
                        p.wait("sync", "sg", sgm)
                    else:
                        p.wait("sync", "vx")
                        p.wait("sync", "sg")
                    ldc = p.do("sync", "dma_start", sem="ld", amt=16,
                               out=wld[buf][:, 0:n_db * 128],
                               in_=w_ap[jb * 128:(jb + 1) * 128, :])
                    p.wait("vector", "ld", ldc)
                    vxm = p.do("vector", "reduce_sum", a_tile[:, jb:jb + 1],
                               wld[buf][:, 0:n_db * 128], axis=AX.X,
                               apply_absolute_value=True, sem="vx")
                    p.wait("scalar", "ld", ldc)
                    if jb >= 2:
                        p.wait("scalar", "tr", cons[jb - 2][2])
                    if jb >= 1:
                        p.wait("scalar", "wr", wr_marks[jb - 1])
                    sgm = p.do("scalar", "sign", sgn[buf][:, 0:n_db * 128],
                               wld[buf][:, 0:n_db * 128], sem="sg")
                    p.wait("tensor", "sg", sgm)
                    for db in range(n_db):
                        bk = db % 2
                        if bk in bank_ev:
                            p.wait("tensor", "ev", bank_ev[bk])
                        p.do("tensor", "transpose", phbank(bk),
                             sgn[buf][:, db * 128:(db + 1) * 128], eyeH[:],
                             sem="tr")
                        trm = p.mark("tr")
                        p.wait("scalar", "tr", trm)
                        bank_ev[bk] = p.do("scalar", "copy",
                                           stg[:, db * 128:(db + 1) * 128],
                                           phbank(bk), sem="ev")
                    p.wait("gpsimd", "ev", p.mark("ev"))
                    wr_marks[jb] = p.do("gpsimd", "dma_start", sem="wr", amt=16,
                                        out=scr_ap[:, jb, :, :].rearrange("p a b -> p (a b)"),
                                        in_=stg[:, 0:n_db * 128])
                    cons[jb] = (vxm, sgm, trm)

            prep_sign(HF, DB, w1, s1T, araw)
            p.wait("vector", "vx")
            p.do("vector", "tensor_scalar_mul", a1[:], araw[:, 0:HF], 1.0 / D)
            prep_sign(DB, HF, w2, s2T, sc2)
            p.wait("vector", "vx")
            p.do("vector", "tensor_scalar_mul", sc2[:], sc2[:, 0:DB], 0.5 / H,
                 sem="vx")

            def prep_bias(b_ap, n, out_tile, scale):
                p.wait("sync", "tr")
                ldc = p.do("sync", "dma_start", sem="ld", amt=16,
                           out=brow[0:n, :],
                           in_=b_ap.rearrange("(a b) -> a b", b=128))
                p.wait("tensor", "ld", ldc)
                p.wait("tensor", "ev")
                p.do("tensor", "transpose", pbank(4, 0, 128, 0, n),
                     brow[0:n, :], eyeS[0:n, 0:n], sem="tr")
                p.wait("vector", "tr", p.mark("tr"))
                p.do("vector", "tensor_scalar_mul", out_tile[:, 0:n],
                     pbank(4, 0, 128, 0, n), scale, sem="vx")

            prep_bias(b1, HF, b1t, 1.0)
            prep_bias(b2, DB, bb2, 0.5)
            prep_bias(b_out, DB, bo, 1.0)

            def prep_split(w_ap, n_db, n_out, hi_scr, lo_scr):
                for db in range(n_db):
                    buf = db % 2
                    p.wait("sync", "vx")
                    p.wait("sync", "sg")
                    ldc = p.do("sync", "dma_start", sem="ld", amt=16,
                               out=wld[buf][:, 0:n_out],
                               in_=w_ap[db * 128:(db + 1) * 128, :])
                    p.wait("vector", "ld", ldc)
                    p.wait("vector", "wr")
                    p.do("vector", "tensor_copy", hi_t[:, 0:n_out],
                         wld[buf][:, 0:n_out])
                    vc = p.do("vector", "tensor_sub", lo_t[:, 0:n_out],
                              wld[buf][:, 0:n_out], hi_t[:, 0:n_out], sem="vx")
                    p.wait("gpsimd", "vx", vc)
                    p.do("gpsimd", "dma_start", sem="wr", amt=16,
                         out=hi_scr[:, :, db, :],
                         in_=hi_t[:, 0:n_out].rearrange("p (a b) -> p a b", b=128))
                    p.do("gpsimd", "dma_start", sem="wr", amt=16,
                         out=lo_scr[:, :, db, :],
                         in_=lo_t[:, 0:n_out].rearrange("p (a b) -> p a b", b=128))

            prep_split(w_qkv, DB, 3 * D, wqh, wql)
            prep_split(w_out, DB, D, woh, wol)

            p.wait("sync", "vx")
            p.wait("sync", "sg")
            p.do("sync", "dma_start", sem="ld", amt=16,
                 out=wld[0][0:64, 0:HEADS * FEAT].rearrange(
                     "d (h f) -> d h f", f=FEAT),
                 in_=proj.rearrange("h d f -> d h f"))
            ldc = p.do("sync", "dma_start", sem="ld", amt=16,
                       out=wld[0][64:128, 0:HEADS * FEAT].rearrange(
                           "d (h f) -> d h f", f=FEAT),
                       in_=proj.rearrange("h d f -> d h f"))
            p.wait("vector", "ld", ldc)
            p.wait("vector", "wr")
            p.do("vector", "tensor_copy", hi_t[:, 0:HEADS * FEAT],
                 wld[0][:, 0:HEADS * FEAT])
            vc = p.do("vector", "tensor_sub", lo_t[:, 0:HEADS * FEAT],
                      wld[0][:, 0:HEADS * FEAT], hi_t[:, 0:HEADS * FEAT],
                      sem="vx")
            p.wait("gpsimd", "vx", vc)
            p.do("gpsimd", "dma_start", sem="wr", amt=16,
                 out=projhl[:, 0, :], in_=hi_t[:, 0:HEADS * FEAT])
            p.do("gpsimd", "dma_start", sem="wr", amt=16,
                 out=projhl[:, 1, :], in_=lo_t[:, 0:HEADS * FEAT])
            p.wait("gpsimd", "wr")
            p.wait("sync", "wr")
            p.wait("sync", "ld")
            p.flush_block("prep")

        def ffn_phase(src, dst, name):
            with ExitStack() as stf:
                def sbf(nm, shape, dt=F32):
                    return stf.enter_context(nc.sbuf_tensor(name + nm, shape, dt))

                s1c = [sbf(f"s1c{i}", [128, DB * 128], F16) for i in range(2)]
                Xf2 = [sbf(f"Xf{i}", [128, DB * T], F32) for i in range(2)]
                xtm = sbf("xtm", [128, D], F32)
                xnh2 = [sbf(f"xnh{i}", [128, DB * T], F16) for i in range(2)]
                xnl2 = [sbf(f"xnl{i}", [128, DB * T], F16) for i in range(2)]
                hh = sbf("hh", [128, HF * T], F16)
                hl = sbf("hl", [128, HF * T], F16)
                gbuf = [sbf(f"g{i}", [128, T], F32) for i in range(2)]
                sqd = [sbf(f"sq{i}", [128, T], F32) for i in range(2)]
                tbuf = sbf("tb", [128, T], F32)
                ubuf = sbf("ub", [128, T], F32)
                mS = sbf("mS", [128, T], F32)
                rS = sbf("rS", [128, T], F32)
                s2c = [sbf(f"s2c{i}", [128, HF * 128], F16) for i in range(2)]
                tmpb = sbf("tmpb", [128, T], F32)
                ytm = xtm

                sumsq_mm = {}
                mmA_m = {}
                ec_last = 0
                bc_last = 0
                wrm = {}
                ytr = {0: 0, 1: 0}
                yev = {0: 0, 1: 0}
                gel_marks = {}
                hs_marks = {}
                epi_marks = {}
                mmB_marks = {}
                bank_ev = {}
                nt_eff = NT if not (taps and not taps_full) else 1

                def emit_loadln(tt):
                    nonlocal bc_last
                    t0 = tt * T
                    Xf = Xf2[tt % 2]
                    xnh, xnl = xnh2[tt % 2], xnl2[tt % 2]
                    if src == "xin":
                        if tt > 1:
                            p.wait("scalar", "epi")
                            p.wait("scalar", "wr")
                        p.wait("tensor", "sx", bc_last)
                        for ts in range(4):
                            if ts > 0:
                                p.wait("sync", "tr")
                            ldc = p.do("sync", "dma_start", sem="ld", amt=16,
                                       out=xtm[:],
                                       in_=x_in[t0 + ts * 128:t0 + (ts + 1) * 128, :])
                            p.wait("tensor", "ld", ldc)
                            for c in range(DB):
                                bk = c % 2
                                if bk in bank_ev:
                                    p.wait("tensor", "ev", bank_ev[bk])
                                p.do("tensor", "transpose",
                                     pbank(bk, 0, 128, 0, 128),
                                     xtm[:, c * 128:(c + 1) * 128], eyeS[:],
                                     sem="tr")
                                p.wait("scalar", "tr", p.mark("tr"))
                                bank_ev[bk] = p.do(
                                    "scalar", "copy",
                                    Xf[:, c * T + ts * 128:c * T + (ts + 1) * 128],
                                    pbank(bk, 0, 128, 0, 128), sem="ev")
                        xf_sem, xf_ready = "ev", p.mark("ev")
                        ldsems = {c: "ev" for c in range(DB)}
                        ldcs = {c: xf_ready for c in range(DB)}
                    else:
                        _, src_ap = src
                        ldcs = {}
                        ldsems = {}
                        for c in range(DB):
                            if tt > 1:
                                if dst != "yout":
                                    p.wait("sync", f"wrx{c}", wrm[(tt % 2, c)])
                                else:
                                    p.wait("sync", "tr", ytr[tt % 2])
                            ldsems[c] = f"ldx{c}"
                            ldcs[c] = p.do("sync", "dma_start", sem=ldsems[c],
                                           amt=16,
                                           out=Xf[:, c * T:(c + 1) * T],
                                           in_=src_ap[:, c, t0:t0 + T])
                        xf_sem, xf_ready = ldsems[DB - 1], ldcs[DB - 1]

                    for c in range(DB):
                        p.wait("scalar", ldsems[c], ldcs[c])
                        if c >= 2:
                            p.wait("scalar", "mm", sumsq_mm[c - 2])
                        elif tt > 0:
                            p.wait("scalar", "mm", sumsq_mm[DB - 2 + c])
                        sqc = p.do("scalar", "activation", sqd[c % 2][:],
                                   Xf[:, c * T:(c + 1) * T], AF.Square, sem="sx")
                        if c == 0 and src != "xin":
                            p.wait("tensor", "sx", bc_last)
                            if dst == "yout" and tt > 1:
                                p.wait("tensor", "ev", yev[tt % 2])
                        p.wait("tensor", ldsems[c], ldcs[c])
                        p.do("tensor", "matmul", pbank(0, 0, 1), ones1[:, 0:1],
                             Xf[:, c * T:(c + 1) * T],
                             start=(c == 0), stop=(c == DB - 1))
                        p.wait("tensor", "sx", sqc)
                        sumsq_mm[c] = p.do("tensor", "matmul", pbank(1, 0, 1),
                                           ones1[:, 0:1], sqd[c % 2][:],
                                           start=(c == 0), stop=(c == DB - 1),
                                           sem="mm")
                    stats_mm = p.mark("mm")
                    R = rows
                    p.wait("vector", "mm", stats_mm)
                    p.do("vector", "tensor_scalar_mul", R[:, 0:T], pbank(0, 0, 1),
                         1.0 / D)
                    p.do("vector", "tensor_scalar_mul", R[:, T:2 * T],
                         pbank(1, 0, 1), 1.0 / D, sem="row")
                    p.wait("scalar", "row", p.mark("row"))
                    p.do("scalar", "activation", R[:, 2 * T:3 * T], R[:, 0:T],
                         AF.Square, sem="row")
                    p.wait("vector", "row", p.mark("row"))
                    p.do("vector", "tensor_sub", R[:, 3 * T:4 * T], R[:, T:2 * T],
                         R[:, 2 * T:3 * T])
                    p.do("vector", "tensor_scalar_add", R[:, 3 * T:4 * T],
                         R[:, 3 * T:4 * T], EPS_LN, sem="row")
                    p.wait("scalar", "row", p.mark("row"))
                    p.do("scalar", "activation", R[:, 2 * T:3 * T],
                         R[:, 3 * T:4 * T], AF.Sqrt, sem="row")
                    p.wait("vector", "row", p.mark("row"))
                    rowc = p.do("vector", "reciprocal", R[:, T:2 * T],
                                R[:, 2 * T:3 * T], sem="row")
                    p.wait("tensor", "row", rowc)
                    p.do("tensor", "matmul", pbank(0), onesr[0:1, :], R[:, 0:T],
                         start=True, stop=True)
                    bc_mm = p.do("tensor", "matmul", pbank(1), onesr[0:1, :],
                                 R[:, T:2 * T], start=True, stop=True,
                                 sem="mm")
                    p.wait("scalar", "mm", bc_mm)
                    p.do("scalar", "copy", mS[:], pbank(0))
                    bc_cp = p.do("scalar", "copy", rS[:], pbank(1), sem="sx")
                    bc_last = bc_cp
                    for c in range(DB):
                        cs = slice(c * T, (c + 1) * T)
                        if c == 0:
                            p.wait("vector", "sx", bc_cp)
                        p.do("vector", "tensor_sub", tbuf[:], Xf[:, cs], mS[:])
                        p.do("vector", "tensor_mul", ubuf[:], tbuf[:], rS[:])
                        p.do("vector", "tensor_copy", xnh[:, cs], ubuf[:])
                        p.do("vector", "tensor_sub", xnl[:, cs], ubuf[:],
                             xnh[:, cs], sem="vx")
                    xn_ready = p.mark("vx")
                    if taps and tt == 0 and name == "ffn1":
                        p.wait("gpsimd", xf_sem, xf_ready)
                        p.do("gpsimd", "dma_start", sem="wr", amt=16,
                             out=tap_aps["t_Xf"], in_=Xf[:])
                        p.wait("gpsimd", "vx", xn_ready)
                        p.do("gpsimd", "dma_start", sem="wr", amt=16,
                             out=tap_aps["t_xnh"], in_=xnh[:])
                        p.do("gpsimd", "dma_start", sem="wr", amt=16,
                             out=tap_aps["t_xnl"], in_=xnl[:])
                        p.wait("gpsimd", "row", rowc)
                        p.do("gpsimd", "dma_start", sem="wr", amt=16,
                             out=tap_aps["t_rows"], in_=rows[:])
                        p.do("gpsimd", "dma_start", sem="wr", amt=16,
                             out=tap_aps["t_a1"], in_=a1[:])
                        p.do("gpsimd", "dma_start", sem="wr", amt=16,
                             out=tap_aps["t_b1t"], in_=b1t[:])
                        p.do("gpsimd", "dma_start", sem="wr", amt=16,
                             out=tap_aps["t_sc2"], in_=sc2[:])
                        p.do("gpsimd", "dma_start", sem="wr", amt=16,
                             out=tap_aps["t_bb2"], in_=bb2[:])
                        p.do("gpsimd", "dma_start", sem="wr", amt=16,
                             out=tap_aps["t_s1"], in_=s1c[0][:, 0:1024])
                    return xn_ready

                xn_ready_by = {0: emit_loadln(0)}

                for tt in range(nt_eff):
                    t0 = tt * T
                    Xf = Xf2[tt % 2]
                    xnh, xnl = xnh2[tt % 2], xnl2[tt % 2]
                    xn_ready = xn_ready_by[tt]

                    p.wait("tensor", "vx", xn_ready)
                    for hf in range(HF):
                        ga = tt * HF + hf
                        s1sem = "s1a" if hf % 2 == 0 else "s1b"
                        if ga >= 2:
                            p.wait("sync", "mm", mmA_m[ga - 2])
                        s1l = p.do("sync", "dma_start", sem=s1sem, amt=16,
                                   out=s1c[hf % 2][:],
                                   in_=s1T[:, hf, :, :].rearrange(
                                       "p a b -> p (a b)"))
                        bk = 4 + (hf % 2)
                        if hf >= 2:
                            p.wait("tensor", "gel", gel_marks[hf - 2])
                        p.wait("tensor", s1sem, s1l)
                        for pi, xn in enumerate((xnh, xnl)):
                            for db in range(DB):
                                last = (pi == 1 and db == DB - 1)
                                p.do("tensor", "matmul", pbank(bk),
                                     s1c[hf % 2][:, db * 128:(db + 1) * 128],
                                     xn[:, db * T:(db + 1) * T],
                                     start=(pi == 0 and db == 0), stop=last,
                                     sem="mm" if last else None)
                        mmA_m[ga] = mmA = p.mark("mm")
                        p.wait("scalar", "mm", mmA)
                        if hf >= 2:
                            p.wait("scalar", "hs", hs_marks[hf - 2])
                        gel_marks[hf] = p.do("scalar", "activation",
                                             gbuf[hf % 2][:], pbank(bk), AF.Gelu,
                                             bias=b1t[:, hf:hf + 1],
                                             scale=a1[:, hf:hf + 1], sem="gel")
                        p.wait("vector", "gel", gel_marks[hf])
                        hsl = slice(hf * T, (hf + 1) * T)
                        p.do("vector", "tensor_copy", hh[:, hsl], gbuf[hf % 2][:])
                        hs_marks[hf] = p.do("vector", "tensor_sub", hl[:, hsl],
                                            gbuf[hf % 2][:], hh[:, hsl], sem="hs")
                    h_ready = p.mark("hs")

                    if tt + 1 < nt_eff:
                        xn_ready_by[tt + 1] = emit_loadln(tt + 1)

                    for c in range(DB):
                        key = c % 2
                        if (tt * DB + c) >= 2:
                            prev = tt * DB + c - 2
                            p.wait("sync", "mm", mmB_marks[prev])
                        s2sem = "s2a" if key == 0 else "s2b"
                        lds2 = p.do("sync", "dma_start", sem=s2sem, amt=16,
                                    out=s2c[key][:],
                                    in_=s2T[:, c, :, :].rearrange("p a b -> p (a b)"))
                        bk = 2 + key
                        p.wait("tensor", s2sem, lds2)
                        if c == 0:
                            p.wait("tensor", "hs", h_ready)
                            if tt > 0:
                                p.wait("tensor", "sx", ec_last)
                        if (tt * DB + c) >= 2:
                            p.wait("tensor", "epi", epi_marks[tt * DB + c - 2])
                        for pi, hx in enumerate((hh, hl)):
                            for jb in range(HF):
                                last = (pi == 1 and jb == HF - 1)
                                p.do("tensor", "matmul", pbank(bk),
                                     s2c[key][:, jb * 128:(jb + 1) * 128],
                                     hx[:, jb * T:(jb + 1) * T],
                                     start=(pi == 0 and jb == 0), stop=last,
                                     sem="mm" if last else None)
                        mmB_marks[tt * DB + c] = p.mark("mm")
                        p.wait("scalar", "mm", mmB_marks[tt * DB + c])
                        if (tt * DB + c) >= 1:
                            p.wait("scalar", "epi", epi_marks[tt * DB + c - 1])
                        ec = p.do("scalar", "activation", tmpb[:], pbank(bk),
                                  AF.Identity, bias=bb2[:, c:c + 1],
                                  scale=sc2[:, c:c + 1], sem="sx")
                        ec_last = ec
                        p.wait("vector", "sx", ec)
                        cs = slice(c * T, (c + 1) * T)
                        epi_marks[tt * DB + c] = p.do("vector", "tensor_add",
                                                      Xf[:, cs], Xf[:, cs],
                                                      tmpb[:], sem="epi")
                        if dst != "yout":
                            _, dst_ap = dst
                            p.wait("gpsimd", "epi", epi_marks[tt * DB + c])
                            wrm[(tt % 2, c)] = p.do("gpsimd", "dma_start",
                                                    sem=f"wrx{c}", amt=16,
                                                    out=dst_ap[:, c, t0:t0 + T],
                                                    in_=Xf[:, cs])

                    if taps and tt == 0 and name == "ffn1":
                        p.wait("gpsimd", "s2a")
                        p.wait("gpsimd", "s2b")
                        p.do("gpsimd", "dma_start", sem="wr", amt=16,
                             out=tap_aps["t_s2"], in_=s2c[1][:, 0:1024])
                    if dst == "yout":
                        x_done = p.mark("epi")
                        p.wait("tensor", "epi", x_done)
                        p.wait("tensor", "sx", bc_last)
                        for ts in range(4):
                            if ts > 0:
                                p.wait("scalar", "wr")
                            for c in range(DB):
                                bk = c % 2
                                if bk in bank_ev:
                                    p.wait("tensor", "ev", bank_ev[bk])
                                p.do("tensor", "transpose",
                                     pbank(bk, 0, 128, 0, 128),
                                     Xf[:, c * T + ts * 128:c * T + (ts + 1) * 128],
                                     eyeS[:], sem="tr")
                                p.wait("scalar", "tr", p.mark("tr"))
                                bank_ev[bk] = p.do("scalar", "copy",
                                                   ytm[:, c * 128:(c + 1) * 128],
                                                   pbank(bk, 0, 128, 0, 128),
                                                   sem="ev")
                            p.wait("gpsimd", "ev", p.mark("ev"))
                            p.do("gpsimd", "dma_start", sem="wr", amt=16,
                                 out=y[t0 + ts * 128:t0 + (ts + 1) * 128, :],
                                 in_=ytm[:])
                        ytr[tt % 2] = p.mark("tr")
                        yev[tt % 2] = p.mark("ev")

                p.wait("gpsimd", "wr")
                p.wait("gpsimd", "s2a")
                p.wait("gpsimd", "s2b")
                p.wait("gpsimd", "ld")
                p.flush_block(name)

        def attn_phase():
            with ExitStack() as sta:
                def sba(nm, shape, dt=F32):
                    return sta.enter_context(nc.sbuf_tensor("at" + nm, shape, dt))

                kv_acc = sba("kvacc", [128, HEADS * FEAT], F32)
                kv65h = sba("kv65h", [128, HEADS * 2 * 65], F16)
                kv65l = sba("kv65l", [128, HEADS * 2 * 65], F16)
                projS_h = sba("pjh", [128, HEADS * FEAT], F16)
                projS_l = sba("pjl", [128, HEADS * FEAT], F16)
                p.do("sync", "dma_start", sem="ld", amt=16, out=projS_h[:],
                     in_=projhl[:, 0, :])
                pjld = p.do("sync", "dma_start", sem="ld", amt=16,
                            out=projS_l[:], in_=projhl[:, 1, :])
                p.wait("tensor", "ld", pjld)

                with ExitStack() as st1:
                    def sb1(nm, shape, dt=F32):
                        return st1.enter_context(nc.sbuf_tensor("a1" + nm, shape, dt))

                    Xf = sb1("Xf", [128, DB * T], F32)
                    xnh2 = [sb1(f"xnh{i}", [128, DB * T], F16) for i in range(2)]
                    xnl2 = [sb1(f"xnl{i}", [128, DB * T], F16) for i in range(2)]
                    k_hi = sb1("kh", [128, DB * T], F16)
                    k_lo = sb1("kl", [128, DB * T], F16)
                    qs_h = sb1("qsh", [128, DB * T], F16)
                    qs_l = sb1("qsl", [128, DB * T], F16)
                    vst = [sb1(f"vst{i}", [128, T], F16) for i in range(2)]
                    vsl = [sb1(f"vsl{i}", [128, T], F16) for i in range(2)]
                    vtmh = sb1("vtmh", [128, 4 * 1040], F16)
                    vtml = sb1("vtml", [128, 4 * 1040], F16)
                    wqs_h = [sb1(f"wqh{i}", [128, DB * 128], F16) for i in range(2)]
                    wqs_l = [sb1(f"wql{i}", [128, DB * 128], F16) for i in range(2)]
                    ke_h = [sb1(f"keh{i}", [128, 4 * FEAT], F16) for i in range(2)]
                    ke_l = [sb1(f"kel{i}", [128, 4 * FEAT], F16) for i in range(2)]
                    el_m = sb1("elm", [128, 4 * FEAT], F32)
                    el_f = sb1("elf", [128, 4 * FEAT], F32)
                    sqb1 = [sb1(f"sq{i}", [128, T], F32) for i in range(2)]
                    tbuf = [sb1(f"tb{i}", [128, T], F32) for i in range(2)]
                    ubuf = [sb1(f"ub{i}", [128, T], F32) for i in range(2)]
                    mS = sb1("mS", [128, T], F32)
                    rS = sb1("rS", [128, T], F32)

                    zer = p.do("gpsimd", "memset", kv_acc[:], 0.0)
                    p.do("gpsimd", "memset", vtmh[:], 1.0)
                    zer = p.do("gpsimd", "memset", vtml[:], 0.0, sem="gx")

                    sumsq_mm = {}
                    kel_m = {}
                    kvm_m = {}
                    mmQ = {}
                    evq = {}
                    kva = {}
                    elu_ps = {}
                    vtr_ev = {}
                    q_wr = {}

                    lnlo = {}
                    hic_last = 0
                    bc1_last = 0
                    nt1 = NT if not (taps and not taps_full) else 1

                    def p1_loadln(tt):
                        nonlocal bc1_last
                        t0 = tt * T
                        xnh, xnl = xnh2[tt % 2], xnl2[tt % 2]
                        ldcs1 = {}
                        for c in range(DB):
                            if tt > 0:
                                p.wait("sync", "vx", lnlo[c])
                            ldcs1[c] = p.do("sync", "dma_start",
                                            sem=f"ldx{c}", amt=16,
                                            out=Xf[:, c * T:(c + 1) * T],
                                            in_=Xa[:, c, t0:t0 + T])
                        for c in range(DB):
                            p.wait("gpsimd", f"ldx{c}", ldcs1[c])
                            if c >= 2:
                                p.wait("gpsimd", "mm", sumsq_mm[c - 2])
                            elif tt > 0:
                                p.wait("gpsimd", "mm", sumsq_mm[DB - 2 + c])
                            sqc = p.do("gpsimd", "tensor_mul", sqb1[c % 2][:],
                                       Xf[:, c * T:(c + 1) * T],
                                       Xf[:, c * T:(c + 1) * T], sem="gx")
                            if c == 0 and tt > 0:
                                p.wait("tensor", "sx", hic_last)
                                p.wait("tensor", "vx",
                                       evq[tt * M_QKV - 1])
                            p.wait("tensor", f"ldx{c}", ldcs1[c])
                            p.do("tensor", "matmul", pbank(2, 0, 1),
                                 ones1[:, 0:1], Xf[:, c * T:(c + 1) * T],
                                 start=(c == 0), stop=(c == DB - 1))
                            p.wait("tensor", "gx", sqc)
                            sumsq_mm[c] = p.do("tensor", "matmul",
                                               pbank(3, 0, 1), ones1[:, 0:1],
                                               sqb1[c % 2][:],
                                               start=(c == 0),
                                               stop=(c == DB - 1), sem="mm")
                        stats_mm = p.mark("mm")
                        R = rows
                        p.wait("vector", "mm", stats_mm)
                        p.do("vector", "tensor_scalar_mul", R[:, 0:T],
                             pbank(2, 0, 1), 1.0 / D)
                        p.do("vector", "tensor_scalar_mul", R[:, T:2 * T],
                             pbank(3, 0, 1), 1.0 / D, sem="row")
                        p.wait("scalar", "row", p.mark("row"))
                        p.do("scalar", "activation", R[:, 2 * T:3 * T],
                             R[:, 0:T], AF.Square, sem="row")
                        p.wait("vector", "row", p.mark("row"))
                        p.do("vector", "tensor_sub", R[:, 3 * T:4 * T],
                             R[:, T:2 * T], R[:, 2 * T:3 * T])
                        p.do("vector", "tensor_scalar_add", R[:, 3 * T:4 * T],
                             R[:, 3 * T:4 * T], EPS_LN, sem="row")
                        p.wait("scalar", "row", p.mark("row"))
                        p.do("scalar", "activation", R[:, 2 * T:3 * T],
                             R[:, 3 * T:4 * T], AF.Sqrt, sem="row")
                        p.wait("vector", "row", p.mark("row"))
                        rowc = p.do("vector", "reciprocal", R[:, T:2 * T],
                                    R[:, 2 * T:3 * T], sem="row")
                        p.wait("tensor", "row", rowc)
                        p.do("tensor", "matmul", pbank(2), onesr[0:1, :],
                             R[:, 0:T], start=True, stop=True)
                        bc_mm = p.do("tensor", "matmul", pbank(3),
                                     onesr[0:1, :], R[:, T:2 * T],
                                     start=True, stop=True, sem="mm")
                        p.wait("scalar", "mm", bc_mm)
                        p.do("scalar", "copy", mS[:], pbank(2))
                        bc_cp = p.do("scalar", "copy", rS[:], pbank(3),
                                     sem="sx")
                        bc1_last = bc_cp

                        def apply_chunk(c):
                            cs = slice(c * T, (c + 1) * T)
                            if c == 0:
                                p.wait("vector", "sx", bc_cp)
                            p.do("vector", "tensor_sub", tbuf[0][:],
                                 Xf[:, cs], mS[:])
                            p.do("vector", "tensor_mul", ubuf[0][:],
                                 tbuf[0][:], rS[:])
                            p.do("vector", "tensor_copy", xnh[:, cs],
                                 ubuf[0][:])
                            lnlo[c] = p.do("vector", "tensor_sub", xnl[:, cs],
                                           ubuf[0][:], xnh[:, cs], sem="vx")
                            return lnlo[c]

                        return apply_chunk

                    def p1_apply_all(tt):
                        ap = p1_loadln(tt)
                        for c in range(DB):
                            ap(c)
                        return p.mark("vx")

                    xnr1 = {0: p1_apply_all(0)}
                    for tt in range(nt1):
                        t0 = tt * T
                        xnh, xnl = xnh2[tt % 2], xnl2[tt % 2]
                        xn_ready = xnr1[tt]

                        p.wait("tensor", "vx", xn_ready)
                        for m in range(M_QKV):
                            key = m % 2
                            gstep = tt * M_QKV + m
                            if gstep >= 2:
                                p.wait("sync", "mm", mmQ[gstep - 2])
                            wqsem = "wqa" if key == 0 else "wqb"
                            lw1 = p.do("sync", "dma_start", sem=wqsem, amt=16,
                                       out=wqs_h[key][:],
                                       in_=wqh[:, m, :, :].rearrange(
                                           "p a b -> p (a b)"))
                            lw2 = p.do("sync", "dma_start", sem=wqsem, amt=16,
                                       out=wqs_l[key][:],
                                       in_=wql[:, m, :, :].rearrange(
                                           "p a b -> p (a b)"))
                            bk = 2 + (m % 4)
                            p.wait("tensor", wqsem, lw2)
                            if gstep >= 4:
                                p.wait("tensor", "vx", evq[gstep - 4])
                            first = True
                            for wop, xop in ((wqs_h[key], xnh), (wqs_l[key], xnh),
                                             (wqs_h[key], xnl)):
                                for db in range(DB):
                                    last = (wop is wqs_h[key] and xop is xnl
                                            and db == DB - 1)
                                    p.do("tensor", "matmul", pbank(bk),
                                         wop[:, db * 128:(db + 1) * 128],
                                         xop[:, db * T:(db + 1) * T],
                                         start=first, stop=last,
                                         sem="mm" if last else None)
                                    first = False
                            mmQ[gstep] = p.mark("mm")
                            p.wait("scalar", "mm", mmQ[gstep])
                            if m < DB:
                                qc = m
                                hidst = qs_h[:, qc * T:(qc + 1) * T]
                                lodst = qs_l[:, qc * T:(qc + 1) * T]
                                if tt > 0:
                                    p.wait("scalar", f"qwr{qc}",
                                           q_wr.get(qc, 0))
                            elif m < 2 * DB:
                                kc = m - DB
                                hidst = k_hi[:, kc * T:(kc + 1) * T]
                                lodst = k_lo[:, kc * T:(kc + 1) * T]
                            else:
                                vc = m - 2 * DB
                                hidst = vst[m % 2][:]
                                lodst = vsl[m % 2][:]
                                if m - 2 >= 2 * DB:
                                    p.wait("scalar", "tr",
                                           evq[("vtr", gstep - 2)])
                            hicp = p.do("scalar", "copy", hidst, pbank(bk),
                                        sem="sx")
                            hic_last = hicp
                            p.wait("vector", "sx", hicp)
                            evq[gstep] = p.do("vector", "tensor_sub", lodst,
                                              pbank(bk), hidst, sem="vx")
                            if m >= 2 * DB:
                                vc = m - 2 * DB
                                p.wait("tensor", "vx", evq[gstep])
                                for sub in range(4):
                                    for part, src in ((0, hidst), (1, lodst)):
                                        hb = sub % 2
                                        if (hb in vtr_ev):
                                            p.wait("tensor", "ev", vtr_ev[hb])
                                        p.do("tensor", "transpose", phbank(hb),
                                             src[:, sub * 128:(sub + 1) * 128],
                                             eyeH[:], sem="tr")
                                        p.wait("scalar", "tr", p.mark("tr"))
                                        dstt = (vtmh if part == 0 else vtml)
                                        o0 = sub * 1040 + (2 * vc) * 65
                                        o1 = sub * 1040 + (2 * vc + 1) * 65
                                        p.do("scalar", "copy",
                                             dstt[:, o0:o0 + 64],
                                             phbank(hb, 0, 128, 0, 64))
                                        vtr_ev[hb] = p.do(
                                            "scalar", "copy",
                                            dstt[:, o1:o1 + 64],
                                            phbank(hb, 0, 128, 64, 128),
                                            sem="ev")
                                evq[("vtr", gstep)] = p.mark("tr")
                        for qc in range(DB):
                            p.wait("gpsimd", "vx", evq[tt * M_QKV + qc])
                            p.do("gpsimd", "dma_start", sem=f"qwr{qc}", amt=16,
                                 out=qhh[:, qc, t0:t0 + T],
                                 in_=qs_h[:, qc * T:(qc + 1) * T])
                            q_wr[qc] = p.do("gpsimd", "dma_start",
                                            sem=f"qwr{qc}", amt=16,
                                            out=qhl[:, qc, t0:t0 + T],
                                            in_=qs_l[:, qc * T:(qc + 1) * T])

                        pend_apply = None
                        if tt + 1 < nt1:
                            pend_apply = p1_loadln(tt + 1)

                        vtm_done = p.mark("ev")
                        if tt == 0:
                            p.wait("vector", "gx", zer)
                        def emit_kv(h):
                            kvb = h % 2
                            if h >= 2:
                                p.wait("tensor", "epi", kva[tt * HEADS + h - 2])
                            elif tt > 0:
                                p.wait("tensor", "epi",
                                       kva[(tt - 1) * HEADS + HEADS - 2 + h])
                            p.wait("tensor", "vx", kel_m[h])
                            keh_b, kel_b = ke_h[h % 2], ke_l[h % 2]
                            for ts in range(4):
                                vh = vtmh[:, ts * 1040 + h * 65:
                                          ts * 1040 + (h + 1) * 65]
                                vl = vtml[:, ts * 1040 + h * 65:
                                          ts * 1040 + (h + 1) * 65]
                                keh_s = keh_b[:, ts * FEAT:(ts + 1) * FEAT]
                                kel_s = kel_b[:, ts * FEAT:(ts + 1) * FEAT]
                                p.do("tensor", "matmul",
                                     pbank(kvb, 0, 65, 0, FEAT), vh, keh_s,
                                     start=(ts == 0), stop=False,
                                     skip_group_check=True)
                                p.do("tensor", "matmul",
                                     pbank(kvb, 0, 65, 0, FEAT), vl, keh_s,
                                     start=False, stop=False,
                                     skip_group_check=True)
                                p.do("tensor", "matmul",
                                     pbank(kvb, 0, 65, 0, FEAT), vh, kel_s,
                                     start=False, stop=(ts == 3),
                                     sem="mm" if ts == 3 else None,
                                     skip_group_check=True)
                            kvm_m[tt * HEADS + h] = p.mark("mm")
                            p.wait("vector", "mm", kvm_m[tt * HEADS + h])
                            kva[tt * HEADS + h] = p.do(
                                "vector", "tensor_add",
                                kv_acc[0:65, h * FEAT:(h + 1) * FEAT],
                                pbank(kvb, 0, 65, 0, FEAT),
                                kv_acc[0:65, h * FEAT:(h + 1) * FEAT],
                                sem="epi")

                        for h in range(HEADS):
                            hb2 = (h % 2) * 64
                            kc = h // 2
                            hkey = tt * HEADS + h
                            if h == 0:
                                p.wait("tensor", "vx",
                                       evq[tt * M_QKV + M_QKV - 1])
                                p.wait("tensor", "ev", vtm_done)
                                p.wait("tensor", "sx", bc1_last)
                            if hkey >= 1:
                                p.wait("tensor", "vx", elu_ps[hkey - 1])
                            ph = projS_h[hb2:hb2 + 64, h * FEAT:(h + 1) * FEAT]
                            pl = projS_l[hb2:hb2 + 64, h * FEAT:(h + 1) * FEAT]
                            for ts in range(4):
                                pb = 2 + ts
                                ksl_h = k_hi[hb2:hb2 + 64,
                                             kc * T + ts * 128:kc * T + (ts + 1) * 128]
                                ksl_l = k_lo[hb2:hb2 + 64,
                                             kc * T + ts * 128:kc * T + (ts + 1) * 128]
                                p.do("tensor", "matmul",
                                     pbank(pb, 0, 128, 0, FEAT),
                                     ksl_h, ph, start=True, stop=False)
                                p.do("tensor", "matmul",
                                     pbank(pb, 0, 128, 0, FEAT),
                                     ksl_l, ph, start=False, stop=False)
                                p.do("tensor", "matmul",
                                     pbank(pb, 0, 128, 0, FEAT),
                                     ksl_h, pl, start=False, stop=(ts == 3),
                                     sem="mm" if ts == 3 else None)
                            kpm = p.mark("mm")
                            if h >= 1:
                                emit_kv(h - 1)
                            psall = psum[0:128, 2:6, 0:FEAT]
                            keh_b, kel_b = ke_h[h % 2], ke_l[h % 2]
                            elm3 = el_m[:].rearrange("p (a b) -> p a b", b=FEAT)
                            elf3 = el_f[:].rearrange("p (a b) -> p a b", b=FEAT)
                            p.wait("vector", "mm", kpm)
                            if hkey >= 2:
                                p.wait("vector", "mm", kvm_m[hkey - 2])
                            p.do("vector", "tensor_scalar_min", elm3, psall,
                                 0.0, sem="vx")
                            p.wait("scalar", "vx", p.mark("vx"))
                            exm = p.do("scalar", "activation", el_f[:], el_m[:],
                                       AF.Exp, sem="sx")
                            p.wait("vector", "sx", exm)
                            p.do("vector", "scalar_tensor_tensor", elm3,
                                 psall, 0.0, elf3, OP.max, OP.add)
                            elu_ps[hkey] = p.mark("vx") + 1
                            p.do("vector", "tensor_copy", keh_b[:], el_m[:],
                                 sem="vx")
                            kel_m[h] = p.do("vector", "tensor_sub", kel_b[:],
                                            el_m[:], keh_b[:], sem="vx")
                            if pend_apply is not None and h >= 8:
                                pend_apply(h - 8)
                                if h == HEADS - 1:
                                    xnr1[tt + 1] = p.mark("vx")
                        emit_kv(HEADS - 1)
                    p.wait("gpsimd", "wr")
                    p.wait("gpsimd", "epi")
                    if taps:
                        p.wait("gpsimd", "epi")
                        p.wait("gpsimd", "vx")
                        p.wait("gpsimd", "sx")
                        p.do("gpsimd", "dma_start", sem="wr", amt=16,
                             out=tap_aps["t_kv"], in_=kv_acc[:])
                        if not taps_full:
                            for nm, tile in (("t_kh", k_hi), ("t_kl", k_lo),
                                             ("t_vh", vtmh), ("t_vl", vtml),
                                             ("t_qh2", qs_h), ("t_ql2", qs_l),
                                             ("t_keh", ke_h[1]), ("t_kel", ke_l[1])):
                                p.do("gpsimd", "dma_start", sem="wr", amt=16,
                                     out=tap_aps[nm], in_=tile[:])
                        p.wait("gpsimd", "wr")

                p.flush_block("attn1")
                if not taps or taps_full:
                    with ExitStack() as st2:
                        def sb2(nm, shape, dt=F32):
                            return st2.enter_context(nc.sbuf_tensor("a2" + nm, shape, dt))

                        Xf = sb2("Xf", [128, DB * T], F32)
                        qs_h = sb2("qsh", [128, DB * T], F16)
                        qs_l = sb2("qsl", [128, DB * T], F16)
                        wosb_h = sb2("woh", [128, DB * DB * 128], F16)
                        wosb_l = sb2("wol", [128, DB * DB * 128], F16)
                        qe_h = [sb2(f"qeh{i}", [128, 4 * T], F16)
                                for i in range(2)]
                        qe_l = [sb2(f"qel{i}", [128, 4 * T], F16)
                                for i in range(2)]
                        el2_m = sb2("elm", [128, 4 * T], F32)
                        el2_f = sb2("elf", [128, 4 * T], F32)
                        at_h = sb2("ath", [128, DB * T], F16)
                        at_l = sb2("atl", [128, DB * T], F16)
                        aof = [sb2(f"aof{i}", [128, T], F32) for i in range(2)]
                        ub2 = sb2("ub2", [128, T], F32)
                        zst2 = sb2("zst2", [64, T], F32)

                        onesP = sb2("onesP", [33, 128], F32)
                        p.do("gpsimd", "memset", onesP[:], 1.0)
                        zmem = p.do("gpsimd", "memset", zst2[:], 1.0, sem="gx")
                        p.do("sync", "dma_start", sem="ld", amt=16, out=wosb_h[:],
                             in_=woh.rearrange("p a b c -> p (a b c)"))
                        wold = p.do("sync", "dma_start", sem="ld", amt=16,
                                    out=wosb_l[:],
                                    in_=wol.rearrange("p a b c -> p (a b c)"))

                        p.wait("tensor", "epi")
                        kv_ev = {}
                        for h in range(HEADS):
                            for fc in range(2):
                                bk = 4 + fc
                                if bk in kv_ev:
                                    p.wait("tensor", "vx", kv_ev[bk])
                                p.do("tensor", "transpose", pbank(bk, 0, 128, 0, 65),
                                     kv_acc[0:65, h * FEAT + fc * 128:
                                            h * FEAT + (fc + 1) * 128],
                                     eyeS[0:65, 0:65], sem="tr")
                                trm = p.mark("tr")
                                p.wait("scalar", "tr", trm)
                                o65 = (h * 2 + fc) * 65
                                hcp = p.do("scalar", "copy",
                                           kv65h[:, o65:o65 + 65],
                                           pbank(bk, 0, 128, 0, 65), sem="sx")
                                p.wait("vector", "sx", hcp)
                                kv_ev[bk] = p.do("vector", "tensor_sub",
                                                 kv65l[:, o65:o65 + 65],
                                                 pbank(bk, 0, 128, 0, 65),
                                                 kv65h[:, o65:o65 + 65], sem="vx")
                        kv65_done = p.mark("vx")
                        p.wait("tensor", "ld", wold)

                        atg = {}
                        stt2 = {}
                        amm_pair = {}
                        zad_m = {}
                        wo_epi = {}
                        zb_rd = {}
                        aof_rd = {}

                        wrm2 = {}
                        last_amm = 0
                        for tt in range(NT if not taps else 1):
                            t0 = tt * T
                            for c in range(DB):
                                if tt > 0:
                                    p.wait("sync", "wr", wrm2[c])
                                    if c == 0:
                                        p.wait("sync", "mm", last_amm)
                                p.do("sync", "dma_start", sem="ld", amt=16,
                                     out=Xf[:, c * T:(c + 1) * T],
                                     in_=Xa[:, c, t0:t0 + T])
                                p.do("sync", "dma_start", sem="ld", amt=16,
                                     out=qs_h[:, c * T:(c + 1) * T],
                                     in_=qhh[:, c, t0:t0 + T])
                                ldc = p.do("sync", "dma_start", sem="ld", amt=16,
                                           out=qs_l[:, c * T:(c + 1) * T],
                                           in_=qhl[:, c, t0:t0 + T])
                            p.wait("tensor", "ld", ldc)
                            p.wait("tensor", "epi")
                            if tt == 0:
                                p.wait("tensor", "vx", kv65_done)
                            def emit_zbc_norm(j):
                                oc = j
                                ob = j % 2
                                p.wait("tensor", "vx", zad_m[j])
                                p.wait("tensor", "sx", atg[2 * j])
                                p.do("tensor", "matmul", pbank(4, 0, 64),
                                     onesr[0:1, 0:64], zst2[0:1, :],
                                     start=True, stop=True, skip_group_check=True)
                                zbm = p.do("tensor", "matmul", pbank(4, 64, 128),
                                           onesP[32:33, 0:64], zst2[32:33, :],
                                           start=True, stop=True, sem="mm",
                                           skip_group_check=True)
                                ocs = slice(oc * T, (oc + 1) * T)
                                p.wait("vector", "mm", zbm)
                                p.do("vector", "tensor_mul", at_h[:, ocs],
                                     aof[ob][:], pbank(4))
                                p.do("vector", "tensor_mul", ub2[:],
                                     aof[ob][:], pbank(4), sem="vx")
                                zb_rd[j] = p.mark("vx")
                                aof_rd[ob] = zb_rd[j]
                                p.do("vector", "tensor_sub", at_l[:, ocs],
                                     ub2[:], at_h[:, ocs], sem="hs")

                            for mpr in range(8):
                                h0, h1 = 2 * mpr, 2 * mpr + 1
                                qc = mpr
                                pkey = tt * 8 + mpr
                                if pkey >= 1:
                                    p.wait("tensor", "vx", stt2[pkey - 1])
                                qpm = 0
                                for hp, hbk in enumerate(((h0, 0), (h1, 2))):
                                    h, bk0 = hbk
                                    hb2 = hp * 64
                                    qsl_h = qs_h[hb2:hb2 + 64, qc * T:(qc + 1) * T]
                                    qsl_l = qs_l[hb2:hb2 + 64, qc * T:(qc + 1) * T]
                                    for fc in range(2):
                                        ph = projS_h[hb2:hb2 + 64,
                                                     h * FEAT + fc * 128:
                                                     h * FEAT + (fc + 1) * 128]
                                        pl = projS_l[hb2:hb2 + 64,
                                                     h * FEAT + fc * 128:
                                                     h * FEAT + (fc + 1) * 128]
                                        pb = bk0 + fc
                                        p.do("tensor", "matmul", pbank(pb), ph,
                                             qsl_h, start=True, stop=False)
                                        p.do("tensor", "matmul", pbank(pb), ph,
                                             qsl_l, start=False, stop=False)
                                        qpm = p.do("tensor", "matmul", pbank(pb),
                                                   pl, qsl_h, start=False, stop=True,
                                                   sem="mm" if (hp == 1 and fc == 1)
                                                   else None)
                                if mpr >= 1:
                                    emit_zbc_norm(mpr - 1)
                                ps4 = psum[0:128, 0:4, 0:512]
                                em3 = el2_m[:].rearrange("p (a b) -> p a b", b=512)
                                ef3 = el2_f[:].rearrange("p (a b) -> p a b", b=512)
                                qeh_b, qel_b = qe_h[mpr % 2], qe_l[mpr % 2]
                                p.wait("vector", "mm", qpm)
                                if pkey >= 2:
                                    p.wait("vector", "mm", amm_pair[pkey - 2])
                                p.do("vector", "tensor_scalar_min", em3, ps4,
                                     0.0, sem="vx")
                                p.wait("scalar", "vx", p.mark("vx"))
                                exm = p.do("scalar", "activation", el2_f[:],
                                           el2_m[:], AF.Exp, sem="sx")
                                p.wait("vector", "sx", exm)
                                p.do("vector", "scalar_tensor_tensor", em3,
                                     ps4, 0.0, ef3, OP.max, OP.add)
                                stt2[pkey] = p.mark("vx") + 1
                                p.do("vector", "tensor_copy", qeh_b[:], el2_m[:],
                                     sem="vx")
                                qeld = p.do("vector", "tensor_sub", qel_b[:],
                                            el2_m[:], qeh_b[:], sem="vx")
                                p.wait("tensor", "vx", qeld)
                                if mpr >= 1:
                                    p.wait("tensor", "vx", zb_rd.get(mpr - 1, 0))
                                    p.wait("tensor", "sx", atg.get(h1 - 2, 0))
                                for hp, hab in enumerate(((h0, 4), (h1, 5))):
                                    h, ab = hab
                                    first = True
                                    for fc in range(2):
                                        o65 = (h * 2 + fc) * 65
                                        qoff = (hp * 2 + fc) * T
                                        for wop, xop in ((kv65h, qeh_b),
                                                         (kv65l, qeh_b),
                                                         (kv65h, qel_b)):
                                            last = (fc == 1 and xop is qel_b)
                                            p.do("tensor", "matmul",
                                                 pbank(ab, 0, 65),
                                                 wop[:, o65:o65 + 65],
                                                 xop[:, qoff:qoff + T],
                                                 start=first, stop=last,
                                                 sem="mm" if last else None,
                                                 skip_group_check=True)
                                            first = False
                                    amm_pair[pkey] = p.mark("mm")
                                    last_amm = amm_pair[pkey]
                                    p.wait("scalar", "mm", amm_pair[pkey])
                                    if pkey == 0:
                                        p.wait("scalar", "gx", zmem)
                                    zrow = 0 if hp == 0 else 32
                                    p.do("scalar", "copy", zst2[zrow:zrow + 1, :],
                                         pbank(ab, 64, 65), sem="sx")
                                    ob = mpr % 2
                                    if hp == 0 and mpr >= 2:
                                        p.wait("scalar", "vx", aof_rd.get(ob, 0))
                                    atg[h] = p.do("scalar", "copy",
                                                  aof[ob][hp * 64:hp * 64 + 64, :],
                                                  pbank(ab, 0, 64), sem="sx")
                                p.wait("vector", "sx", atg[h1])
                                p.do("vector", "tensor_scalar_add",
                                     zst2[0:33, :], zst2[0:33, :], EPS_Z)
                                zad_m[mpr] = p.do("vector", "reciprocal",
                                                  zst2[0:33, :], zst2[0:33, :],
                                                  sem="vx")
                            emit_zbc_norm(7)

                            at_done = p.mark("hs")
                            for co in range(DB):
                                wb = 2 + (co % 2)
                                if co == 0:
                                    p.wait("tensor", "hs", at_done)
                                if co >= 2:
                                    p.wait("tensor", "epi",
                                           wo_epi[tt * DB + co - 2])
                                first = True
                                for wop, xop in ((wosb_h, at_h), (wosb_l, at_h),
                                                 (wosb_h, at_l)):
                                    for ci in range(DB):
                                        last = (wop is wosb_h and xop is at_l
                                                and ci == DB - 1)
                                        p.do("tensor", "matmul", pbank(wb),
                                             wop[:, (co * DB + ci) * 128:
                                                 (co * DB + ci + 1) * 128],
                                             xop[:, ci * T:(ci + 1) * T],
                                             start=first, stop=last,
                                             sem="mm" if last else None)
                                        first = False
                                wmm = p.mark("mm")
                                p.wait("vector", "mm", wmm)
                                cs = slice(co * T, (co + 1) * T)
                                wo_epi[tt * DB + co] = p.do(
                                    "vector", "scalar_tensor_tensor", Xf[:, cs],
                                    pbank(wb), bo[:, co:co + 1], Xf[:, cs],
                                    OP.add, OP.add, sem="epi")
                                p.wait("gpsimd", "epi", wo_epi[tt * DB + co])
                                wrm2[co] = p.do("gpsimd", "dma_start", sem="wr",
                                                amt=16,
                                                out=Xb[:, co, t0:t0 + T],
                                                in_=Xf[:, cs])
                        p.wait("gpsimd", "wr")
                p.flush_block("attn")

        if "ffn1" in phases:
            ffn_phase("xin", ("fm", Xa), "ffn1")
        if "attn" in phases:
            attn_phase()
        if "ffn2" in phases:
            src_ap = Xb if "attn" in phases else Xa
            ffn_phase(("fm", src_ap), "yout", "ffn2")

        if debug_out is not None:
            src_ap = Xa if debug_out == "Xa" else Xb
            with ExitStack() as std:
                dbg = std.enter_context(nc.sbuf_tensor("dbg", [128, NTOK], F32))
                for c in range(DB):
                    ldc = p.do("sync", "dma_start", sem="ld", amt=16,
                               out=dbg[:], in_=src_ap[:, c, :])
                    p.wait("gpsimd", "ld", ldc)
                    wc = p.do("gpsimd", "dma_start", sem="wr", amt=16,
                              out=y[:, c, :], in_=dbg[:])
                    p.wait("sync", "wr", wc)
                p.wait("gpsimd", "wr")
                p.flush_block("dbg")

    return nc


_NC_CACHE = {}


def _get_nc(phases, debug_out, taps=False, taps_full=False):
    key = (tuple(phases), debug_out, taps, taps_full)
    if key not in _NC_CACHE:
        _NC_CACHE[key] = _build_nc(phases, debug_out, taps, taps_full)
    return _NC_CACHE[key]


def _make_in_maps(inputs):
    x = np.ascontiguousarray(np.asarray(inputs["L0_attn_x"], dtype=np.float32))
    eye = np.eye(128, dtype=np.float32)
    ones_in = np.zeros((128, 2), dtype=np.float32)
    ones_in[:, 0] = 1.0
    e2 = np.zeros((2, 128), dtype=np.float32)
    e2[0, :] = 1.0
    zsel = np.zeros((2, 128), dtype=np.float32)
    zsel[0, 0:64] = 1.0
    zsel[1, 64:128] = 1.0
    common = {
        "w_qkv": np.ascontiguousarray(inputs["L0_attn_w_qkv"], dtype=np.float32),
        "proj": np.ascontiguousarray(inputs["L0_attn_proj_matrix"], dtype=np.float32),
        "w_out": np.ascontiguousarray(inputs["L0_attn_w_out"], dtype=np.float32),
        "b_out": np.ascontiguousarray(inputs["L0_attn_b_out"], dtype=np.float32),
        "w1": np.ascontiguousarray(inputs["L0_ffn_w1"], dtype=np.float32),
        "b1": np.ascontiguousarray(inputs["L0_ffn_b1"], dtype=np.float32),
        "w2": np.ascontiguousarray(inputs["L0_ffn_w2"], dtype=np.float32),
        "b2": np.ascontiguousarray(inputs["L0_ffn_b2"], dtype=np.float32),
        "eye32": eye, "eye16": eye, "ones_in": ones_in, "e2_in": e2,
        "zsel_in": zsel,
    }
    return [dict(common, x=np.ascontiguousarray(x[i])) for i in range(8)]


def kernel(**inputs):
    nc = _get_nc(ALL_PHASES, None)
    in_maps = _make_in_maps(inputs)
    res = run_bass_kernel_spmd(nc, in_maps, core_ids=list(range(8)))
    return np.stack([np.asarray(r["y"], dtype=np.float32) for r in res.results])


# version 12
# speedup vs baseline: 1.0728x; 1.0728x over previous
import numpy as np
from contextlib import ExitStack

import concourse.bass as bass
import concourse.mybir as mybir
from concourse.bass_utils import run_bass_kernel_spmd

F32 = mybir.dt.float32
F16 = mybir.dt.float16
AF = mybir.ActivationFunctionType
OP = mybir.AluOpType
AX = mybir.AxisListType

NTOK = 4096
T = 512
NT = NTOK // T
D = 1024
DB = D // 128
H = 4096
HF = H // 128
HEADS = 16
HD = 64
FEAT = 256
QC = DB
M_QKV = 3 * DB
EPS_LN = 1e-5
EPS_Z = 1e-8

ENGINES = ("sync", "tensor", "vector", "scalar", "gpsimd")
ALL_PHASES = ("ffn1", "attn", "ffn2")


class Prog:

    def __init__(self, nc, stack):
        self.nc = nc
        self.stack = stack
        self.q = {e: [] for e in ENGINES}
        self.sems = {}
        self.cnt = {}

    def sem(self, *names):
        for n in names:
            self.sems[n] = self.stack.enter_context(self.nc.semaphore(n))
            self.cnt[n] = 0

    def do(self, eng, meth, *args, sem=None, amt=1, **kw):
        s = self.sems[sem] if sem is not None else None
        if sem is not None:
            self.cnt[sem] += amt

        def fn(e, meth=meth, args=args, kw=kw, s=s, amt=amt):
            inst = getattr(e, meth)(*args, **kw)
            if s is not None:
                inst.then_inc(s, amt)

        self.q[eng].append(fn)
        return self.cnt.get(sem)

    def wait(self, eng, sem, val=None):
        v = self.cnt[sem] if val is None else val
        if v <= 0:
            return
        s = self.sems[sem]
        self.q[eng].append(lambda e, s=s, v=v: e.wait_ge(s, v))

    def mark(self, sem):
        return self.cnt[sem]

    def flush_block(self, name=None):
        with self.nc.Block(name) as blk:
            for eng in ENGINES:
                fns = self.q[eng]
                if not fns:
                    continue
                getattr(blk, eng)(lambda e, fns=tuple(fns): [f(e) for f in fns])
        self.q = {e: [] for e in ENGINES}


def _build_nc(phases=ALL_PHASES, debug_out=None, taps=False, taps_full=False):
    nc = bass.Bass("TRN2", target_bir_lowering=False)

    x_in = nc.dram_tensor("x", [NTOK, D], F32, kind="ExternalInput").ap()
    w_qkv = nc.dram_tensor("w_qkv", [D, 3 * D], F32, kind="ExternalInput").ap()
    proj = nc.dram_tensor("proj", [HEADS, HD, FEAT], F32, kind="ExternalInput").ap()
    w_out = nc.dram_tensor("w_out", [D, D], F32, kind="ExternalInput").ap()
    b_out = nc.dram_tensor("b_out", [D], F32, kind="ExternalInput").ap()
    w1 = nc.dram_tensor("w1", [H, D], F32, kind="ExternalInput").ap()
    b1 = nc.dram_tensor("b1", [H], F32, kind="ExternalInput").ap()
    w2 = nc.dram_tensor("w2", [D, H], F32, kind="ExternalInput").ap()
    b2 = nc.dram_tensor("b2", [D], F32, kind="ExternalInput").ap()
    eye32 = nc.dram_tensor("eye32", [128, 128], F32, kind="ExternalInput").ap()
    eye16 = nc.dram_tensor("eye16", [128, 128], F32, kind="ExternalInput").ap()
    ones_in = nc.dram_tensor("ones_in", [128, 2], F32, kind="ExternalInput").ap()
    e2_in = nc.dram_tensor("e2_in", [2, 128], F32, kind="ExternalInput").ap()
    zsel_in = nc.dram_tensor("zsel_in", [2, 128], F32, kind="ExternalInput").ap()

    if debug_out is None:
        y = nc.dram_tensor("y", [NTOK, D], F32, kind="ExternalOutput").ap()
    else:
        y = nc.dram_tensor("y", [128, DB, NTOK], F32, kind="ExternalOutput").ap()
    tap_aps = {}
    if taps:
        for nm, shape in (("t_Xf", [128, DB * T]), ("t_xnh", [128, DB * T]),
                          ("t_xnl", [128, DB * T]), ("t_rows", [1, 4 * T]),
                          ("t_a1", [128, HF]), ("t_b1t", [128, HF]),
                          ("t_sc2", [128, DB]), ("t_bb2", [128, DB]),
                          ("t_s1", [128, 1024]), ("t_s2", [128, 1024])):
            tap_aps[nm] = nc.dram_tensor(nm, shape, F32, kind="ExternalOutput").ap()
        for nm in ("t_hh", "t_hl"):
            tap_aps[nm] = nc.dram_tensor(nm, [128, HF * T], F16,
                                         kind="ExternalOutput").ap()
        tap_aps["t_kv"] = nc.dram_tensor("t_kv", [128, HEADS * FEAT], F32,
                                         kind="ExternalOutput").ap()
        for nm, wd in (("t_kv65h", HEADS * 2 * 65), ("t_kv65l", HEADS * 2 * 65),
                       ("t_ath", DB * T), ("t_atl", DB * T),
                       ("t_qsh", DB * T), ("t_qsl", DB * T)):
            tap_aps[nm] = nc.dram_tensor(nm, [128, wd], F16,
                                         kind="ExternalOutput").ap()
        for nm, wd in (("t_kh", DB * T), ("t_kl", DB * T), ("t_vh", 4 * 1040),
                       ("t_vl", 4 * 1040), ("t_qh2", DB * T), ("t_ql2", DB * T),
                       ("t_keh", 4 * FEAT), ("t_kel", 4 * FEAT)):
            tap_aps[nm] = nc.dram_tensor(nm, [128, wd], F16,
                                         kind="ExternalOutput").ap()

    Xa = nc.dram_tensor("Xa", [128, DB, NTOK], F32).ap()
    Xb = nc.dram_tensor("Xb", [128, DB, NTOK], F32).ap()
    s1T = nc.dram_tensor("s1T", [128, HF, DB, 128], F16).ap()
    s2T = nc.dram_tensor("s2T", [128, DB, HF, 128], F16).ap()
    wqh = nc.dram_tensor("wqh", [128, M_QKV, DB, 128], F16).ap()
    wql = nc.dram_tensor("wql", [128, M_QKV, DB, 128], F16).ap()
    woh = nc.dram_tensor("woh", [128, DB, DB, 128], F16).ap()
    wol = nc.dram_tensor("wol", [128, DB, DB, 128], F16).ap()
    projhl = nc.dram_tensor("projhl", [128, 2, HEADS * FEAT], F16).ap()
    qhh = nc.dram_tensor("qhh", [128, QC, NTOK], F16).ap()
    qhl = nc.dram_tensor("qhl", [128, QC, NTOK], F16).ap()

    with ExitStack() as st:
        def sb(name, shape, dt=F32):
            return st.enter_context(nc.sbuf_tensor(name, shape, dt))

        p = Prog(nc, st)
        p.sem("ld", "wr", "sg", "tr", "ev", "vx", "sx", "gx", "mm",
              "row", "hs", "s2l", "gel", "epi",
              *[f"ldx{c}" for c in range(8)],
              *[f"wrx{c}" for c in range(8)],
              *[f"qwr{c}" for c in range(8)],
              "s2a", "s2b", "wqa", "wqb", "s1a", "s1b")

        eyeS = sb("eyeS", [128, 128], F32)
        eyeH = sb("eyeH", [128, 128], F16)
        ones1 = sb("ones1", [128, 2], F32)
        onesr = sb("onesr", [2, 128], F32)
        a1 = sb("a1", [128, HF], F32)
        b1t = sb("b1t", [128, HF], F32)
        sc2 = sb("sc2", [128, DB], F32)
        bb2 = sb("bb2", [128, DB], F32)
        bo = sb("bo", [128, DB], F32)
        zselS = sb("zselS", [2, 128], F32)
        ones16S = sb("ones16S", [128, 1], F16)
        rows = sb("rows", [1, T * 4], F32)
        psum = st.enter_context(nc.psum_tensor("psum", [128, 6, 512], F32))
        psumh = st.enter_context(nc.psum_tensor("psumh", [128, 2, 1024], F16))

        def pbank(i, p0=0, p1=128, f0=0, f1=512):
            return psum[p0:p1, i, f0:f1]

        def phbank(i, p0=0, p1=128, f0=0, f1=128):
            return psumh[p0:p1, i, f0:f1]

        p.do("sync", "dma_start", sem="ld", amt=16, out=eyeS[:], in_=eye32)
        p.do("sync", "dma_start", sem="ld", amt=16, out=ones1[:], in_=ones_in)
        p.do("sync", "dma_start", sem="ld", amt=16, out=onesr[:], in_=e2_in)
        p.do("gpsimd", "dma_start", sem="wr", amt=16, out=eyeH[:], in_=eye16)
        p.do("sync", "dma_start", sem="ld", amt=16, out=zselS[:], in_=zsel_in)
        p.do("gpsimd", "dma_start", sem="wr", amt=16, out=ones16S[:],
             in_=ones_in[:, 0:1], allow_slow_non_contiguous=True)

        with ExitStack() as st0:
            def sb0(name, shape, dt=F32):
                return st0.enter_context(nc.sbuf_tensor(name, shape, dt))

            wld = [sb0(f"wld{i}", [128, H], F32) for i in range(2)]
            sgn = [sb0(f"sgn{i}", [128, H], F16) for i in range(2)]
            stg = sb0("stg", [128, HF * 128], F16)
            hi_t = sb0("hi_t", [128, 4 * D], F16)
            lo_t = sb0("lo_t", [128, 4 * D], F16)
            araw = sb0("araw", [128, HF], F32)
            brow = sb0("brow", [HF, 128], F32)

            p.wait("vector", "ld")
            p.wait("scalar", "ld")
            p.wait("tensor", "ld")
            p.wait("tensor", "wr")

            def prep_sign(n_jb, n_db, w_ap, scr_ap, a_tile):
                cons = {}
                bank_ev = {}
                wr_marks = {}
                for jb in range(n_jb):
                    buf = jb % 2
                    if jb >= 2:
                        vxm, sgm, trm = cons[jb - 2]
                        p.wait("sync", "vx", vxm)
                        p.wait("sync", "sg", sgm)
                    else:
                        p.wait("sync", "vx")
                        p.wait("sync", "sg")
                    ldc = p.do("sync", "dma_start", sem="ld", amt=16,
                               out=wld[buf][:, 0:n_db * 128],
                               in_=w_ap[jb * 128:(jb + 1) * 128, :])
                    p.wait("vector", "ld", ldc)
                    vxm = p.do("vector", "reduce_sum", a_tile[:, jb:jb + 1],
                               wld[buf][:, 0:n_db * 128], axis=AX.X,
                               apply_absolute_value=True, sem="vx")
                    p.wait("scalar", "ld", ldc)
                    if jb >= 2:
                        p.wait("scalar", "tr", cons[jb - 2][2])
                    if jb >= 1:
                        p.wait("scalar", "wr", wr_marks[jb - 1])
                    sgm = p.do("scalar", "sign", sgn[buf][:, 0:n_db * 128],
                               wld[buf][:, 0:n_db * 128], sem="sg")
                    p.wait("tensor", "sg", sgm)
                    for db in range(n_db):
                        bk = db % 2
                        if bk in bank_ev:
                            p.wait("tensor", "ev", bank_ev[bk])
                        p.do("tensor", "transpose", phbank(bk),
                             sgn[buf][:, db * 128:(db + 1) * 128], eyeH[:],
                             sem="tr")
                        trm = p.mark("tr")
                        p.wait("scalar", "tr", trm)
                        bank_ev[bk] = p.do("scalar", "copy",
                                           stg[:, db * 128:(db + 1) * 128],
                                           phbank(bk), sem="ev")
                    p.wait("gpsimd", "ev", p.mark("ev"))
                    wr_marks[jb] = p.do("gpsimd", "dma_start", sem="wr", amt=16,
                                        out=scr_ap[:, jb, :, :].rearrange("p a b -> p (a b)"),
                                        in_=stg[:, 0:n_db * 128])
                    cons[jb] = (vxm, sgm, trm)

            prep_sign(HF, DB, w1, s1T, araw)
            p.wait("vector", "vx")
            p.do("vector", "tensor_scalar_mul", a1[:], araw[:, 0:HF], 1.0 / D)
            prep_sign(DB, HF, w2, s2T, sc2)
            p.wait("vector", "vx")
            p.do("vector", "tensor_scalar_mul", sc2[:], sc2[:, 0:DB], 0.5 / H,
                 sem="vx")

            def prep_bias(b_ap, n, out_tile, scale):
                p.wait("sync", "tr")
                ldc = p.do("sync", "dma_start", sem="ld", amt=16,
                           out=brow[0:n, :],
                           in_=b_ap.rearrange("(a b) -> a b", b=128))
                p.wait("tensor", "ld", ldc)
                p.wait("tensor", "ev")
                p.do("tensor", "transpose", pbank(4, 0, 128, 0, n),
                     brow[0:n, :], eyeS[0:n, 0:n], sem="tr")
                p.wait("vector", "tr", p.mark("tr"))
                p.do("vector", "tensor_scalar_mul", out_tile[:, 0:n],
                     pbank(4, 0, 128, 0, n), scale, sem="vx")

            prep_bias(b1, HF, b1t, 1.0)
            prep_bias(b2, DB, bb2, 0.5)
            prep_bias(b_out, DB, bo, 1.0)

            def prep_split(w_ap, n_db, n_out, hi_scr, lo_scr):
                for db in range(n_db):
                    buf = db % 2
                    p.wait("sync", "vx")
                    p.wait("sync", "sg")
                    ldc = p.do("sync", "dma_start", sem="ld", amt=16,
                               out=wld[buf][:, 0:n_out],
                               in_=w_ap[db * 128:(db + 1) * 128, :])
                    p.wait("vector", "ld", ldc)
                    p.wait("vector", "wr")
                    p.do("vector", "tensor_copy", hi_t[:, 0:n_out],
                         wld[buf][:, 0:n_out])
                    vc = p.do("vector", "tensor_sub", lo_t[:, 0:n_out],
                              wld[buf][:, 0:n_out], hi_t[:, 0:n_out], sem="vx")
                    p.wait("gpsimd", "vx", vc)
                    p.do("gpsimd", "dma_start", sem="wr", amt=16,
                         out=hi_scr[:, :, db, :],
                         in_=hi_t[:, 0:n_out].rearrange("p (a b) -> p a b", b=128))
                    p.do("gpsimd", "dma_start", sem="wr", amt=16,
                         out=lo_scr[:, :, db, :],
                         in_=lo_t[:, 0:n_out].rearrange("p (a b) -> p a b", b=128))

            prep_split(w_qkv, DB, 3 * D, wqh, wql)
            prep_split(w_out, DB, D, woh, wol)

            p.wait("sync", "vx")
            p.wait("sync", "sg")
            p.do("sync", "dma_start", sem="ld", amt=16,
                 out=wld[0][0:64, 0:HEADS * FEAT].rearrange(
                     "d (h f) -> d h f", f=FEAT),
                 in_=proj.rearrange("h d f -> d h f"))
            ldc = p.do("sync", "dma_start", sem="ld", amt=16,
                       out=wld[0][64:128, 0:HEADS * FEAT].rearrange(
                           "d (h f) -> d h f", f=FEAT),
                       in_=proj.rearrange("h d f -> d h f"))
            p.wait("vector", "ld", ldc)
            p.wait("vector", "wr")
            p.do("vector", "tensor_copy", hi_t[:, 0:HEADS * FEAT],
                 wld[0][:, 0:HEADS * FEAT])
            vc = p.do("vector", "tensor_sub", lo_t[:, 0:HEADS * FEAT],
                      wld[0][:, 0:HEADS * FEAT], hi_t[:, 0:HEADS * FEAT],
                      sem="vx")
            p.wait("gpsimd", "vx", vc)
            p.do("gpsimd", "dma_start", sem="wr", amt=16,
                 out=projhl[:, 0, :], in_=hi_t[:, 0:HEADS * FEAT])
            p.do("gpsimd", "dma_start", sem="wr", amt=16,
                 out=projhl[:, 1, :], in_=lo_t[:, 0:HEADS * FEAT])
            p.wait("gpsimd", "wr")
            p.wait("sync", "wr")
            p.wait("sync", "ld")
            p.flush_block("prep")

        def ffn_phase(src, dst, name):
            with ExitStack() as stf:
                def sbf(nm, shape, dt=F32):
                    return stf.enter_context(nc.sbuf_tensor(name + nm, shape, dt))

                s1c = [sbf(f"s1c{i}", [128, DB * 128], F16) for i in range(2)]
                Xf2 = [sbf(f"Xf{i}", [128, DB * T], F32) for i in range(2)]
                xtm = sbf("xtm", [128, D], F32)
                xnh2 = [sbf(f"xnh{i}", [128, DB * T], F16) for i in range(2)]
                xnl2 = [sbf(f"xnl{i}", [128, DB * T], F16) for i in range(2)]
                hh = sbf("hh", [128, HF * T], F16)
                hl = sbf("hl", [128, HF * T], F16)
                gbuf = [sbf(f"g{i}", [128, T], F32) for i in range(2)]
                sqd = [sbf(f"sq{i}", [128, T], F32) for i in range(2)]
                tbuf = sbf("tb", [128, T], F32)
                ubuf = sbf("ub", [128, T], F32)
                mS = sbf("mS", [128, T], F32)
                rS = sbf("rS", [128, T], F32)
                s2c = [sbf(f"s2c{i}", [128, HF * 128], F16) for i in range(2)]
                tmpb = sbf("tmpb", [128, T], F32)
                ytm = xtm

                sumsq_mm = {}
                mmA_m = {}
                ec_last = 0
                bc_last = 0
                wrm = {}
                ytr = {0: 0, 1: 0}
                yev = {0: 0, 1: 0}
                gel_marks = {}
                hs_marks = {}
                epi_marks = {}
                mmB_marks = {}
                bank_ev = {}
                nt_eff = NT if not (taps and not taps_full) else 1

                def emit_loadln(tt):
                    nonlocal bc_last
                    t0 = tt * T
                    Xf = Xf2[tt % 2]
                    xnh, xnl = xnh2[tt % 2], xnl2[tt % 2]
                    if src == "xin":
                        if tt > 1:
                            p.wait("scalar", "epi")
                            p.wait("scalar", "wr")
                        p.wait("tensor", "sx", bc_last)
                        for ts in range(4):
                            if ts > 0:
                                p.wait("sync", "tr")
                            ldc = p.do("sync", "dma_start", sem="ld", amt=16,
                                       out=xtm[:],
                                       in_=x_in[t0 + ts * 128:t0 + (ts + 1) * 128, :])
                            p.wait("tensor", "ld", ldc)
                            for c in range(DB):
                                bk = c % 2
                                if bk in bank_ev:
                                    p.wait("tensor", "ev", bank_ev[bk])
                                p.do("tensor", "transpose",
                                     pbank(bk, 0, 128, 0, 128),
                                     xtm[:, c * 128:(c + 1) * 128], eyeS[:],
                                     sem="tr")
                                p.wait("scalar", "tr", p.mark("tr"))
                                bank_ev[bk] = p.do(
                                    "scalar", "copy",
                                    Xf[:, c * T + ts * 128:c * T + (ts + 1) * 128],
                                    pbank(bk, 0, 128, 0, 128), sem="ev")
                        xf_sem, xf_ready = "ev", p.mark("ev")
                        ldsems = {c: "ev" for c in range(DB)}
                        ldcs = {c: xf_ready for c in range(DB)}
                    else:
                        _, src_ap = src
                        ldcs = {}
                        ldsems = {}
                        for c in range(DB):
                            if tt > 1:
                                if dst != "yout":
                                    p.wait("sync", f"wrx{c}", wrm[(tt % 2, c)])
                                else:
                                    p.wait("sync", "tr", ytr[tt % 2])
                            ldsems[c] = f"ldx{c}"
                            ldcs[c] = p.do("sync", "dma_start", sem=ldsems[c],
                                           amt=16,
                                           out=Xf[:, c * T:(c + 1) * T],
                                           in_=src_ap[:, c, t0:t0 + T])
                        xf_sem, xf_ready = ldsems[DB - 1], ldcs[DB - 1]

                    for c in range(DB):
                        p.wait("scalar", ldsems[c], ldcs[c])
                        if c >= 2:
                            p.wait("scalar", "mm", sumsq_mm[c - 2])
                        elif tt > 0:
                            p.wait("scalar", "mm", sumsq_mm[DB - 2 + c])
                        sqc = p.do("scalar", "activation", sqd[c % 2][:],
                                   Xf[:, c * T:(c + 1) * T], AF.Square, sem="sx")
                        if c == 0 and src != "xin":
                            p.wait("tensor", "sx", bc_last)
                            if dst == "yout" and tt > 1:
                                p.wait("tensor", "ev", yev[tt % 2])
                        p.wait("tensor", ldsems[c], ldcs[c])
                        p.do("tensor", "matmul", pbank(0, 0, 1), ones1[:, 0:1],
                             Xf[:, c * T:(c + 1) * T],
                             start=(c == 0), stop=(c == DB - 1))
                        p.wait("tensor", "sx", sqc)
                        sumsq_mm[c] = p.do("tensor", "matmul", pbank(1, 0, 1),
                                           ones1[:, 0:1], sqd[c % 2][:],
                                           start=(c == 0), stop=(c == DB - 1),
                                           sem="mm")
                    stats_mm = p.mark("mm")
                    R = rows
                    p.wait("vector", "mm", stats_mm)
                    p.do("vector", "tensor_scalar_mul", R[:, 0:T], pbank(0, 0, 1),
                         1.0 / D)
                    p.do("vector", "tensor_scalar_mul", R[:, T:2 * T],
                         pbank(1, 0, 1), 1.0 / D, sem="row")
                    p.wait("scalar", "row", p.mark("row"))
                    p.do("scalar", "activation", R[:, 2 * T:3 * T], R[:, 0:T],
                         AF.Square, sem="row")
                    p.wait("vector", "row", p.mark("row"))
                    p.do("vector", "tensor_sub", R[:, 3 * T:4 * T], R[:, T:2 * T],
                         R[:, 2 * T:3 * T])
                    p.do("vector", "tensor_scalar_add", R[:, 3 * T:4 * T],
                         R[:, 3 * T:4 * T], EPS_LN, sem="row")
                    p.wait("scalar", "row", p.mark("row"))
                    p.do("scalar", "activation", R[:, 2 * T:3 * T],
                         R[:, 3 * T:4 * T], AF.Sqrt, sem="row")
                    p.wait("vector", "row", p.mark("row"))
                    rowc = p.do("vector", "reciprocal", R[:, T:2 * T],
                                R[:, 2 * T:3 * T], sem="row")
                    p.wait("tensor", "row", rowc)
                    p.do("tensor", "matmul", pbank(0), onesr[0:1, :], R[:, 0:T],
                         start=True, stop=True)
                    bc_mm = p.do("tensor", "matmul", pbank(1), onesr[0:1, :],
                                 R[:, T:2 * T], start=True, stop=True,
                                 sem="mm")
                    p.wait("scalar", "mm", bc_mm)
                    p.do("scalar", "copy", mS[:], pbank(0))
                    bc_cp = p.do("scalar", "copy", rS[:], pbank(1), sem="sx")
                    bc_last = bc_cp
                    for c in range(DB):
                        cs = slice(c * T, (c + 1) * T)
                        if c == 0:
                            p.wait("vector", "sx", bc_cp)
                        p.do("vector", "tensor_sub", tbuf[:], Xf[:, cs], mS[:])
                        p.do("vector", "tensor_mul", ubuf[:], tbuf[:], rS[:])
                        p.do("vector", "tensor_copy", xnh[:, cs], ubuf[:])
                        p.do("vector", "tensor_sub", xnl[:, cs], ubuf[:],
                             xnh[:, cs], sem="vx")
                    xn_ready = p.mark("vx")
                    if taps and tt == 0 and name == "ffn1":
                        p.wait("gpsimd", xf_sem, xf_ready)
                        p.do("gpsimd", "dma_start", sem="wr", amt=16,
                             out=tap_aps["t_Xf"], in_=Xf[:])
                        p.wait("gpsimd", "vx", xn_ready)
                        p.do("gpsimd", "dma_start", sem="wr", amt=16,
                             out=tap_aps["t_xnh"], in_=xnh[:])
                        p.do("gpsimd", "dma_start", sem="wr", amt=16,
                             out=tap_aps["t_xnl"], in_=xnl[:])
                        p.wait("gpsimd", "row", rowc)
                        p.do("gpsimd", "dma_start", sem="wr", amt=16,
                             out=tap_aps["t_rows"], in_=rows[:])
                        p.do("gpsimd", "dma_start", sem="wr", amt=16,
                             out=tap_aps["t_a1"], in_=a1[:])
                        p.do("gpsimd", "dma_start", sem="wr", amt=16,
                             out=tap_aps["t_b1t"], in_=b1t[:])
                        p.do("gpsimd", "dma_start", sem="wr", amt=16,
                             out=tap_aps["t_sc2"], in_=sc2[:])
                        p.do("gpsimd", "dma_start", sem="wr", amt=16,
                             out=tap_aps["t_bb2"], in_=bb2[:])
                        p.do("gpsimd", "dma_start", sem="wr", amt=16,
                             out=tap_aps["t_s1"], in_=s1c[0][:, 0:1024])
                    return xn_ready

                xn_ready_by = {0: emit_loadln(0)}

                for tt in range(nt_eff):
                    t0 = tt * T
                    Xf = Xf2[tt % 2]
                    xnh, xnl = xnh2[tt % 2], xnl2[tt % 2]
                    xn_ready = xn_ready_by[tt]

                    p.wait("tensor", "vx", xn_ready)
                    for hf in range(HF):
                        ga = tt * HF + hf
                        s1sem = "s1a" if hf % 2 == 0 else "s1b"
                        if ga >= 2:
                            p.wait("sync", "mm", mmA_m[ga - 2])
                        s1l = p.do("sync", "dma_start", sem=s1sem, amt=16,
                                   out=s1c[hf % 2][:],
                                   in_=s1T[:, hf, :, :].rearrange(
                                       "p a b -> p (a b)"))
                        bk = 4 + (hf % 2)
                        if hf >= 2:
                            p.wait("tensor", "gel", gel_marks[hf - 2])
                        p.wait("tensor", s1sem, s1l)
                        for pi, xn in enumerate((xnh, xnl)):
                            for db in range(DB):
                                last = (pi == 1 and db == DB - 1)
                                p.do("tensor", "matmul", pbank(bk),
                                     s1c[hf % 2][:, db * 128:(db + 1) * 128],
                                     xn[:, db * T:(db + 1) * T],
                                     start=(pi == 0 and db == 0), stop=last,
                                     sem="mm" if last else None)
                        mmA_m[ga] = mmA = p.mark("mm")
                        p.wait("scalar", "mm", mmA)
                        if hf >= 2:
                            p.wait("scalar", "hs", hs_marks[hf - 2])
                        gel_marks[hf] = p.do("scalar", "activation",
                                             gbuf[hf % 2][:], pbank(bk), AF.Gelu,
                                             bias=b1t[:, hf:hf + 1],
                                             scale=a1[:, hf:hf + 1], sem="gel")
                        p.wait("vector", "gel", gel_marks[hf])
                        hsl = slice(hf * T, (hf + 1) * T)
                        p.do("vector", "tensor_copy", hh[:, hsl], gbuf[hf % 2][:])
                        hs_marks[hf] = p.do("vector", "tensor_sub", hl[:, hsl],
                                            gbuf[hf % 2][:], hh[:, hsl], sem="hs")
                    h_ready = p.mark("hs")

                    if tt + 1 < nt_eff:
                        xn_ready_by[tt + 1] = emit_loadln(tt + 1)

                    for c in range(DB):
                        key = c % 2
                        if (tt * DB + c) >= 2:
                            prev = tt * DB + c - 2
                            p.wait("sync", "mm", mmB_marks[prev])
                        s2sem = "s2a" if key == 0 else "s2b"
                        lds2 = p.do("sync", "dma_start", sem=s2sem, amt=16,
                                    out=s2c[key][:],
                                    in_=s2T[:, c, :, :].rearrange("p a b -> p (a b)"))
                        bk = 2 + key
                        p.wait("tensor", s2sem, lds2)
                        if c == 0:
                            p.wait("tensor", "hs", h_ready)
                            if tt > 0:
                                p.wait("tensor", "sx", ec_last)
                        if (tt * DB + c) >= 2:
                            p.wait("tensor", "epi", epi_marks[tt * DB + c - 2])
                        for pi, hx in enumerate((hh, hl)):
                            for jb in range(HF):
                                last = (pi == 1 and jb == HF - 1)
                                p.do("tensor", "matmul", pbank(bk),
                                     s2c[key][:, jb * 128:(jb + 1) * 128],
                                     hx[:, jb * T:(jb + 1) * T],
                                     start=(pi == 0 and jb == 0), stop=last,
                                     sem="mm" if last else None)
                        mmB_marks[tt * DB + c] = p.mark("mm")
                        p.wait("scalar", "mm", mmB_marks[tt * DB + c])
                        if (tt * DB + c) >= 1:
                            p.wait("scalar", "epi", epi_marks[tt * DB + c - 1])
                        ec = p.do("scalar", "activation", tmpb[:], pbank(bk),
                                  AF.Identity, bias=bb2[:, c:c + 1],
                                  scale=sc2[:, c:c + 1], sem="sx")
                        ec_last = ec
                        p.wait("vector", "sx", ec)
                        cs = slice(c * T, (c + 1) * T)
                        epi_marks[tt * DB + c] = p.do("vector", "tensor_add",
                                                      Xf[:, cs], Xf[:, cs],
                                                      tmpb[:], sem="epi")
                        if dst != "yout":
                            _, dst_ap = dst
                            p.wait("gpsimd", "epi", epi_marks[tt * DB + c])
                            wrm[(tt % 2, c)] = p.do("gpsimd", "dma_start",
                                                    sem=f"wrx{c}", amt=16,
                                                    out=dst_ap[:, c, t0:t0 + T],
                                                    in_=Xf[:, cs])

                    if taps and tt == 0 and name == "ffn1":
                        p.wait("gpsimd", "s2a")
                        p.wait("gpsimd", "s2b")
                        p.do("gpsimd", "dma_start", sem="wr", amt=16,
                             out=tap_aps["t_s2"], in_=s2c[1][:, 0:1024])
                    if dst == "yout":
                        x_done = p.mark("epi")
                        p.wait("tensor", "epi", x_done)
                        p.wait("tensor", "sx", bc_last)
                        for ts in range(4):
                            if ts > 0:
                                p.wait("scalar", "wr")
                            for c in range(DB):
                                bk = c % 2
                                if bk in bank_ev:
                                    p.wait("tensor", "ev", bank_ev[bk])
                                p.do("tensor", "transpose",
                                     pbank(bk, 0, 128, 0, 128),
                                     Xf[:, c * T + ts * 128:c * T + (ts + 1) * 128],
                                     eyeS[:], sem="tr")
                                p.wait("scalar", "tr", p.mark("tr"))
                                bank_ev[bk] = p.do("scalar", "copy",
                                                   ytm[:, c * 128:(c + 1) * 128],
                                                   pbank(bk, 0, 128, 0, 128),
                                                   sem="ev")
                            p.wait("gpsimd", "ev", p.mark("ev"))
                            p.do("gpsimd", "dma_start", sem="wr", amt=16,
                                 out=y[t0 + ts * 128:t0 + (ts + 1) * 128, :],
                                 in_=ytm[:])
                        ytr[tt % 2] = p.mark("tr")
                        yev[tt % 2] = p.mark("ev")

                p.wait("gpsimd", "wr")
                p.wait("gpsimd", "s2a")
                p.wait("gpsimd", "s2b")
                p.wait("gpsimd", "ld")
                p.flush_block(name)

        def attn_phase():
            with ExitStack() as sta:
                def sba(nm, shape, dt=F32):
                    return sta.enter_context(nc.sbuf_tensor("at" + nm, shape, dt))

                kv_acc = sba("kvacc", [128, HEADS * FEAT], F32)
                kv65h = sba("kv65h", [128, HEADS * 2 * 65], F16)
                kv65l = sba("kv65l", [128, HEADS * 2 * 65], F16)
                projS_h = sba("pjh", [128, HEADS * FEAT], F16)
                projS_l = sba("pjl", [128, HEADS * FEAT], F16)
                p.do("sync", "dma_start", sem="ld", amt=16, out=projS_h[:],
                     in_=projhl[:, 0, :])
                pjld = p.do("sync", "dma_start", sem="ld", amt=16,
                            out=projS_l[:], in_=projhl[:, 1, :])
                p.wait("tensor", "ld", pjld)

                with ExitStack() as st1:
                    def sb1(nm, shape, dt=F32):
                        return st1.enter_context(nc.sbuf_tensor("a1" + nm, shape, dt))

                    Xf = sb1("Xf", [128, DB * T], F32)
                    xnh2 = [sb1(f"xnh{i}", [128, DB * T], F16) for i in range(2)]
                    xnl2 = [sb1(f"xnl{i}", [128, DB * T], F16) for i in range(2)]
                    k_hi = sb1("kh", [128, DB * T], F16)
                    k_lo = sb1("kl", [128, DB * T], F16)
                    qs_h = sb1("qsh", [128, DB * T], F16)
                    qs_l = sb1("qsl", [128, DB * T], F16)
                    vst = [sb1(f"vst{i}", [128, T], F16) for i in range(2)]
                    vsl = [sb1(f"vsl{i}", [128, T], F16) for i in range(2)]
                    vtmh = sb1("vtmh", [128, 4 * 1040], F16)
                    vtml = sb1("vtml", [128, 4 * 1040], F16)
                    wqs_h = [sb1(f"wqh{i}", [128, DB * 128], F16) for i in range(2)]
                    wqs_l = [sb1(f"wql{i}", [128, DB * 128], F16) for i in range(2)]
                    ke_h = [sb1(f"keh{i}", [128, 4 * FEAT], F16) for i in range(2)]
                    ke_l = [sb1(f"kel{i}", [128, 4 * FEAT], F16) for i in range(2)]
                    el_m = sb1("elm", [128, 4 * FEAT], F32)
                    el_f = sb1("elf", [128, 4 * FEAT], F32)
                    sqb1 = [sb1(f"sq{i}", [128, T], F32) for i in range(2)]
                    tbuf = [sb1(f"tb{i}", [128, T], F32) for i in range(2)]
                    ubuf = [sb1(f"ub{i}", [128, T], F32) for i in range(2)]
                    mS = sb1("mS", [128, T], F32)
                    rS = sb1("rS", [128, T], F32)

                    zer = p.do("gpsimd", "memset", kv_acc[:], 0.0)
                    p.do("gpsimd", "memset", vtmh[:], 1.0)
                    zer = p.do("gpsimd", "memset", vtml[:], 0.0, sem="gx")

                    sumsq_mm = {}
                    kel_m = {}
                    kvm_m = {}
                    mmQ = {}
                    evq = {}
                    kva = {}
                    elu_ps = {}
                    vtr_ev = {}
                    q_wr = {}

                    lnlo = {}
                    hic_last = 0
                    bc1_last = 0
                    nt1 = NT if not (taps and not taps_full) else 1

                    def p1_loadln(tt):
                        nonlocal bc1_last
                        t0 = tt * T
                        xnh, xnl = xnh2[tt % 2], xnl2[tt % 2]
                        ldcs1 = {}
                        for c in range(DB):
                            if tt > 0:
                                p.wait("sync", "vx", lnlo[c])
                            ldcs1[c] = p.do("sync", "dma_start",
                                            sem=f"ldx{c}", amt=16,
                                            out=Xf[:, c * T:(c + 1) * T],
                                            in_=Xa[:, c, t0:t0 + T])
                        for c in range(DB):
                            p.wait("gpsimd", f"ldx{c}", ldcs1[c])
                            if c >= 2:
                                p.wait("gpsimd", "mm", sumsq_mm[c - 2])
                            elif tt > 0:
                                p.wait("gpsimd", "mm", sumsq_mm[DB - 2 + c])
                            sqc = p.do("gpsimd", "tensor_mul", sqb1[c % 2][:],
                                       Xf[:, c * T:(c + 1) * T],
                                       Xf[:, c * T:(c + 1) * T], sem="gx")
                            if c == 0 and tt > 0:
                                p.wait("tensor", "sx", hic_last)
                                p.wait("tensor", "vx",
                                       evq[tt * M_QKV - 1])
                            p.wait("tensor", f"ldx{c}", ldcs1[c])
                            p.do("tensor", "matmul", pbank(2, 0, 1),
                                 ones1[:, 0:1], Xf[:, c * T:(c + 1) * T],
                                 start=(c == 0), stop=(c == DB - 1))
                            p.wait("tensor", "gx", sqc)
                            sumsq_mm[c] = p.do("tensor", "matmul",
                                               pbank(3, 0, 1), ones1[:, 0:1],
                                               sqb1[c % 2][:],
                                               start=(c == 0),
                                               stop=(c == DB - 1), sem="mm")
                        stats_mm = p.mark("mm")
                        R = rows
                        p.wait("vector", "mm", stats_mm)
                        p.do("vector", "tensor_scalar_mul", R[:, 0:T],
                             pbank(2, 0, 1), 1.0 / D)
                        p.do("vector", "tensor_scalar_mul", R[:, T:2 * T],
                             pbank(3, 0, 1), 1.0 / D, sem="row")
                        p.wait("scalar", "row", p.mark("row"))
                        p.do("scalar", "activation", R[:, 2 * T:3 * T],
                             R[:, 0:T], AF.Square, sem="row")
                        p.wait("vector", "row", p.mark("row"))
                        p.do("vector", "tensor_sub", R[:, 3 * T:4 * T],
                             R[:, T:2 * T], R[:, 2 * T:3 * T])
                        p.do("vector", "tensor_scalar_add", R[:, 3 * T:4 * T],
                             R[:, 3 * T:4 * T], EPS_LN, sem="row")
                        p.wait("scalar", "row", p.mark("row"))
                        p.do("scalar", "activation", R[:, 2 * T:3 * T],
                             R[:, 3 * T:4 * T], AF.Sqrt, sem="row")
                        p.wait("vector", "row", p.mark("row"))
                        rowc = p.do("vector", "reciprocal", R[:, T:2 * T],
                                    R[:, 2 * T:3 * T], sem="row")
                        p.wait("tensor", "row", rowc)
                        p.do("tensor", "matmul", pbank(2), onesr[0:1, :],
                             R[:, 0:T], start=True, stop=True)
                        bc_mm = p.do("tensor", "matmul", pbank(3),
                                     onesr[0:1, :], R[:, T:2 * T],
                                     start=True, stop=True, sem="mm")
                        p.wait("scalar", "mm", bc_mm)
                        p.do("scalar", "copy", mS[:], pbank(2))
                        bc_cp = p.do("scalar", "copy", rS[:], pbank(3),
                                     sem="sx")
                        bc1_last = bc_cp

                        def apply_chunk(c):
                            cs = slice(c * T, (c + 1) * T)
                            if c == 0:
                                p.wait("vector", "sx", bc_cp)
                            p.do("vector", "tensor_sub", tbuf[0][:],
                                 Xf[:, cs], mS[:])
                            p.do("vector", "tensor_mul", ubuf[0][:],
                                 tbuf[0][:], rS[:])
                            p.do("vector", "tensor_copy", xnh[:, cs],
                                 ubuf[0][:])
                            lnlo[c] = p.do("vector", "tensor_sub", xnl[:, cs],
                                           ubuf[0][:], xnh[:, cs], sem="vx")
                            return lnlo[c]

                        return apply_chunk

                    def p1_apply_all(tt):
                        ap = p1_loadln(tt)
                        for c in range(DB):
                            ap(c)
                        return p.mark("vx")

                    xnr1 = {0: p1_apply_all(0)}
                    for tt in range(nt1):
                        t0 = tt * T
                        xnh, xnl = xnh2[tt % 2], xnl2[tt % 2]
                        xn_ready = xnr1[tt]

                        p.wait("tensor", "vx", xn_ready)
                        for m in range(M_QKV):
                            key = m % 2
                            gstep = tt * M_QKV + m
                            if gstep >= 2:
                                p.wait("sync", "mm", mmQ[gstep - 2])
                            wqsem = "wqa" if key == 0 else "wqb"
                            lw1 = p.do("sync", "dma_start", sem=wqsem, amt=16,
                                       out=wqs_h[key][:],
                                       in_=wqh[:, m, :, :].rearrange(
                                           "p a b -> p (a b)"))
                            lw2 = p.do("sync", "dma_start", sem=wqsem, amt=16,
                                       out=wqs_l[key][:],
                                       in_=wql[:, m, :, :].rearrange(
                                           "p a b -> p (a b)"))
                            bk = 2 + (m % 4)
                            p.wait("tensor", wqsem, lw2)
                            if gstep >= 4:
                                p.wait("tensor", "vx", evq[gstep - 4])
                            first = True
                            for wop, xop in ((wqs_h[key], xnh), (wqs_l[key], xnh),
                                             (wqs_h[key], xnl)):
                                for db in range(DB):
                                    last = (wop is wqs_h[key] and xop is xnl
                                            and db == DB - 1)
                                    p.do("tensor", "matmul", pbank(bk),
                                         wop[:, db * 128:(db + 1) * 128],
                                         xop[:, db * T:(db + 1) * T],
                                         start=first, stop=last,
                                         sem="mm" if last else None)
                                    first = False
                            mmQ[gstep] = p.mark("mm")
                            p.wait("scalar", "mm", mmQ[gstep])
                            if m < DB:
                                qc = m
                                hidst = qs_h[:, qc * T:(qc + 1) * T]
                                lodst = qs_l[:, qc * T:(qc + 1) * T]
                                if tt > 0:
                                    p.wait("scalar", f"qwr{qc}",
                                           q_wr.get(qc, 0))
                            elif m < 2 * DB:
                                kc = m - DB
                                hidst = k_hi[:, kc * T:(kc + 1) * T]
                                lodst = k_lo[:, kc * T:(kc + 1) * T]
                            else:
                                vc = m - 2 * DB
                                hidst = vst[m % 2][:]
                                lodst = vsl[m % 2][:]
                                if m - 2 >= 2 * DB:
                                    p.wait("scalar", "tr",
                                           evq[("vtr", gstep - 2)])
                            hicp = p.do("scalar", "copy", hidst, pbank(bk),
                                        sem="sx")
                            hic_last = hicp
                            p.wait("vector", "sx", hicp)
                            evq[gstep] = p.do("vector", "tensor_sub", lodst,
                                              pbank(bk), hidst, sem="vx")
                            if m >= 2 * DB:
                                vc = m - 2 * DB
                                p.wait("tensor", "vx", evq[gstep])
                                for sub in range(4):
                                    for part, src in ((0, hidst), (1, lodst)):
                                        hb = sub % 2
                                        if (hb in vtr_ev):
                                            p.wait("tensor", "ev", vtr_ev[hb])
                                        p.do("tensor", "transpose", phbank(hb),
                                             src[:, sub * 128:(sub + 1) * 128],
                                             eyeH[:], sem="tr")
                                        p.wait("scalar", "tr", p.mark("tr"))
                                        dstt = (vtmh if part == 0 else vtml)
                                        o0 = sub * 1040 + (2 * vc) * 65
                                        o1 = sub * 1040 + (2 * vc + 1) * 65
                                        p.do("scalar", "copy",
                                             dstt[:, o0:o0 + 64],
                                             phbank(hb, 0, 128, 0, 64))
                                        vtr_ev[hb] = p.do(
                                            "scalar", "copy",
                                            dstt[:, o1:o1 + 64],
                                            phbank(hb, 0, 128, 64, 128),
                                            sem="ev")
                                evq[("vtr", gstep)] = p.mark("tr")
                        for qc in range(DB):
                            p.wait("gpsimd", "vx", evq[tt * M_QKV + qc])
                            p.do("gpsimd", "dma_start", sem=f"qwr{qc}", amt=16,
                                 out=qhh[:, qc, t0:t0 + T],
                                 in_=qs_h[:, qc * T:(qc + 1) * T])
                            q_wr[qc] = p.do("gpsimd", "dma_start",
                                            sem=f"qwr{qc}", amt=16,
                                            out=qhl[:, qc, t0:t0 + T],
                                            in_=qs_l[:, qc * T:(qc + 1) * T])

                        pend_apply = None
                        if tt + 1 < nt1:
                            pend_apply = p1_loadln(tt + 1)

                        vtm_done = p.mark("ev")
                        if tt == 0:
                            p.wait("vector", "gx", zer)
                        def emit_kv(h):
                            kvb = h % 2
                            if h >= 2:
                                p.wait("tensor", "epi", kva[tt * HEADS + h - 2])
                            elif tt > 0:
                                p.wait("tensor", "epi",
                                       kva[(tt - 1) * HEADS + HEADS - 2 + h])
                            p.wait("tensor", "vx", kel_m[h])
                            keh_b, kel_b = ke_h[h % 2], ke_l[h % 2]
                            for ts in range(4):
                                vh = vtmh[:, ts * 1040 + h * 65:
                                          ts * 1040 + (h + 1) * 65]
                                vl = vtml[:, ts * 1040 + h * 65:
                                          ts * 1040 + (h + 1) * 65]
                                keh_s = keh_b[:, ts * FEAT:(ts + 1) * FEAT]
                                kel_s = kel_b[:, ts * FEAT:(ts + 1) * FEAT]
                                p.do("tensor", "matmul",
                                     pbank(kvb, 0, 65, 0, FEAT), vh, keh_s,
                                     start=(ts == 0), stop=False,
                                     skip_group_check=True)
                                p.do("tensor", "matmul",
                                     pbank(kvb, 0, 65, 0, FEAT), vl, keh_s,
                                     start=False, stop=False,
                                     skip_group_check=True)
                                p.do("tensor", "matmul",
                                     pbank(kvb, 0, 65, 0, FEAT), vh, kel_s,
                                     start=False, stop=(ts == 3),
                                     sem="mm" if ts == 3 else None,
                                     skip_group_check=True)
                            kvm_m[tt * HEADS + h] = p.mark("mm")
                            p.wait("vector", "mm", kvm_m[tt * HEADS + h])
                            kva[tt * HEADS + h] = p.do(
                                "vector", "tensor_add",
                                kv_acc[0:65, h * FEAT:(h + 1) * FEAT],
                                pbank(kvb, 0, 65, 0, FEAT),
                                kv_acc[0:65, h * FEAT:(h + 1) * FEAT],
                                sem="epi")

                        for h in range(HEADS):
                            hb2 = (h % 2) * 64
                            kc = h // 2
                            hkey = tt * HEADS + h
                            if h == 0:
                                p.wait("tensor", "vx",
                                       evq[tt * M_QKV + M_QKV - 1])
                                p.wait("tensor", "ev", vtm_done)
                                p.wait("tensor", "sx", bc1_last)
                            if hkey >= 1:
                                p.wait("tensor", "vx", elu_ps[hkey - 1])
                            ph = projS_h[hb2:hb2 + 64, h * FEAT:(h + 1) * FEAT]
                            pl = projS_l[hb2:hb2 + 64, h * FEAT:(h + 1) * FEAT]
                            for ts in range(4):
                                pb = 2 + ts
                                ksl_h = k_hi[hb2:hb2 + 64,
                                             kc * T + ts * 128:kc * T + (ts + 1) * 128]
                                ksl_l = k_lo[hb2:hb2 + 64,
                                             kc * T + ts * 128:kc * T + (ts + 1) * 128]
                                p.do("tensor", "matmul",
                                     pbank(pb, 0, 128, 0, FEAT),
                                     ksl_h, ph, start=True, stop=False)
                                p.do("tensor", "matmul",
                                     pbank(pb, 0, 128, 0, FEAT),
                                     ksl_l, ph, start=False, stop=False)
                                p.do("tensor", "matmul",
                                     pbank(pb, 0, 128, 0, FEAT),
                                     ksl_h, pl, start=False, stop=(ts == 3),
                                     sem="mm" if ts == 3 else None)
                            kpm = p.mark("mm")
                            if h >= 1:
                                emit_kv(h - 1)
                            psall = psum[0:128, 2:6, 0:FEAT]
                            keh_b, kel_b = ke_h[h % 2], ke_l[h % 2]
                            elm3 = el_m[:].rearrange("p (a b) -> p a b", b=FEAT)
                            elf3 = el_f[:].rearrange("p (a b) -> p a b", b=FEAT)
                            p.wait("scalar", "mm", kpm)
                            if hkey >= 1:
                                p.wait("scalar", "vx", elu_ps[hkey - 1])
                            exm = p.do("scalar", "activation", elf3, psall,
                                       AF.Exp, sem="sx")
                            p.wait("vector", "sx", exm)
                            if hkey >= 2:
                                p.wait("vector", "mm", kvm_m[hkey - 2])
                            p.do("vector", "tensor_scalar_min", el_f[:],
                                 el_f[:], 1.0, sem="vx")
                            p.do("vector", "scalar_tensor_tensor", elm3,
                                 psall, 0.0, elf3, OP.max, OP.add)
                            elu_ps[hkey] = p.mark("vx") + 1
                            p.do("vector", "tensor_copy", keh_b[:], el_m[:],
                                 sem="vx")
                            kel_m[h] = p.do("vector", "tensor_sub", kel_b[:],
                                            el_m[:], keh_b[:], sem="vx")
                            if pend_apply is not None and h >= 8:
                                pend_apply(h - 8)
                                if h == HEADS - 1:
                                    xnr1[tt + 1] = p.mark("vx")
                        emit_kv(HEADS - 1)
                    p.wait("gpsimd", "wr")
                    p.wait("gpsimd", "epi")
                    if taps:
                        p.wait("gpsimd", "epi")
                        p.wait("gpsimd", "vx")
                        p.wait("gpsimd", "sx")
                        p.do("gpsimd", "dma_start", sem="wr", amt=16,
                             out=tap_aps["t_kv"], in_=kv_acc[:])
                        if not taps_full:
                            for nm, tile in (("t_kh", k_hi), ("t_kl", k_lo),
                                             ("t_vh", vtmh), ("t_vl", vtml),
                                             ("t_qh2", qs_h), ("t_ql2", qs_l),
                                             ("t_keh", ke_h[1]), ("t_kel", ke_l[1])):
                                p.do("gpsimd", "dma_start", sem="wr", amt=16,
                                     out=tap_aps[nm], in_=tile[:])
                        p.wait("gpsimd", "wr")

                p.flush_block("attn1")
                if not taps or taps_full:
                    with ExitStack() as st2:
                        def sb2(nm, shape, dt=F32):
                            return st2.enter_context(nc.sbuf_tensor("a2" + nm, shape, dt))

                        Xf = sb2("Xf", [128, DB * T], F32)
                        qs_h = sb2("qsh", [128, DB * T], F16)
                        qs_l = sb2("qsl", [128, DB * T], F16)
                        wosb_h = sb2("woh", [128, DB * DB * 128], F16)
                        wosb_l = sb2("wol", [128, DB * DB * 128], F16)
                        qe_h = [sb2(f"qeh{i}", [128, 4 * T], F16)
                                for i in range(2)]
                        qe_l = [sb2(f"qel{i}", [128, 4 * T], F16)
                                for i in range(2)]
                        el2_m = sb2("elm", [128, 4 * T], F32)
                        el2_f = sb2("elf", [128, 4 * T], F32)
                        at_h = sb2("ath", [128, DB * T], F16)
                        at_l = sb2("atl", [128, DB * T], F16)
                        aof = [sb2(f"aof{i}", [128, T], F32) for i in range(2)]
                        ub2 = sb2("ub2", [128, T], F32)
                        zst2 = sb2("zst2", [64, T], F32)

                        onesP = sb2("onesP", [33, 128], F32)
                        p.do("gpsimd", "memset", onesP[:], 1.0)
                        zmem = p.do("gpsimd", "memset", zst2[:], 1.0, sem="gx")
                        p.do("sync", "dma_start", sem="ld", amt=16, out=wosb_h[:],
                             in_=woh.rearrange("p a b c -> p (a b c)"))
                        wold = p.do("sync", "dma_start", sem="ld", amt=16,
                                    out=wosb_l[:],
                                    in_=wol.rearrange("p a b c -> p (a b c)"))

                        p.wait("tensor", "epi")
                        kv_ev = {}
                        for h in range(HEADS):
                            for fc in range(2):
                                bk = 4 + fc
                                if bk in kv_ev:
                                    p.wait("tensor", "vx", kv_ev[bk])
                                p.do("tensor", "transpose", pbank(bk, 0, 128, 0, 65),
                                     kv_acc[0:65, h * FEAT + fc * 128:
                                            h * FEAT + (fc + 1) * 128],
                                     eyeS[0:65, 0:65], sem="tr")
                                trm = p.mark("tr")
                                p.wait("scalar", "tr", trm)
                                o65 = (h * 2 + fc) * 65
                                hcp = p.do("scalar", "copy",
                                           kv65h[:, o65:o65 + 65],
                                           pbank(bk, 0, 128, 0, 65), sem="sx")
                                p.wait("vector", "sx", hcp)
                                kv_ev[bk] = p.do("vector", "tensor_sub",
                                                 kv65l[:, o65:o65 + 65],
                                                 pbank(bk, 0, 128, 0, 65),
                                                 kv65h[:, o65:o65 + 65], sem="vx")
                        kv65_done = p.mark("vx")
                        p.wait("tensor", "ld", wold)

                        atg = {}
                        stt2 = {}
                        amm_pair = {}
                        zad_m = {}
                        wo_epi = {}
                        zb_rd = {}
                        aof_rd = {}

                        wrm2 = {}
                        last_amm = 0
                        for tt in range(NT if not taps else 1):
                            t0 = tt * T
                            for c in range(DB):
                                if tt > 0:
                                    p.wait("sync", "wr", wrm2[c])
                                    if c == 0:
                                        p.wait("sync", "mm", last_amm)
                                p.do("sync", "dma_start", sem="ld", amt=16,
                                     out=Xf[:, c * T:(c + 1) * T],
                                     in_=Xa[:, c, t0:t0 + T])
                                p.do("sync", "dma_start", sem="ld", amt=16,
                                     out=qs_h[:, c * T:(c + 1) * T],
                                     in_=qhh[:, c, t0:t0 + T])
                                ldc = p.do("sync", "dma_start", sem="ld", amt=16,
                                           out=qs_l[:, c * T:(c + 1) * T],
                                           in_=qhl[:, c, t0:t0 + T])
                            p.wait("tensor", "ld", ldc)
                            p.wait("tensor", "epi")
                            if tt == 0:
                                p.wait("tensor", "vx", kv65_done)
                            def emit_zbc_norm(j):
                                oc = j
                                ob = j % 2
                                p.wait("tensor", "vx", zad_m[j])
                                p.wait("tensor", "sx", atg[2 * j])
                                p.do("tensor", "matmul", pbank(4, 0, 64),
                                     onesr[0:1, 0:64], zst2[0:1, :],
                                     start=True, stop=True, skip_group_check=True)
                                zbm = p.do("tensor", "matmul", pbank(4, 64, 128),
                                           onesP[32:33, 0:64], zst2[32:33, :],
                                           start=True, stop=True, sem="mm",
                                           skip_group_check=True)
                                ocs = slice(oc * T, (oc + 1) * T)
                                p.wait("vector", "mm", zbm)
                                p.do("vector", "tensor_mul", at_h[:, ocs],
                                     aof[ob][:], pbank(4))
                                p.do("vector", "tensor_mul", ub2[:],
                                     aof[ob][:], pbank(4), sem="vx")
                                zb_rd[j] = p.mark("vx")
                                aof_rd[ob] = zb_rd[j]
                                p.do("vector", "tensor_sub", at_l[:, ocs],
                                     ub2[:], at_h[:, ocs], sem="hs")

                            for mpr in range(8):
                                h0, h1 = 2 * mpr, 2 * mpr + 1
                                qc = mpr
                                pkey = tt * 8 + mpr
                                if pkey >= 1:
                                    p.wait("tensor", "vx", stt2[pkey - 1])
                                qpm = 0
                                for hp, hbk in enumerate(((h0, 0), (h1, 2))):
                                    h, bk0 = hbk
                                    hb2 = hp * 64
                                    qsl_h = qs_h[hb2:hb2 + 64, qc * T:(qc + 1) * T]
                                    qsl_l = qs_l[hb2:hb2 + 64, qc * T:(qc + 1) * T]
                                    for fc in range(2):
                                        ph = projS_h[hb2:hb2 + 64,
                                                     h * FEAT + fc * 128:
                                                     h * FEAT + (fc + 1) * 128]
                                        pl = projS_l[hb2:hb2 + 64,
                                                     h * FEAT + fc * 128:
                                                     h * FEAT + (fc + 1) * 128]
                                        pb = bk0 + fc
                                        p.do("tensor", "matmul", pbank(pb), ph,
                                             qsl_h, start=True, stop=False)
                                        p.do("tensor", "matmul", pbank(pb), ph,
                                             qsl_l, start=False, stop=False)
                                        qpm = p.do("tensor", "matmul", pbank(pb),
                                                   pl, qsl_h, start=False, stop=True,
                                                   sem="mm" if (hp == 1 and fc == 1)
                                                   else None)
                                if mpr >= 1:
                                    emit_zbc_norm(mpr - 1)
                                ps4 = psum[0:128, 0:4, 0:512]
                                em3 = el2_m[:].rearrange("p (a b) -> p a b", b=512)
                                ef3 = el2_f[:].rearrange("p (a b) -> p a b", b=512)
                                qeh_b, qel_b = qe_h[mpr % 2], qe_l[mpr % 2]
                                p.wait("scalar", "mm", qpm)
                                if pkey >= 1:
                                    p.wait("scalar", "vx", stt2[pkey - 1])
                                exm = p.do("scalar", "activation", ef3, ps4,
                                           AF.Exp, sem="sx")
                                p.wait("vector", "sx", exm)
                                if pkey >= 2:
                                    p.wait("vector", "mm", amm_pair[pkey - 2])
                                p.do("vector", "tensor_scalar_min", el2_f[:],
                                     el2_f[:], 1.0, sem="vx")
                                p.do("vector", "scalar_tensor_tensor", em3,
                                     ps4, 0.0, ef3, OP.max, OP.add)
                                stt2[pkey] = p.mark("vx") + 1
                                p.do("vector", "tensor_copy", qeh_b[:], el2_m[:],
                                     sem="vx")
                                qeld = p.do("vector", "tensor_sub", qel_b[:],
                                            el2_m[:], qeh_b[:], sem="vx")
                                p.wait("tensor", "vx", qeld)
                                if mpr >= 1:
                                    p.wait("tensor", "vx", zb_rd.get(mpr - 1, 0))
                                    p.wait("tensor", "sx", atg.get(h1 - 2, 0))
                                for hp, hab in enumerate(((h0, 4), (h1, 5))):
                                    h, ab = hab
                                    first = True
                                    for fc in range(2):
                                        o65 = (h * 2 + fc) * 65
                                        qoff = (hp * 2 + fc) * T
                                        for wop, xop in ((kv65h, qeh_b),
                                                         (kv65l, qeh_b),
                                                         (kv65h, qel_b)):
                                            last = (fc == 1 and xop is qel_b)
                                            p.do("tensor", "matmul",
                                                 pbank(ab, 0, 65),
                                                 wop[:, o65:o65 + 65],
                                                 xop[:, qoff:qoff + T],
                                                 start=first, stop=last,
                                                 sem="mm" if last else None,
                                                 skip_group_check=True)
                                            first = False
                                    amm_pair[pkey] = p.mark("mm")
                                    last_amm = amm_pair[pkey]
                                    p.wait("scalar", "mm", amm_pair[pkey])
                                    if pkey == 0:
                                        p.wait("scalar", "gx", zmem)
                                    zrow = 0 if hp == 0 else 32
                                    p.do("scalar", "copy", zst2[zrow:zrow + 1, :],
                                         pbank(ab, 64, 65), sem="sx")
                                    ob = mpr % 2
                                    if hp == 0 and mpr >= 2:
                                        p.wait("scalar", "vx", aof_rd.get(ob, 0))
                                    atg[h] = p.do("scalar", "copy",
                                                  aof[ob][hp * 64:hp * 64 + 64, :],
                                                  pbank(ab, 0, 64), sem="sx")
                                p.wait("vector", "sx", atg[h1])
                                p.do("vector", "tensor_scalar_add",
                                     zst2[0:33, :], zst2[0:33, :], EPS_Z)
                                zad_m[mpr] = p.do("vector", "reciprocal",
                                                  zst2[0:33, :], zst2[0:33, :],
                                                  sem="vx")
                            emit_zbc_norm(7)

                            at_done = p.mark("hs")
                            for co in range(DB):
                                wb = 2 + (co % 2)
                                if co == 0:
                                    p.wait("tensor", "hs", at_done)
                                if co >= 2:
                                    p.wait("tensor", "epi",
                                           wo_epi[tt * DB + co - 2])
                                first = True
                                for wop, xop in ((wosb_h, at_h), (wosb_l, at_h),
                                                 (wosb_h, at_l)):
                                    for ci in range(DB):
                                        last = (wop is wosb_h and xop is at_l
                                                and ci == DB - 1)
                                        p.do("tensor", "matmul", pbank(wb),
                                             wop[:, (co * DB + ci) * 128:
                                                 (co * DB + ci + 1) * 128],
                                             xop[:, ci * T:(ci + 1) * T],
                                             start=first, stop=last,
                                             sem="mm" if last else None)
                                        first = False
                                wmm = p.mark("mm")
                                p.wait("vector", "mm", wmm)
                                cs = slice(co * T, (co + 1) * T)
                                wo_epi[tt * DB + co] = p.do(
                                    "vector", "scalar_tensor_tensor", Xf[:, cs],
                                    pbank(wb), bo[:, co:co + 1], Xf[:, cs],
                                    OP.add, OP.add, sem="epi")
                                p.wait("gpsimd", "epi", wo_epi[tt * DB + co])
                                wrm2[co] = p.do("gpsimd", "dma_start", sem="wr",
                                                amt=16,
                                                out=Xb[:, co, t0:t0 + T],
                                                in_=Xf[:, cs])
                        p.wait("gpsimd", "wr")
                p.flush_block("attn")

        if "ffn1" in phases:
            ffn_phase("xin", ("fm", Xa), "ffn1")
        if "attn" in phases:
            attn_phase()
        if "ffn2" in phases:
            src_ap = Xb if "attn" in phases else Xa
            ffn_phase(("fm", src_ap), "yout", "ffn2")

        if debug_out is not None:
            src_ap = Xa if debug_out == "Xa" else Xb
            with ExitStack() as std:
                dbg = std.enter_context(nc.sbuf_tensor("dbg", [128, NTOK], F32))
                for c in range(DB):
                    ldc = p.do("sync", "dma_start", sem="ld", amt=16,
                               out=dbg[:], in_=src_ap[:, c, :])
                    p.wait("gpsimd", "ld", ldc)
                    wc = p.do("gpsimd", "dma_start", sem="wr", amt=16,
                              out=y[:, c, :], in_=dbg[:])
                    p.wait("sync", "wr", wc)
                p.wait("gpsimd", "wr")
                p.flush_block("dbg")

    return nc


_NC_CACHE = {}


def _get_nc(phases, debug_out, taps=False, taps_full=False):
    key = (tuple(phases), debug_out, taps, taps_full)
    if key not in _NC_CACHE:
        _NC_CACHE[key] = _build_nc(phases, debug_out, taps, taps_full)
    return _NC_CACHE[key]


def _make_in_maps(inputs):
    x = np.ascontiguousarray(np.asarray(inputs["L0_attn_x"], dtype=np.float32))
    eye = np.eye(128, dtype=np.float32)
    ones_in = np.zeros((128, 2), dtype=np.float32)
    ones_in[:, 0] = 1.0
    e2 = np.zeros((2, 128), dtype=np.float32)
    e2[0, :] = 1.0
    zsel = np.zeros((2, 128), dtype=np.float32)
    zsel[0, 0:64] = 1.0
    zsel[1, 64:128] = 1.0
    common = {
        "w_qkv": np.ascontiguousarray(inputs["L0_attn_w_qkv"], dtype=np.float32),
        "proj": np.ascontiguousarray(inputs["L0_attn_proj_matrix"], dtype=np.float32),
        "w_out": np.ascontiguousarray(inputs["L0_attn_w_out"], dtype=np.float32),
        "b_out": np.ascontiguousarray(inputs["L0_attn_b_out"], dtype=np.float32),
        "w1": np.ascontiguousarray(inputs["L0_ffn_w1"], dtype=np.float32),
        "b1": np.ascontiguousarray(inputs["L0_ffn_b1"], dtype=np.float32),
        "w2": np.ascontiguousarray(inputs["L0_ffn_w2"], dtype=np.float32),
        "b2": np.ascontiguousarray(inputs["L0_ffn_b2"], dtype=np.float32),
        "eye32": eye, "eye16": eye, "ones_in": ones_in, "e2_in": e2,
        "zsel_in": zsel,
    }
    return [dict(common, x=np.ascontiguousarray(x[i])) for i in range(8)]


def kernel(**inputs):
    nc = _get_nc(ALL_PHASES, None)
    in_maps = _make_in_maps(inputs)
    res = run_bass_kernel_spmd(nc, in_maps, core_ids=list(range(8)))
    return np.stack([np.asarray(r["y"], dtype=np.float32) for r in res.results])


# version 13 (final)
# speedup vs baseline: 1.0821x; 1.0087x over previous
import numpy as np
from contextlib import ExitStack

import concourse.bass as bass
import concourse.mybir as mybir
from concourse.bass_utils import run_bass_kernel_spmd

F32 = mybir.dt.float32
F16 = mybir.dt.float16
AF = mybir.ActivationFunctionType
OP = mybir.AluOpType
AX = mybir.AxisListType

NTOK = 4096
T = 512
NT = NTOK // T
D = 1024
DB = D // 128
H = 4096
HF = H // 128
HEADS = 16
HD = 64
FEAT = 256
QC = DB
M_QKV = 3 * DB
EPS_LN = 1e-5
EPS_Z = 1e-8

ENGINES = ("sync", "tensor", "vector", "scalar", "gpsimd")
ALL_PHASES = ("ffn1", "attn", "ffn2")


class Prog:

    def __init__(self, nc, stack):
        self.nc = nc
        self.stack = stack
        self.q = {e: [] for e in ENGINES}
        self.sems = {}
        self.cnt = {}

    def sem(self, *names):
        for n in names:
            self.sems[n] = self.stack.enter_context(self.nc.semaphore(n))
            self.cnt[n] = 0

    def do(self, eng, meth, *args, sem=None, amt=1, **kw):
        s = self.sems[sem] if sem is not None else None
        if sem is not None:
            self.cnt[sem] += amt

        def fn(e, meth=meth, args=args, kw=kw, s=s, amt=amt):
            inst = getattr(e, meth)(*args, **kw)
            if s is not None:
                inst.then_inc(s, amt)

        self.q[eng].append(fn)
        return self.cnt.get(sem)

    def wait(self, eng, sem, val=None):
        v = self.cnt[sem] if val is None else val
        if v <= 0:
            return
        s = self.sems[sem]
        self.q[eng].append(lambda e, s=s, v=v: e.wait_ge(s, v))

    def mark(self, sem):
        return self.cnt[sem]

    def flush_block(self, name=None):
        with self.nc.Block(name) as blk:
            for eng in ENGINES:
                fns = self.q[eng]
                if not fns:
                    continue
                getattr(blk, eng)(lambda e, fns=tuple(fns): [f(e) for f in fns])
        self.q = {e: [] for e in ENGINES}


def _build_nc(phases=ALL_PHASES, debug_out=None, taps=False, taps_full=False):
    nc = bass.Bass("TRN2", target_bir_lowering=False)

    x_in = nc.dram_tensor("x", [NTOK, D], F32, kind="ExternalInput").ap()
    w_qkv = nc.dram_tensor("w_qkv", [D, 3 * D], F32, kind="ExternalInput").ap()
    proj = nc.dram_tensor("proj", [HEADS, HD, FEAT], F32, kind="ExternalInput").ap()
    w_out = nc.dram_tensor("w_out", [D, D], F32, kind="ExternalInput").ap()
    b_out = nc.dram_tensor("b_out", [D], F32, kind="ExternalInput").ap()
    w1 = nc.dram_tensor("w1", [H, D], F32, kind="ExternalInput").ap()
    b1 = nc.dram_tensor("b1", [H], F32, kind="ExternalInput").ap()
    w2 = nc.dram_tensor("w2", [D, H], F32, kind="ExternalInput").ap()
    b2 = nc.dram_tensor("b2", [D], F32, kind="ExternalInput").ap()
    eye32 = nc.dram_tensor("eye32", [128, 128], F32, kind="ExternalInput").ap()
    eye16 = nc.dram_tensor("eye16", [128, 128], F32, kind="ExternalInput").ap()
    ones_in = nc.dram_tensor("ones_in", [128, 2], F32, kind="ExternalInput").ap()
    e2_in = nc.dram_tensor("e2_in", [2, 128], F32, kind="ExternalInput").ap()
    zsel_in = nc.dram_tensor("zsel_in", [2, 128], F32, kind="ExternalInput").ap()

    if debug_out is None:
        y = nc.dram_tensor("y", [NTOK, D], F32, kind="ExternalOutput").ap()
    else:
        y = nc.dram_tensor("y", [128, DB, NTOK], F32, kind="ExternalOutput").ap()
    tap_aps = {}
    if taps:
        for nm, shape in (("t_Xf", [128, DB * T]), ("t_xnh", [128, DB * T]),
                          ("t_xnl", [128, DB * T]), ("t_rows", [1, 4 * T]),
                          ("t_a1", [128, HF]), ("t_b1t", [128, HF]),
                          ("t_sc2", [128, DB]), ("t_bb2", [128, DB]),
                          ("t_s1", [128, 1024]), ("t_s2", [128, 1024])):
            tap_aps[nm] = nc.dram_tensor(nm, shape, F32, kind="ExternalOutput").ap()
        for nm in ("t_hh", "t_hl"):
            tap_aps[nm] = nc.dram_tensor(nm, [128, HF * T], F16,
                                         kind="ExternalOutput").ap()
        tap_aps["t_kv"] = nc.dram_tensor("t_kv", [128, HEADS * FEAT], F32,
                                         kind="ExternalOutput").ap()
        for nm, wd in (("t_kv65h", HEADS * 2 * 65), ("t_kv65l", HEADS * 2 * 65),
                       ("t_ath", DB * T), ("t_atl", DB * T),
                       ("t_qsh", DB * T), ("t_qsl", DB * T)):
            tap_aps[nm] = nc.dram_tensor(nm, [128, wd], F16,
                                         kind="ExternalOutput").ap()
        for nm, wd in (("t_kh", DB * T), ("t_kl", DB * T), ("t_vh", 4 * 1040),
                       ("t_vl", 4 * 1040), ("t_qh2", DB * T), ("t_ql2", DB * T),
                       ("t_keh", 4 * FEAT), ("t_kel", 4 * FEAT)):
            tap_aps[nm] = nc.dram_tensor(nm, [128, wd], F16,
                                         kind="ExternalOutput").ap()

    Xa = nc.dram_tensor("Xa", [128, DB, NTOK], F32).ap()
    Xb = nc.dram_tensor("Xb", [128, DB, NTOK], F32).ap()
    s1T = nc.dram_tensor("s1T", [128, HF, DB, 128], F16).ap()
    s2T = nc.dram_tensor("s2T", [128, DB, HF, 128], F16).ap()
    wqh = nc.dram_tensor("wqh", [128, M_QKV, DB, 128], F16).ap()
    wql = nc.dram_tensor("wql", [128, M_QKV, DB, 128], F16).ap()
    woh = nc.dram_tensor("woh", [128, DB, DB, 128], F16).ap()
    wol = nc.dram_tensor("wol", [128, DB, DB, 128], F16).ap()
    projhl = nc.dram_tensor("projhl", [128, 2, HEADS * FEAT], F16).ap()
    qhh = nc.dram_tensor("qhh", [128, QC, NTOK], F16).ap()
    qhl = nc.dram_tensor("qhl", [128, QC, NTOK], F16).ap()

    with ExitStack() as st:
        def sb(name, shape, dt=F32):
            return st.enter_context(nc.sbuf_tensor(name, shape, dt))

        p = Prog(nc, st)
        p.sem("ld", "wr", "sg", "tr", "ev", "vx", "sx", "gx", "mm",
              "row", "hs", "s2l", "gel", "epi",
              *[f"ldx{c}" for c in range(8)],
              *[f"wrx{c}" for c in range(8)],
              *[f"qwr{c}" for c in range(8)],
              "s2a", "s2b", "wqa", "wqb", "s1a", "s1b", "wpl", "wpv", "wpw")

        eyeS = sb("eyeS", [128, 128], F32)
        eyeH = sb("eyeH", [128, 128], F16)
        ones1 = sb("ones1", [128, 2], F32)
        onesr = sb("onesr", [2, 128], F32)
        a1 = sb("a1", [128, HF], F32)
        b1t = sb("b1t", [128, HF], F32)
        sc2 = sb("sc2", [128, DB], F32)
        bb2 = sb("bb2", [128, DB], F32)
        bo = sb("bo", [128, DB], F32)
        zselS = sb("zselS", [2, 128], F32)
        ones16S = sb("ones16S", [128, 1], F16)
        rows = sb("rows", [1, T * 4], F32)
        psum = st.enter_context(nc.psum_tensor("psum", [128, 6, 512], F32))
        psumh = st.enter_context(nc.psum_tensor("psumh", [128, 2, 1024], F16))

        def pbank(i, p0=0, p1=128, f0=0, f1=512):
            return psum[p0:p1, i, f0:f1]

        def phbank(i, p0=0, p1=128, f0=0, f1=128):
            return psumh[p0:p1, i, f0:f1]

        p.do("sync", "dma_start", sem="ld", amt=16, out=eyeS[:], in_=eye32)
        p.do("sync", "dma_start", sem="ld", amt=16, out=ones1[:], in_=ones_in)
        p.do("sync", "dma_start", sem="ld", amt=16, out=onesr[:], in_=e2_in)
        p.do("gpsimd", "dma_start", sem="wr", amt=16, out=eyeH[:], in_=eye16)
        p.do("sync", "dma_start", sem="ld", amt=16, out=zselS[:], in_=zsel_in)
        p.do("gpsimd", "dma_start", sem="wr", amt=16, out=ones16S[:],
             in_=ones_in[:, 0:1], allow_slow_non_contiguous=True)

        with ExitStack() as st0:
            def sb0(name, shape, dt=F32):
                return st0.enter_context(nc.sbuf_tensor(name, shape, dt))

            wld = [sb0(f"wld{i}", [128, H], F32) for i in range(2)]
            sgn = [sb0(f"sgn{i}", [128, H], F16) for i in range(2)]
            stg = sb0("stg", [128, HF * 128], F16)
            hi_t = sb0("hi_t", [128, 4 * D], F16)
            lo_t = sb0("lo_t", [128, 4 * D], F16)
            araw = sb0("araw", [128, HF], F32)
            brow = sb0("brow", [HF, 128], F32)

            p.wait("vector", "ld")
            p.wait("scalar", "ld")
            p.wait("tensor", "ld")
            p.wait("tensor", "wr")

            def prep_sign(n_jb, n_db, w_ap, scr_ap, a_tile):
                cons = {}
                bank_ev = {}
                wr_marks = {}
                for jb in range(n_jb):
                    buf = jb % 2
                    if jb >= 2:
                        vxm, sgm, trm = cons[jb - 2]
                        p.wait("sync", "vx", vxm)
                        p.wait("sync", "sg", sgm)
                    else:
                        p.wait("sync", "vx")
                        p.wait("sync", "sg")
                    ldc = p.do("sync", "dma_start", sem="ld", amt=16,
                               out=wld[buf][:, 0:n_db * 128],
                               in_=w_ap[jb * 128:(jb + 1) * 128, :])
                    p.wait("vector", "ld", ldc)
                    vxm = p.do("vector", "reduce_sum", a_tile[:, jb:jb + 1],
                               wld[buf][:, 0:n_db * 128], axis=AX.X,
                               apply_absolute_value=True, sem="vx")
                    p.wait("scalar", "ld", ldc)
                    if jb >= 2:
                        p.wait("scalar", "tr", cons[jb - 2][2])
                    if jb >= 1:
                        p.wait("scalar", "wr", wr_marks[jb - 1])
                    sgm = p.do("scalar", "sign", sgn[buf][:, 0:n_db * 128],
                               wld[buf][:, 0:n_db * 128], sem="sg")
                    p.wait("tensor", "sg", sgm)
                    for db in range(n_db):
                        bk = db % 2
                        if bk in bank_ev:
                            p.wait("tensor", "ev", bank_ev[bk])
                        p.do("tensor", "transpose", phbank(bk),
                             sgn[buf][:, db * 128:(db + 1) * 128], eyeH[:],
                             sem="tr")
                        trm = p.mark("tr")
                        p.wait("scalar", "tr", trm)
                        bank_ev[bk] = p.do("scalar", "copy",
                                           stg[:, db * 128:(db + 1) * 128],
                                           phbank(bk), sem="ev")
                    p.wait("gpsimd", "ev", p.mark("ev"))
                    wr_marks[jb] = p.do("gpsimd", "dma_start", sem="wr", amt=16,
                                        out=scr_ap[:, jb, :, :].rearrange("p a b -> p (a b)"),
                                        in_=stg[:, 0:n_db * 128])
                    cons[jb] = (vxm, sgm, trm)

            prep_sign(HF, DB, w1, s1T, araw)
            p.wait("vector", "vx")
            p.do("vector", "tensor_scalar_mul", a1[:], araw[:, 0:HF], 1.0 / D)
            prep_sign(DB, HF, w2, s2T, sc2)
            p.wait("vector", "vx")
            p.do("vector", "tensor_scalar_mul", sc2[:], sc2[:, 0:DB], 0.5 / H,
                 sem="vx")

            def prep_bias(b_ap, n, out_tile, scale):
                p.wait("sync", "tr")
                ldc = p.do("sync", "dma_start", sem="ld", amt=16,
                           out=brow[0:n, :],
                           in_=b_ap.rearrange("(a b) -> a b", b=128))
                p.wait("tensor", "ld", ldc)
                p.wait("tensor", "ev")
                p.do("tensor", "transpose", pbank(4, 0, 128, 0, n),
                     brow[0:n, :], eyeS[0:n, 0:n], sem="tr")
                p.wait("vector", "tr", p.mark("tr"))
                p.do("vector", "tensor_scalar_mul", out_tile[:, 0:n],
                     pbank(4, 0, 128, 0, n), scale, sem="vx")

            prep_bias(b1, HF, b1t, 1.0)
            prep_bias(b2, DB, bb2, 0.5)
            prep_bias(b_out, DB, bo, 1.0)

            def prep_split(w_ap, n_db, n_out, hi_scr, lo_scr):
                for db in range(n_db):
                    buf = db % 2
                    p.wait("sync", "vx")
                    p.wait("sync", "sg")
                    ldc = p.do("sync", "dma_start", sem="ld", amt=16,
                               out=wld[buf][:, 0:n_out],
                               in_=w_ap[db * 128:(db + 1) * 128, :])
                    p.wait("vector", "ld", ldc)
                    p.wait("vector", "wr")
                    p.do("vector", "tensor_copy", hi_t[:, 0:n_out],
                         wld[buf][:, 0:n_out])
                    vc = p.do("vector", "tensor_sub", lo_t[:, 0:n_out],
                              wld[buf][:, 0:n_out], hi_t[:, 0:n_out], sem="vx")
                    p.wait("gpsimd", "vx", vc)
                    p.do("gpsimd", "dma_start", sem="wr", amt=16,
                         out=hi_scr[:, :, db, :],
                         in_=hi_t[:, 0:n_out].rearrange("p (a b) -> p a b", b=128))
                    p.do("gpsimd", "dma_start", sem="wr", amt=16,
                         out=lo_scr[:, :, db, :],
                         in_=lo_t[:, 0:n_out].rearrange("p (a b) -> p a b", b=128))

            p.wait("gpsimd", "wr")
            p.wait("sync", "wr")
            p.wait("sync", "ld")
            p.flush_block("prep")

        def ffn_phase(src, dst, name):
            with ExitStack() as stf:
                def sbf(nm, shape, dt=F32):
                    return stf.enter_context(nc.sbuf_tensor(name + nm, shape, dt))

                s1c = [sbf(f"s1c{i}", [128, DB * 128], F16) for i in range(2)]
                Xf2 = [sbf(f"Xf{i}", [128, DB * T], F32) for i in range(2)]
                xtm = sbf("xtm", [128, D], F32)
                xnh2 = [sbf(f"xnh{i}", [128, DB * T], F16) for i in range(2)]
                xnl2 = [sbf(f"xnl{i}", [128, DB * T], F16) for i in range(2)]
                hh = sbf("hh", [128, HF * T], F16)
                hl = sbf("hl", [128, HF * T], F16)
                gbuf = [sbf(f"g{i}", [128, T], F32) for i in range(2)]
                sqd = [sbf(f"sq{i}", [128, T], F32) for i in range(2)]
                tbuf = sbf("tb", [128, T], F32)
                ubuf = sbf("ub", [128, T], F32)
                mS = sbf("mS", [128, T], F32)
                rS = sbf("rS", [128, T], F32)
                s2c = [sbf(f"s2c{i}", [128, HF * 128], F16) for i in range(2)]
                tmpb = sbf("tmpb", [128, T], F32)
                ytm = xtm
                if name == "ffn1":
                    wp_ld = sbf("wpld", [128, 3 * D], F32)
                    wp_hi = sbf("wphi", [128, 3 * D], F16)
                    wp_lo = sbf("wplo", [128, 3 * D], F16)
                    wp_items = ([("wq", i) for i in range(DB)]
                                + [("wo", i) for i in range(DB)]
                                + [("pj", 0), ("pj", 1)])

                def emit_wp_item(kind, db):
                    p.wait("sync", "wpv")
                    if kind == "wq":
                        n_out = 3 * D
                        lw = p.do("sync", "dma_start", sem="wpl", amt=16,
                                  out=wp_ld[:, 0:n_out],
                                  in_=w_qkv[db * 128:(db + 1) * 128, :])
                    elif kind == "wo":
                        n_out = D
                        lw = p.do("sync", "dma_start", sem="wpl", amt=16,
                                  out=wp_ld[:, 0:n_out],
                                  in_=w_out[db * 128:(db + 1) * 128, :])
                    else:
                        n_out = HEADS * FEAT // 2
                        hs8 = slice(db * 8, db * 8 + 8)
                        p.do("sync", "dma_start", sem="wpl", amt=16,
                             out=wp_ld[0:64, 0:n_out].rearrange(
                                 "d (h f) -> d h f", f=FEAT),
                             in_=proj[hs8].rearrange("h d f -> d h f"))
                        lw = p.do("sync", "dma_start", sem="wpl", amt=16,
                                  out=wp_ld[64:128, 0:n_out].rearrange(
                                      "d (h f) -> d h f", f=FEAT),
                                  in_=proj[hs8].rearrange("h d f -> d h f"))
                    p.wait("vector", "wpl", lw)
                    p.wait("vector", "wpw")
                    p.do("vector", "tensor_copy", wp_hi[:, 0:n_out],
                         wp_ld[:, 0:n_out])
                    vc = p.do("vector", "tensor_sub", wp_lo[:, 0:n_out],
                              wp_ld[:, 0:n_out], wp_hi[:, 0:n_out], sem="wpv")
                    p.wait("gpsimd", "wpv", vc)
                    if kind == "wq":
                        p.do("gpsimd", "dma_start", sem="wpw", amt=16,
                             out=wqh[:, :, db, :],
                             in_=wp_hi[:, 0:n_out].rearrange(
                                 "p (a b) -> p a b", b=128))
                        p.do("gpsimd", "dma_start", sem="wpw", amt=16,
                             out=wql[:, :, db, :],
                             in_=wp_lo[:, 0:n_out].rearrange(
                                 "p (a b) -> p a b", b=128))
                    elif kind == "wo":
                        p.do("gpsimd", "dma_start", sem="wpw", amt=16,
                             out=woh[:, :, db, :],
                             in_=wp_hi[:, 0:n_out].rearrange(
                                 "p (a b) -> p a b", b=128))
                        p.do("gpsimd", "dma_start", sem="wpw", amt=16,
                             out=wol[:, :, db, :],
                             in_=wp_lo[:, 0:n_out].rearrange(
                                 "p (a b) -> p a b", b=128))
                    else:
                        o8 = db * (HEADS * FEAT // 2)
                        p.do("gpsimd", "dma_start", sem="wpw", amt=16,
                             out=projhl[:, 0, o8:o8 + n_out],
                             in_=wp_hi[:, 0:n_out])
                        p.do("gpsimd", "dma_start", sem="wpw", amt=16,
                             out=projhl[:, 1, o8:o8 + n_out],
                             in_=wp_lo[:, 0:n_out])

                sumsq_mm = {}
                mmA_m = {}
                ec_last = 0
                bc_last = 0
                wrm = {}
                ytr = {0: 0, 1: 0}
                yev = {0: 0, 1: 0}
                gel_marks = {}
                hs_marks = {}
                epi_marks = {}
                mmB_marks = {}
                bank_ev = {}
                nt_eff = NT if not (taps and not taps_full) else 1

                def emit_loadln(tt):
                    nonlocal bc_last
                    t0 = tt * T
                    Xf = Xf2[tt % 2]
                    xnh, xnl = xnh2[tt % 2], xnl2[tt % 2]
                    if src == "xin":
                        if tt > 1:
                            p.wait("scalar", "epi")
                            p.wait("scalar", "wr")
                        p.wait("tensor", "sx", bc_last)
                        for ts in range(4):
                            if ts > 0:
                                p.wait("sync", "tr")
                            ldc = p.do("sync", "dma_start", sem="ld", amt=16,
                                       out=xtm[:],
                                       in_=x_in[t0 + ts * 128:t0 + (ts + 1) * 128, :])
                            p.wait("tensor", "ld", ldc)
                            for c in range(DB):
                                bk = c % 2
                                if bk in bank_ev:
                                    p.wait("tensor", "ev", bank_ev[bk])
                                p.do("tensor", "transpose",
                                     pbank(bk, 0, 128, 0, 128),
                                     xtm[:, c * 128:(c + 1) * 128], eyeS[:],
                                     sem="tr")
                                p.wait("scalar", "tr", p.mark("tr"))
                                bank_ev[bk] = p.do(
                                    "scalar", "copy",
                                    Xf[:, c * T + ts * 128:c * T + (ts + 1) * 128],
                                    pbank(bk, 0, 128, 0, 128), sem="ev")
                        xf_sem, xf_ready = "ev", p.mark("ev")
                        ldsems = {c: "ev" for c in range(DB)}
                        ldcs = {c: xf_ready for c in range(DB)}
                    else:
                        _, src_ap = src
                        ldcs = {}
                        ldsems = {}
                        for c in range(DB):
                            if tt > 1:
                                if dst != "yout":
                                    p.wait("sync", f"wrx{c}", wrm[(tt % 2, c)])
                                else:
                                    p.wait("sync", "tr", ytr[tt % 2])
                            ldsems[c] = f"ldx{c}"
                            ldcs[c] = p.do("sync", "dma_start", sem=ldsems[c],
                                           amt=16,
                                           out=Xf[:, c * T:(c + 1) * T],
                                           in_=src_ap[:, c, t0:t0 + T])
                        xf_sem, xf_ready = ldsems[DB - 1], ldcs[DB - 1]

                    for c in range(DB):
                        p.wait("scalar", ldsems[c], ldcs[c])
                        if c >= 2:
                            p.wait("scalar", "mm", sumsq_mm[c - 2])
                        elif tt > 0:
                            p.wait("scalar", "mm", sumsq_mm[DB - 2 + c])
                        sqc = p.do("scalar", "activation", sqd[c % 2][:],
                                   Xf[:, c * T:(c + 1) * T], AF.Square, sem="sx")
                        if c == 0 and src != "xin":
                            p.wait("tensor", "sx", bc_last)
                            if dst == "yout" and tt > 1:
                                p.wait("tensor", "ev", yev[tt % 2])
                        p.wait("tensor", ldsems[c], ldcs[c])
                        p.do("tensor", "matmul", pbank(0, 0, 1), ones1[:, 0:1],
                             Xf[:, c * T:(c + 1) * T],
                             start=(c == 0), stop=(c == DB - 1))
                        p.wait("tensor", "sx", sqc)
                        sumsq_mm[c] = p.do("tensor", "matmul", pbank(1, 0, 1),
                                           ones1[:, 0:1], sqd[c % 2][:],
                                           start=(c == 0), stop=(c == DB - 1),
                                           sem="mm")
                    stats_mm = p.mark("mm")
                    R = rows
                    p.wait("vector", "mm", stats_mm)
                    p.do("vector", "tensor_scalar_mul", R[:, 0:T], pbank(0, 0, 1),
                         1.0 / D)
                    p.do("vector", "tensor_scalar_mul", R[:, T:2 * T],
                         pbank(1, 0, 1), 1.0 / D, sem="row")
                    p.wait("scalar", "row", p.mark("row"))
                    p.do("scalar", "activation", R[:, 2 * T:3 * T], R[:, 0:T],
                         AF.Square, sem="row")
                    p.wait("vector", "row", p.mark("row"))
                    p.do("vector", "tensor_sub", R[:, 3 * T:4 * T], R[:, T:2 * T],
                         R[:, 2 * T:3 * T])
                    p.do("vector", "tensor_scalar_add", R[:, 3 * T:4 * T],
                         R[:, 3 * T:4 * T], EPS_LN, sem="row")
                    p.wait("scalar", "row", p.mark("row"))
                    p.do("scalar", "activation", R[:, 2 * T:3 * T],
                         R[:, 3 * T:4 * T], AF.Sqrt, sem="row")
                    p.wait("vector", "row", p.mark("row"))
                    rowc = p.do("vector", "reciprocal", R[:, T:2 * T],
                                R[:, 2 * T:3 * T], sem="row")
                    p.wait("tensor", "row", rowc)
                    p.do("tensor", "matmul", pbank(0), onesr[0:1, :], R[:, 0:T],
                         start=True, stop=True)
                    bc_mm = p.do("tensor", "matmul", pbank(1), onesr[0:1, :],
                                 R[:, T:2 * T], start=True, stop=True,
                                 sem="mm")
                    p.wait("scalar", "mm", bc_mm)
                    p.do("scalar", "copy", mS[:], pbank(0))
                    bc_cp = p.do("scalar", "copy", rS[:], pbank(1), sem="sx")
                    bc_last = bc_cp
                    for c in range(DB):
                        cs = slice(c * T, (c + 1) * T)
                        if c == 0:
                            p.wait("vector", "sx", bc_cp)
                        p.do("vector", "tensor_sub", tbuf[:], Xf[:, cs], mS[:])
                        p.do("vector", "tensor_mul", ubuf[:], tbuf[:], rS[:])
                        p.do("vector", "tensor_copy", xnh[:, cs], ubuf[:])
                        p.do("vector", "tensor_sub", xnl[:, cs], ubuf[:],
                             xnh[:, cs], sem="vx")
                    xn_ready = p.mark("vx")
                    if taps and tt == 0 and name == "ffn1":
                        p.wait("gpsimd", xf_sem, xf_ready)
                        p.do("gpsimd", "dma_start", sem="wr", amt=16,
                             out=tap_aps["t_Xf"], in_=Xf[:])
                        p.wait("gpsimd", "vx", xn_ready)
                        p.do("gpsimd", "dma_start", sem="wr", amt=16,
                             out=tap_aps["t_xnh"], in_=xnh[:])
                        p.do("gpsimd", "dma_start", sem="wr", amt=16,
                             out=tap_aps["t_xnl"], in_=xnl[:])
                        p.wait("gpsimd", "row", rowc)
                        p.do("gpsimd", "dma_start", sem="wr", amt=16,
                             out=tap_aps["t_rows"], in_=rows[:])
                        p.do("gpsimd", "dma_start", sem="wr", amt=16,
                             out=tap_aps["t_a1"], in_=a1[:])
                        p.do("gpsimd", "dma_start", sem="wr", amt=16,
                             out=tap_aps["t_b1t"], in_=b1t[:])
                        p.do("gpsimd", "dma_start", sem="wr", amt=16,
                             out=tap_aps["t_sc2"], in_=sc2[:])
                        p.do("gpsimd", "dma_start", sem="wr", amt=16,
                             out=tap_aps["t_bb2"], in_=bb2[:])
                        p.do("gpsimd", "dma_start", sem="wr", amt=16,
                             out=tap_aps["t_s1"], in_=s1c[0][:, 0:1024])
                    return xn_ready

                xn_ready_by = {0: emit_loadln(0)}

                for tt in range(nt_eff):
                    t0 = tt * T
                    Xf = Xf2[tt % 2]
                    xnh, xnl = xnh2[tt % 2], xnl2[tt % 2]
                    xn_ready = xn_ready_by[tt]

                    p.wait("tensor", "vx", xn_ready)
                    for hf in range(HF):
                        ga = tt * HF + hf
                        s1sem = "s1a" if hf % 2 == 0 else "s1b"
                        if ga >= 2:
                            p.wait("sync", "mm", mmA_m[ga - 2])
                        s1l = p.do("sync", "dma_start", sem=s1sem, amt=16,
                                   out=s1c[hf % 2][:],
                                   in_=s1T[:, hf, :, :].rearrange(
                                       "p a b -> p (a b)"))
                        bk = 4 + (hf % 2)
                        if hf >= 2:
                            p.wait("tensor", "gel", gel_marks[hf - 2])
                        p.wait("tensor", s1sem, s1l)
                        for pi, xn in enumerate((xnh, xnl)):
                            for db in range(DB):
                                last = (pi == 1 and db == DB - 1)
                                p.do("tensor", "matmul", pbank(bk),
                                     s1c[hf % 2][:, db * 128:(db + 1) * 128],
                                     xn[:, db * T:(db + 1) * T],
                                     start=(pi == 0 and db == 0), stop=last,
                                     sem="mm" if last else None)
                        mmA_m[ga] = mmA = p.mark("mm")
                        p.wait("scalar", "mm", mmA)
                        if hf >= 2:
                            p.wait("scalar", "hs", hs_marks[hf - 2])
                        gel_marks[hf] = p.do("scalar", "activation",
                                             gbuf[hf % 2][:], pbank(bk), AF.Gelu,
                                             bias=b1t[:, hf:hf + 1],
                                             scale=a1[:, hf:hf + 1], sem="gel")
                        p.wait("vector", "gel", gel_marks[hf])
                        hsl = slice(hf * T, (hf + 1) * T)
                        p.do("vector", "tensor_copy", hh[:, hsl], gbuf[hf % 2][:])
                        hs_marks[hf] = p.do("vector", "tensor_sub", hl[:, hsl],
                                            gbuf[hf % 2][:], hh[:, hsl], sem="hs")
                    h_ready = p.mark("hs")

                    if tt + 1 < nt_eff:
                        xn_ready_by[tt + 1] = emit_loadln(tt + 1)

                    for c in range(DB):
                        key = c % 2
                        if (tt * DB + c) >= 2:
                            prev = tt * DB + c - 2
                            p.wait("sync", "mm", mmB_marks[prev])
                        s2sem = "s2a" if key == 0 else "s2b"
                        lds2 = p.do("sync", "dma_start", sem=s2sem, amt=16,
                                    out=s2c[key][:],
                                    in_=s2T[:, c, :, :].rearrange("p a b -> p (a b)"))
                        bk = 2 + key
                        p.wait("tensor", s2sem, lds2)
                        if c == 0:
                            p.wait("tensor", "hs", h_ready)
                            if tt > 0:
                                p.wait("tensor", "sx", ec_last)
                        if (tt * DB + c) >= 2:
                            p.wait("tensor", "epi", epi_marks[tt * DB + c - 2])
                        for pi, hx in enumerate((hh, hl)):
                            for jb in range(HF):
                                last = (pi == 1 and jb == HF - 1)
                                p.do("tensor", "matmul", pbank(bk),
                                     s2c[key][:, jb * 128:(jb + 1) * 128],
                                     hx[:, jb * T:(jb + 1) * T],
                                     start=(pi == 0 and jb == 0), stop=last,
                                     sem="mm" if last else None)
                        mmB_marks[tt * DB + c] = p.mark("mm")
                        p.wait("scalar", "mm", mmB_marks[tt * DB + c])
                        if (tt * DB + c) >= 1:
                            p.wait("scalar", "epi", epi_marks[tt * DB + c - 1])
                        ec = p.do("scalar", "activation", tmpb[:], pbank(bk),
                                  AF.Identity, bias=bb2[:, c:c + 1],
                                  scale=sc2[:, c:c + 1], sem="sx")
                        ec_last = ec
                        p.wait("vector", "sx", ec)
                        cs = slice(c * T, (c + 1) * T)
                        epi_marks[tt * DB + c] = p.do("vector", "tensor_add",
                                                      Xf[:, cs], Xf[:, cs],
                                                      tmpb[:], sem="epi")
                        if dst != "yout":
                            _, dst_ap = dst
                            p.wait("gpsimd", "epi", epi_marks[tt * DB + c])
                            wrm[(tt % 2, c)] = p.do("gpsimd", "dma_start",
                                                    sem=f"wrx{c}", amt=16,
                                                    out=dst_ap[:, c, t0:t0 + T],
                                                    in_=Xf[:, cs])

                    if name == "ffn1":
                        for _ in range(3):
                            if wp_items:
                                emit_wp_item(*wp_items.pop(0))
                    if taps and tt == 0 and name == "ffn1":
                        p.wait("gpsimd", "s2a")
                        p.wait("gpsimd", "s2b")
                        p.do("gpsimd", "dma_start", sem="wr", amt=16,
                             out=tap_aps["t_s2"], in_=s2c[1][:, 0:1024])
                    if dst == "yout":
                        x_done = p.mark("epi")
                        p.wait("tensor", "epi", x_done)
                        p.wait("tensor", "sx", bc_last)
                        for ts in range(4):
                            if ts > 0:
                                p.wait("scalar", "wr")
                            for c in range(DB):
                                bk = c % 2
                                if bk in bank_ev:
                                    p.wait("tensor", "ev", bank_ev[bk])
                                p.do("tensor", "transpose",
                                     pbank(bk, 0, 128, 0, 128),
                                     Xf[:, c * T + ts * 128:c * T + (ts + 1) * 128],
                                     eyeS[:], sem="tr")
                                p.wait("scalar", "tr", p.mark("tr"))
                                bank_ev[bk] = p.do("scalar", "copy",
                                                   ytm[:, c * 128:(c + 1) * 128],
                                                   pbank(bk, 0, 128, 0, 128),
                                                   sem="ev")
                            p.wait("gpsimd", "ev", p.mark("ev"))
                            p.do("gpsimd", "dma_start", sem="wr", amt=16,
                                 out=y[t0 + ts * 128:t0 + (ts + 1) * 128, :],
                                 in_=ytm[:])
                        ytr[tt % 2] = p.mark("tr")
                        yev[tt % 2] = p.mark("ev")

                p.wait("gpsimd", "wr")
                p.wait("gpsimd", "s2a")
                p.wait("gpsimd", "s2b")
                p.wait("gpsimd", "ld")
                p.wait("gpsimd", "wpl")
                p.wait("gpsimd", "wpw")
                p.flush_block(name)

        def attn_phase():
            with ExitStack() as sta:
                def sba(nm, shape, dt=F32):
                    return sta.enter_context(nc.sbuf_tensor("at" + nm, shape, dt))

                kv_acc = sba("kvacc", [128, HEADS * FEAT], F32)
                kv65h = sba("kv65h", [128, HEADS * 2 * 65], F16)
                kv65l = sba("kv65l", [128, HEADS * 2 * 65], F16)
                projS_h = sba("pjh", [128, HEADS * FEAT], F16)
                projS_l = sba("pjl", [128, HEADS * FEAT], F16)
                p.do("sync", "dma_start", sem="ld", amt=16, out=projS_h[:],
                     in_=projhl[:, 0, :])
                pjld = p.do("sync", "dma_start", sem="ld", amt=16,
                            out=projS_l[:], in_=projhl[:, 1, :])
                p.wait("tensor", "ld", pjld)

                with ExitStack() as st1:
                    def sb1(nm, shape, dt=F32):
                        return st1.enter_context(nc.sbuf_tensor("a1" + nm, shape, dt))

                    Xf = sb1("Xf", [128, DB * T], F32)
                    xnh2 = [sb1(f"xnh{i}", [128, DB * T], F16) for i in range(2)]
                    xnl2 = [sb1(f"xnl{i}", [128, DB * T], F16) for i in range(2)]
                    k_hi = sb1("kh", [128, DB * T], F16)
                    k_lo = sb1("kl", [128, DB * T], F16)
                    qs_h = sb1("qsh", [128, DB * T], F16)
                    qs_l = sb1("qsl", [128, DB * T], F16)
                    vst = [sb1(f"vst{i}", [128, T], F16) for i in range(2)]
                    vsl = [sb1(f"vsl{i}", [128, T], F16) for i in range(2)]
                    vtmh = sb1("vtmh", [128, 4 * 1040], F16)
                    vtml = sb1("vtml", [128, 4 * 1040], F16)
                    wqs_h = [sb1(f"wqh{i}", [128, DB * 128], F16) for i in range(2)]
                    wqs_l = [sb1(f"wql{i}", [128, DB * 128], F16) for i in range(2)]
                    ke_h = [sb1(f"keh{i}", [128, 4 * FEAT], F16) for i in range(2)]
                    ke_l = [sb1(f"kel{i}", [128, 4 * FEAT], F16) for i in range(2)]
                    el_m = sb1("elm", [128, 4 * FEAT], F32)
                    el_f = sb1("elf", [128, 4 * FEAT], F32)
                    sqb1 = [sb1(f"sq{i}", [128, T], F32) for i in range(2)]
                    tbuf = [sb1(f"tb{i}", [128, T], F32) for i in range(2)]
                    ubuf = [sb1(f"ub{i}", [128, T], F32) for i in range(2)]
                    mS = sb1("mS", [128, T], F32)
                    rS = sb1("rS", [128, T], F32)

                    zer = p.do("gpsimd", "memset", kv_acc[:], 0.0)
                    p.do("gpsimd", "memset", vtmh[:], 1.0)
                    zer = p.do("gpsimd", "memset", vtml[:], 0.0, sem="gx")

                    sumsq_mm = {}
                    kel_m = {}
                    kvm_m = {}
                    mmQ = {}
                    evq = {}
                    kva = {}
                    elu_ps = {}
                    vtr_ev = {}
                    q_wr = {}

                    lnlo = {}
                    hic_last = 0
                    bc1_last = 0
                    nt1 = NT if not (taps and not taps_full) else 1

                    def p1_loadln(tt):
                        nonlocal bc1_last
                        t0 = tt * T
                        xnh, xnl = xnh2[tt % 2], xnl2[tt % 2]
                        ldcs1 = {}
                        for c in range(DB):
                            if tt > 0:
                                p.wait("sync", "vx", lnlo[c])
                            ldcs1[c] = p.do("sync", "dma_start",
                                            sem=f"ldx{c}", amt=16,
                                            out=Xf[:, c * T:(c + 1) * T],
                                            in_=Xa[:, c, t0:t0 + T])
                        for c in range(DB):
                            p.wait("gpsimd", f"ldx{c}", ldcs1[c])
                            if c >= 2:
                                p.wait("gpsimd", "mm", sumsq_mm[c - 2])
                            elif tt > 0:
                                p.wait("gpsimd", "mm", sumsq_mm[DB - 2 + c])
                            sqc = p.do("gpsimd", "tensor_mul", sqb1[c % 2][:],
                                       Xf[:, c * T:(c + 1) * T],
                                       Xf[:, c * T:(c + 1) * T], sem="gx")
                            if c == 0 and tt > 0:
                                p.wait("tensor", "sx", hic_last)
                                p.wait("tensor", "vx",
                                       evq[tt * M_QKV - 1])
                            p.wait("tensor", f"ldx{c}", ldcs1[c])
                            p.do("tensor", "matmul", pbank(2, 0, 1),
                                 ones1[:, 0:1], Xf[:, c * T:(c + 1) * T],
                                 start=(c == 0), stop=(c == DB - 1))
                            p.wait("tensor", "gx", sqc)
                            sumsq_mm[c] = p.do("tensor", "matmul",
                                               pbank(3, 0, 1), ones1[:, 0:1],
                                               sqb1[c % 2][:],
                                               start=(c == 0),
                                               stop=(c == DB - 1), sem="mm")
                        stats_mm = p.mark("mm")
                        R = rows
                        p.wait("vector", "mm", stats_mm)
                        p.do("vector", "tensor_scalar_mul", R[:, 0:T],
                             pbank(2, 0, 1), 1.0 / D)
                        p.do("vector", "tensor_scalar_mul", R[:, T:2 * T],
                             pbank(3, 0, 1), 1.0 / D, sem="row")
                        p.wait("scalar", "row", p.mark("row"))
                        p.do("scalar", "activation", R[:, 2 * T:3 * T],
                             R[:, 0:T], AF.Square, sem="row")
                        p.wait("vector", "row", p.mark("row"))
                        p.do("vector", "tensor_sub", R[:, 3 * T:4 * T],
                             R[:, T:2 * T], R[:, 2 * T:3 * T])
                        p.do("vector", "tensor_scalar_add", R[:, 3 * T:4 * T],
                             R[:, 3 * T:4 * T], EPS_LN, sem="row")
                        p.wait("scalar", "row", p.mark("row"))
                        p.do("scalar", "activation", R[:, 2 * T:3 * T],
                             R[:, 3 * T:4 * T], AF.Sqrt, sem="row")
                        p.wait("vector", "row", p.mark("row"))
                        rowc = p.do("vector", "reciprocal", R[:, T:2 * T],
                                    R[:, 2 * T:3 * T], sem="row")
                        p.wait("tensor", "row", rowc)
                        p.do("tensor", "matmul", pbank(2), onesr[0:1, :],
                             R[:, 0:T], start=True, stop=True)
                        bc_mm = p.do("tensor", "matmul", pbank(3),
                                     onesr[0:1, :], R[:, T:2 * T],
                                     start=True, stop=True, sem="mm")
                        p.wait("scalar", "mm", bc_mm)
                        p.do("scalar", "copy", mS[:], pbank(2))
                        bc_cp = p.do("scalar", "copy", rS[:], pbank(3),
                                     sem="sx")
                        bc1_last = bc_cp

                        def apply_chunk(c):
                            cs = slice(c * T, (c + 1) * T)
                            if c == 0:
                                p.wait("vector", "sx", bc_cp)
                            p.do("vector", "tensor_sub", tbuf[0][:],
                                 Xf[:, cs], mS[:])
                            p.do("vector", "tensor_mul", ubuf[0][:],
                                 tbuf[0][:], rS[:])
                            p.do("vector", "tensor_copy", xnh[:, cs],
                                 ubuf[0][:])
                            lnlo[c] = p.do("vector", "tensor_sub", xnl[:, cs],
                                           ubuf[0][:], xnh[:, cs], sem="vx")
                            return lnlo[c]

                        return apply_chunk

                    def p1_apply_all(tt):
                        ap = p1_loadln(tt)
                        for c in range(DB):
                            ap(c)
                        return p.mark("vx")

                    xnr1 = {0: p1_apply_all(0)}
                    for tt in range(nt1):
                        t0 = tt * T
                        xnh, xnl = xnh2[tt % 2], xnl2[tt % 2]
                        xn_ready = xnr1[tt]

                        p.wait("tensor", "vx", xn_ready)
                        for m in range(M_QKV):
                            key = m % 2
                            gstep = tt * M_QKV + m
                            if gstep >= 2:
                                p.wait("sync", "mm", mmQ[gstep - 2])
                            wqsem = "wqa" if key == 0 else "wqb"
                            lw1 = p.do("sync", "dma_start", sem=wqsem, amt=16,
                                       out=wqs_h[key][:],
                                       in_=wqh[:, m, :, :].rearrange(
                                           "p a b -> p (a b)"))
                            lw2 = p.do("sync", "dma_start", sem=wqsem, amt=16,
                                       out=wqs_l[key][:],
                                       in_=wql[:, m, :, :].rearrange(
                                           "p a b -> p (a b)"))
                            bk = 2 + (m % 4)
                            p.wait("tensor", wqsem, lw2)
                            if gstep >= 4:
                                p.wait("tensor", "vx", evq[gstep - 4])
                            first = True
                            for wop, xop in ((wqs_h[key], xnh), (wqs_l[key], xnh),
                                             (wqs_h[key], xnl)):
                                for db in range(DB):
                                    last = (wop is wqs_h[key] and xop is xnl
                                            and db == DB - 1)
                                    p.do("tensor", "matmul", pbank(bk),
                                         wop[:, db * 128:(db + 1) * 128],
                                         xop[:, db * T:(db + 1) * T],
                                         start=first, stop=last,
                                         sem="mm" if last else None)
                                    first = False
                            mmQ[gstep] = p.mark("mm")
                            p.wait("scalar", "mm", mmQ[gstep])
                            if m < DB:
                                qc = m
                                hidst = qs_h[:, qc * T:(qc + 1) * T]
                                lodst = qs_l[:, qc * T:(qc + 1) * T]
                                if tt > 0:
                                    p.wait("scalar", f"qwr{qc}",
                                           q_wr.get(qc, 0))
                            elif m < 2 * DB:
                                kc = m - DB
                                hidst = k_hi[:, kc * T:(kc + 1) * T]
                                lodst = k_lo[:, kc * T:(kc + 1) * T]
                            else:
                                vc = m - 2 * DB
                                hidst = vst[m % 2][:]
                                lodst = vsl[m % 2][:]
                                if m - 2 >= 2 * DB:
                                    p.wait("scalar", "tr",
                                           evq[("vtr", gstep - 2)])
                            hicp = p.do("scalar", "copy", hidst, pbank(bk),
                                        sem="sx")
                            hic_last = hicp
                            p.wait("vector", "sx", hicp)
                            evq[gstep] = p.do("vector", "tensor_sub", lodst,
                                              pbank(bk), hidst, sem="vx")
                            if m >= 2 * DB:
                                vc = m - 2 * DB
                                p.wait("tensor", "vx", evq[gstep])
                                for sub in range(4):
                                    for part, src in ((0, hidst), (1, lodst)):
                                        hb = sub % 2
                                        if (hb in vtr_ev):
                                            p.wait("tensor", "ev", vtr_ev[hb])
                                        p.do("tensor", "transpose", phbank(hb),
                                             src[:, sub * 128:(sub + 1) * 128],
                                             eyeH[:], sem="tr")
                                        p.wait("scalar", "tr", p.mark("tr"))
                                        dstt = (vtmh if part == 0 else vtml)
                                        o0 = sub * 1040 + (2 * vc) * 65
                                        o1 = sub * 1040 + (2 * vc + 1) * 65
                                        p.do("scalar", "copy",
                                             dstt[:, o0:o0 + 64],
                                             phbank(hb, 0, 128, 0, 64))
                                        vtr_ev[hb] = p.do(
                                            "scalar", "copy",
                                            dstt[:, o1:o1 + 64],
                                            phbank(hb, 0, 128, 64, 128),
                                            sem="ev")
                                evq[("vtr", gstep)] = p.mark("tr")
                        for qc in range(DB):
                            p.wait("gpsimd", "vx", evq[tt * M_QKV + qc])
                            p.do("gpsimd", "dma_start", sem=f"qwr{qc}", amt=16,
                                 out=qhh[:, qc, t0:t0 + T],
                                 in_=qs_h[:, qc * T:(qc + 1) * T])
                            q_wr[qc] = p.do("gpsimd", "dma_start",
                                            sem=f"qwr{qc}", amt=16,
                                            out=qhl[:, qc, t0:t0 + T],
                                            in_=qs_l[:, qc * T:(qc + 1) * T])

                        pend_apply = None
                        if tt + 1 < nt1:
                            pend_apply = p1_loadln(tt + 1)

                        vtm_done = p.mark("ev")
                        if tt == 0:
                            p.wait("vector", "gx", zer)
                        def emit_kv(h):
                            kvb = h % 2
                            if h >= 2:
                                p.wait("tensor", "epi", kva[tt * HEADS + h - 2])
                            elif tt > 0:
                                p.wait("tensor", "epi",
                                       kva[(tt - 1) * HEADS + HEADS - 2 + h])
                            p.wait("tensor", "vx", kel_m[h])
                            keh_b, kel_b = ke_h[h % 2], ke_l[h % 2]
                            for ts in range(4):
                                vh = vtmh[:, ts * 1040 + h * 65:
                                          ts * 1040 + (h + 1) * 65]
                                vl = vtml[:, ts * 1040 + h * 65:
                                          ts * 1040 + (h + 1) * 65]
                                keh_s = keh_b[:, ts * FEAT:(ts + 1) * FEAT]
                                kel_s = kel_b[:, ts * FEAT:(ts + 1) * FEAT]
                                p.do("tensor", "matmul",
                                     pbank(kvb, 0, 65, 0, FEAT), vh, keh_s,
                                     start=(ts == 0), stop=False,
                                     skip_group_check=True)
                                p.do("tensor", "matmul",
                                     pbank(kvb, 0, 65, 0, FEAT), vl, keh_s,
                                     start=False, stop=False,
                                     skip_group_check=True)
                                p.do("tensor", "matmul",
                                     pbank(kvb, 0, 65, 0, FEAT), vh, kel_s,
                                     start=False, stop=(ts == 3),
                                     sem="mm" if ts == 3 else None,
                                     skip_group_check=True)
                            kvm_m[tt * HEADS + h] = p.mark("mm")
                            p.wait("vector", "mm", kvm_m[tt * HEADS + h])
                            kva[tt * HEADS + h] = p.do(
                                "vector", "tensor_add",
                                kv_acc[0:65, h * FEAT:(h + 1) * FEAT],
                                pbank(kvb, 0, 65, 0, FEAT),
                                kv_acc[0:65, h * FEAT:(h + 1) * FEAT],
                                sem="epi")

                        for h in range(HEADS):
                            hb2 = (h % 2) * 64
                            kc = h // 2
                            hkey = tt * HEADS + h
                            if h == 0:
                                p.wait("tensor", "vx",
                                       evq[tt * M_QKV + M_QKV - 1])
                                p.wait("tensor", "ev", vtm_done)
                                p.wait("tensor", "sx", bc1_last)
                            if hkey >= 1:
                                p.wait("tensor", "vx", elu_ps[hkey - 1])
                            ph = projS_h[hb2:hb2 + 64, h * FEAT:(h + 1) * FEAT]
                            pl = projS_l[hb2:hb2 + 64, h * FEAT:(h + 1) * FEAT]
                            for ts in range(4):
                                pb = 2 + ts
                                ksl_h = k_hi[hb2:hb2 + 64,
                                             kc * T + ts * 128:kc * T + (ts + 1) * 128]
                                ksl_l = k_lo[hb2:hb2 + 64,
                                             kc * T + ts * 128:kc * T + (ts + 1) * 128]
                                p.do("tensor", "matmul",
                                     pbank(pb, 0, 128, 0, FEAT),
                                     ksl_h, ph, start=True, stop=False)
                                p.do("tensor", "matmul",
                                     pbank(pb, 0, 128, 0, FEAT),
                                     ksl_l, ph, start=False, stop=False)
                                p.do("tensor", "matmul",
                                     pbank(pb, 0, 128, 0, FEAT),
                                     ksl_h, pl, start=False, stop=(ts == 3),
                                     sem="mm" if ts == 3 else None)
                            kpm = p.mark("mm")
                            if h >= 1:
                                emit_kv(h - 1)
                            psall = psum[0:128, 2:6, 0:FEAT]
                            keh_b, kel_b = ke_h[h % 2], ke_l[h % 2]
                            elm3 = el_m[:].rearrange("p (a b) -> p a b", b=FEAT)
                            elf3 = el_f[:].rearrange("p (a b) -> p a b", b=FEAT)
                            p.wait("scalar", "mm", kpm)
                            if hkey >= 1:
                                p.wait("scalar", "vx", elu_ps[hkey - 1])
                            exm = p.do("scalar", "activation", elf3, psall,
                                       AF.Exp, sem="sx")
                            p.wait("vector", "sx", exm)
                            if hkey >= 2:
                                p.wait("vector", "mm", kvm_m[hkey - 2])
                            p.do("vector", "tensor_scalar_min", el_f[:],
                                 el_f[:], 1.0, sem="vx")
                            p.do("vector", "scalar_tensor_tensor", elm3,
                                 psall, 0.0, elf3, OP.max, OP.add)
                            elu_ps[hkey] = p.mark("vx") + 1
                            p.do("vector", "tensor_copy", keh_b[:], el_m[:],
                                 sem="vx")
                            kel_m[h] = p.do("vector", "tensor_sub", kel_b[:],
                                            el_m[:], keh_b[:], sem="vx")
                            if pend_apply is not None and h >= 8:
                                pend_apply(h - 8)
                                if h == HEADS - 1:
                                    xnr1[tt + 1] = p.mark("vx")
                        emit_kv(HEADS - 1)
                    p.wait("gpsimd", "wr")
                    p.wait("gpsimd", "epi")
                    if taps:
                        p.wait("gpsimd", "epi")
                        p.wait("gpsimd", "vx")
                        p.wait("gpsimd", "sx")
                        p.do("gpsimd", "dma_start", sem="wr", amt=16,
                             out=tap_aps["t_kv"], in_=kv_acc[:])
                        if not taps_full:
                            for nm, tile in (("t_kh", k_hi), ("t_kl", k_lo),
                                             ("t_vh", vtmh), ("t_vl", vtml),
                                             ("t_qh2", qs_h), ("t_ql2", qs_l),
                                             ("t_keh", ke_h[1]), ("t_kel", ke_l[1])):
                                p.do("gpsimd", "dma_start", sem="wr", amt=16,
                                     out=tap_aps[nm], in_=tile[:])
                        p.wait("gpsimd", "wr")

                p.flush_block("attn1")
                if not taps or taps_full:
                    with ExitStack() as st2:
                        def sb2(nm, shape, dt=F32):
                            return st2.enter_context(nc.sbuf_tensor("a2" + nm, shape, dt))

                        Xf = sb2("Xf", [128, DB * T], F32)
                        qs_h = sb2("qsh", [128, DB * T], F16)
                        qs_l = sb2("qsl", [128, DB * T], F16)
                        wosb_h = sb2("woh", [128, DB * DB * 128], F16)
                        wosb_l = sb2("wol", [128, DB * DB * 128], F16)
                        qe_h = [sb2(f"qeh{i}", [128, 4 * T], F16)
                                for i in range(2)]
                        qe_l = [sb2(f"qel{i}", [128, 4 * T], F16)
                                for i in range(2)]
                        el2_m = sb2("elm", [128, 4 * T], F32)
                        el2_f = sb2("elf", [128, 4 * T], F32)
                        at_h = sb2("ath", [128, DB * T], F16)
                        at_l = sb2("atl", [128, DB * T], F16)
                        aof = [sb2(f"aof{i}", [128, T], F32) for i in range(2)]
                        ub2 = sb2("ub2", [128, T], F32)
                        zst2 = sb2("zst2", [64, T], F32)

                        onesP = sb2("onesP", [33, 128], F32)
                        p.do("gpsimd", "memset", onesP[:], 1.0)
                        zmem = p.do("gpsimd", "memset", zst2[:], 1.0, sem="gx")
                        p.do("sync", "dma_start", sem="ld", amt=16, out=wosb_h[:],
                             in_=woh.rearrange("p a b c -> p (a b c)"))
                        wold = p.do("sync", "dma_start", sem="ld", amt=16,
                                    out=wosb_l[:],
                                    in_=wol.rearrange("p a b c -> p (a b c)"))

                        p.wait("tensor", "epi")
                        kv_ev = {}
                        for h in range(HEADS):
                            for fc in range(2):
                                bk = 4 + fc
                                if bk in kv_ev:
                                    p.wait("tensor", "vx", kv_ev[bk])
                                p.do("tensor", "transpose", pbank(bk, 0, 128, 0, 65),
                                     kv_acc[0:65, h * FEAT + fc * 128:
                                            h * FEAT + (fc + 1) * 128],
                                     eyeS[0:65, 0:65], sem="tr")
                                trm = p.mark("tr")
                                p.wait("scalar", "tr", trm)
                                o65 = (h * 2 + fc) * 65
                                hcp = p.do("scalar", "copy",
                                           kv65h[:, o65:o65 + 65],
                                           pbank(bk, 0, 128, 0, 65), sem="sx")
                                p.wait("vector", "sx", hcp)
                                kv_ev[bk] = p.do("vector", "tensor_sub",
                                                 kv65l[:, o65:o65 + 65],
                                                 pbank(bk, 0, 128, 0, 65),
                                                 kv65h[:, o65:o65 + 65], sem="vx")
                        kv65_done = p.mark("vx")
                        p.wait("tensor", "ld", wold)

                        atg = {}
                        stt2 = {}
                        amm_pair = {}
                        zad_m = {}
                        wo_epi = {}
                        zb_rd = {}
                        aof_rd = {}

                        wrm2 = {}
                        last_amm = 0
                        for tt in range(NT if not taps else 1):
                            t0 = tt * T
                            for c in range(DB):
                                if tt > 0:
                                    p.wait("sync", "wr", wrm2[c])
                                    if c == 0:
                                        p.wait("sync", "mm", last_amm)
                                p.do("sync", "dma_start", sem="ld", amt=16,
                                     out=Xf[:, c * T:(c + 1) * T],
                                     in_=Xa[:, c, t0:t0 + T])
                                p.do("sync", "dma_start", sem="ld", amt=16,
                                     out=qs_h[:, c * T:(c + 1) * T],
                                     in_=qhh[:, c, t0:t0 + T])
                                ldc = p.do("sync", "dma_start", sem="ld", amt=16,
                                           out=qs_l[:, c * T:(c + 1) * T],
                                           in_=qhl[:, c, t0:t0 + T])
                            p.wait("tensor", "ld", ldc)
                            p.wait("tensor", "epi")
                            if tt == 0:
                                p.wait("tensor", "vx", kv65_done)
                            def emit_zbc_norm(j):
                                oc = j
                                ob = j % 2
                                p.wait("tensor", "vx", zad_m[j])
                                p.wait("tensor", "sx", atg[2 * j])
                                p.do("tensor", "matmul", pbank(4, 0, 64),
                                     onesr[0:1, 0:64], zst2[0:1, :],
                                     start=True, stop=True, skip_group_check=True)
                                zbm = p.do("tensor", "matmul", pbank(4, 64, 128),
                                           onesP[32:33, 0:64], zst2[32:33, :],
                                           start=True, stop=True, sem="mm",
                                           skip_group_check=True)
                                ocs = slice(oc * T, (oc + 1) * T)
                                p.wait("vector", "mm", zbm)
                                p.do("vector", "tensor_mul", at_h[:, ocs],
                                     aof[ob][:], pbank(4))
                                p.do("vector", "tensor_mul", ub2[:],
                                     aof[ob][:], pbank(4), sem="vx")
                                zb_rd[j] = p.mark("vx")
                                aof_rd[ob] = zb_rd[j]
                                p.do("vector", "tensor_sub", at_l[:, ocs],
                                     ub2[:], at_h[:, ocs], sem="hs")

                            for mpr in range(8):
                                h0, h1 = 2 * mpr, 2 * mpr + 1
                                qc = mpr
                                pkey = tt * 8 + mpr
                                if pkey >= 1:
                                    p.wait("tensor", "vx", stt2[pkey - 1])
                                qpm = 0
                                for hp, hbk in enumerate(((h0, 0), (h1, 2))):
                                    h, bk0 = hbk
                                    hb2 = hp * 64
                                    qsl_h = qs_h[hb2:hb2 + 64, qc * T:(qc + 1) * T]
                                    qsl_l = qs_l[hb2:hb2 + 64, qc * T:(qc + 1) * T]
                                    for fc in range(2):
                                        ph = projS_h[hb2:hb2 + 64,
                                                     h * FEAT + fc * 128:
                                                     h * FEAT + (fc + 1) * 128]
                                        pl = projS_l[hb2:hb2 + 64,
                                                     h * FEAT + fc * 128:
                                                     h * FEAT + (fc + 1) * 128]
                                        pb = bk0 + fc
                                        p.do("tensor", "matmul", pbank(pb), ph,
                                             qsl_h, start=True, stop=False)
                                        p.do("tensor", "matmul", pbank(pb), ph,
                                             qsl_l, start=False, stop=False)
                                        qpm = p.do("tensor", "matmul", pbank(pb),
                                                   pl, qsl_h, start=False, stop=True,
                                                   sem="mm" if (hp == 1 and fc == 1)
                                                   else None)
                                if mpr >= 1:
                                    emit_zbc_norm(mpr - 1)
                                ps4 = psum[0:128, 0:4, 0:512]
                                em3 = el2_m[:].rearrange("p (a b) -> p a b", b=512)
                                ef3 = el2_f[:].rearrange("p (a b) -> p a b", b=512)
                                qeh_b, qel_b = qe_h[mpr % 2], qe_l[mpr % 2]
                                p.wait("scalar", "mm", qpm)
                                if pkey >= 1:
                                    p.wait("scalar", "vx", stt2[pkey - 1])
                                exm = p.do("scalar", "activation", ef3, ps4,
                                           AF.Exp, sem="sx")
                                p.wait("vector", "sx", exm)
                                if pkey >= 2:
                                    p.wait("vector", "mm", amm_pair[pkey - 2])
                                p.do("vector", "tensor_scalar_min", el2_f[:],
                                     el2_f[:], 1.0, sem="vx")
                                p.do("vector", "scalar_tensor_tensor", em3,
                                     ps4, 0.0, ef3, OP.max, OP.add)
                                stt2[pkey] = p.mark("vx") + 1
                                p.do("vector", "tensor_copy", qeh_b[:], el2_m[:],
                                     sem="vx")
                                qeld = p.do("vector", "tensor_sub", qel_b[:],
                                            el2_m[:], qeh_b[:], sem="vx")
                                p.wait("tensor", "vx", qeld)
                                if mpr >= 1:
                                    p.wait("tensor", "vx", zb_rd.get(mpr - 1, 0))
                                    p.wait("tensor", "sx", atg.get(h1 - 2, 0))
                                for hp, hab in enumerate(((h0, 4), (h1, 5))):
                                    h, ab = hab
                                    first = True
                                    for fc in range(2):
                                        o65 = (h * 2 + fc) * 65
                                        qoff = (hp * 2 + fc) * T
                                        for wop, xop in ((kv65h, qeh_b),
                                                         (kv65l, qeh_b),
                                                         (kv65h, qel_b)):
                                            last = (fc == 1 and xop is qel_b)
                                            p.do("tensor", "matmul",
                                                 pbank(ab, 0, 65),
                                                 wop[:, o65:o65 + 65],
                                                 xop[:, qoff:qoff + T],
                                                 start=first, stop=last,
                                                 sem="mm" if last else None,
                                                 skip_group_check=True)
                                            first = False
                                    amm_pair[pkey] = p.mark("mm")
                                    last_amm = amm_pair[pkey]
                                    p.wait("scalar", "mm", amm_pair[pkey])
                                    if pkey == 0:
                                        p.wait("scalar", "gx", zmem)
                                    zrow = 0 if hp == 0 else 32
                                    p.do("scalar", "copy", zst2[zrow:zrow + 1, :],
                                         pbank(ab, 64, 65), sem="sx")
                                    ob = mpr % 2
                                    if hp == 0 and mpr >= 2:
                                        p.wait("scalar", "vx", aof_rd.get(ob, 0))
                                    atg[h] = p.do("scalar", "copy",
                                                  aof[ob][hp * 64:hp * 64 + 64, :],
                                                  pbank(ab, 0, 64), sem="sx")
                                p.wait("vector", "sx", atg[h1])
                                p.do("vector", "tensor_scalar_add",
                                     zst2[0:33, :], zst2[0:33, :], EPS_Z)
                                zad_m[mpr] = p.do("vector", "reciprocal",
                                                  zst2[0:33, :], zst2[0:33, :],
                                                  sem="vx")
                            emit_zbc_norm(7)

                            at_done = p.mark("hs")
                            for co in range(DB):
                                wb = 2 + (co % 2)
                                if co == 0:
                                    p.wait("tensor", "hs", at_done)
                                if co >= 2:
                                    p.wait("tensor", "epi",
                                           wo_epi[tt * DB + co - 2])
                                first = True
                                for wop, xop in ((wosb_h, at_h), (wosb_l, at_h),
                                                 (wosb_h, at_l)):
                                    for ci in range(DB):
                                        last = (wop is wosb_h and xop is at_l
                                                and ci == DB - 1)
                                        p.do("tensor", "matmul", pbank(wb),
                                             wop[:, (co * DB + ci) * 128:
                                                 (co * DB + ci + 1) * 128],
                                             xop[:, ci * T:(ci + 1) * T],
                                             start=first, stop=last,
                                             sem="mm" if last else None)
                                        first = False
                                wmm = p.mark("mm")
                                p.wait("vector", "mm", wmm)
                                cs = slice(co * T, (co + 1) * T)
                                wo_epi[tt * DB + co] = p.do(
                                    "vector", "scalar_tensor_tensor", Xf[:, cs],
                                    pbank(wb), bo[:, co:co + 1], Xf[:, cs],
                                    OP.add, OP.add, sem="epi")
                                p.wait("gpsimd", "epi", wo_epi[tt * DB + co])
                                wrm2[co] = p.do("gpsimd", "dma_start", sem="wr",
                                                amt=16,
                                                out=Xb[:, co, t0:t0 + T],
                                                in_=Xf[:, cs])
                        p.wait("gpsimd", "wr")
                p.flush_block("attn")

        if "ffn1" in phases:
            ffn_phase("xin", ("fm", Xa), "ffn1")
        if "attn" in phases:
            attn_phase()
        if "ffn2" in phases:
            src_ap = Xb if "attn" in phases else Xa
            ffn_phase(("fm", src_ap), "yout", "ffn2")

        if debug_out is not None:
            src_ap = Xa if debug_out == "Xa" else Xb
            with ExitStack() as std:
                dbg = std.enter_context(nc.sbuf_tensor("dbg", [128, NTOK], F32))
                for c in range(DB):
                    ldc = p.do("sync", "dma_start", sem="ld", amt=16,
                               out=dbg[:], in_=src_ap[:, c, :])
                    p.wait("gpsimd", "ld", ldc)
                    wc = p.do("gpsimd", "dma_start", sem="wr", amt=16,
                              out=y[:, c, :], in_=dbg[:])
                    p.wait("sync", "wr", wc)
                p.wait("gpsimd", "wr")
                p.flush_block("dbg")

    return nc


_NC_CACHE = {}


def _get_nc(phases, debug_out, taps=False, taps_full=False):
    key = (tuple(phases), debug_out, taps, taps_full)
    if key not in _NC_CACHE:
        _NC_CACHE[key] = _build_nc(phases, debug_out, taps, taps_full)
    return _NC_CACHE[key]


def _make_in_maps(inputs):
    x = np.ascontiguousarray(np.asarray(inputs["L0_attn_x"], dtype=np.float32))
    eye = np.eye(128, dtype=np.float32)
    ones_in = np.zeros((128, 2), dtype=np.float32)
    ones_in[:, 0] = 1.0
    e2 = np.zeros((2, 128), dtype=np.float32)
    e2[0, :] = 1.0
    zsel = np.zeros((2, 128), dtype=np.float32)
    zsel[0, 0:64] = 1.0
    zsel[1, 64:128] = 1.0
    common = {
        "w_qkv": np.ascontiguousarray(inputs["L0_attn_w_qkv"], dtype=np.float32),
        "proj": np.ascontiguousarray(inputs["L0_attn_proj_matrix"], dtype=np.float32),
        "w_out": np.ascontiguousarray(inputs["L0_attn_w_out"], dtype=np.float32),
        "b_out": np.ascontiguousarray(inputs["L0_attn_b_out"], dtype=np.float32),
        "w1": np.ascontiguousarray(inputs["L0_ffn_w1"], dtype=np.float32),
        "b1": np.ascontiguousarray(inputs["L0_ffn_b1"], dtype=np.float32),
        "w2": np.ascontiguousarray(inputs["L0_ffn_w2"], dtype=np.float32),
        "b2": np.ascontiguousarray(inputs["L0_ffn_b2"], dtype=np.float32),
        "eye32": eye, "eye16": eye, "ones_in": ones_in, "e2_in": e2,
        "zsel_in": zsel,
    }
    return [dict(common, x=np.ascontiguousarray(x[i])) for i in range(8)]


def kernel(**inputs):
    nc = _get_nc(ALL_PHASES, None)
    in_maps = _make_in_maps(inputs)
    res = run_bass_kernel_spmd(nc, in_maps, core_ids=list(range(8)))
    return np.stack([np.asarray(r["y"], dtype=np.float32) for r in res.results])


# version 14
# speedup vs baseline: 1.0887x; 1.0061x over previous
import numpy as np
from contextlib import ExitStack

import concourse.bass as bass
import concourse.mybir as mybir
from concourse.bass_utils import run_bass_kernel_spmd

F32 = mybir.dt.float32
F16 = mybir.dt.float16
AF = mybir.ActivationFunctionType
OP = mybir.AluOpType
AX = mybir.AxisListType

NTOK = 4096
T = 512
NT = NTOK // T
D = 1024
DB = D // 128
H = 4096
HF = H // 128
HEADS = 16
HD = 64
FEAT = 256
QC = DB
M_QKV = 3 * DB
EPS_LN = 1e-5
EPS_Z = 1e-8

ENGINES = ("sync", "tensor", "vector", "scalar", "gpsimd")
ALL_PHASES = ("ffn1", "attn", "ffn2")


class Prog:

    def __init__(self, nc, stack):
        self.nc = nc
        self.stack = stack
        self.q = {e: [] for e in ENGINES}
        self.sems = {}
        self.cnt = {}

    def sem(self, *names):
        for n in names:
            self.sems[n] = self.stack.enter_context(self.nc.semaphore(n))
            self.cnt[n] = 0

    def do(self, eng, meth, *args, sem=None, amt=1, **kw):
        s = self.sems[sem] if sem is not None else None
        if sem is not None:
            self.cnt[sem] += amt

        def fn(e, meth=meth, args=args, kw=kw, s=s, amt=amt):
            inst = getattr(e, meth)(*args, **kw)
            if s is not None:
                inst.then_inc(s, amt)

        self.q[eng].append(fn)
        return self.cnt.get(sem)

    def wait(self, eng, sem, val=None):
        v = self.cnt[sem] if val is None else val
        if v <= 0:
            return
        s = self.sems[sem]
        self.q[eng].append(lambda e, s=s, v=v: e.wait_ge(s, v))

    def mark(self, sem):
        return self.cnt[sem]

    def flush_block(self, name=None):
        with self.nc.Block(name) as blk:
            for eng in ENGINES:
                fns = self.q[eng]
                if not fns:
                    continue
                getattr(blk, eng)(lambda e, fns=tuple(fns): [f(e) for f in fns])
        self.q = {e: [] for e in ENGINES}


def _build_nc(phases=ALL_PHASES, debug_out=None, taps=False, taps_full=False):
    nc = bass.Bass("TRN2", target_bir_lowering=False)

    x_in = nc.dram_tensor("x", [NTOK, D], F32, kind="ExternalInput").ap()
    w_qkv = nc.dram_tensor("w_qkv", [D, 3 * D], F32, kind="ExternalInput").ap()
    proj = nc.dram_tensor("proj", [HEADS, HD, FEAT], F32, kind="ExternalInput").ap()
    w_out = nc.dram_tensor("w_out", [D, D], F32, kind="ExternalInput").ap()
    b_out = nc.dram_tensor("b_out", [D], F32, kind="ExternalInput").ap()
    w1 = nc.dram_tensor("w1", [H, D], F32, kind="ExternalInput").ap()
    b1 = nc.dram_tensor("b1", [H], F32, kind="ExternalInput").ap()
    w2 = nc.dram_tensor("w2", [D, H], F32, kind="ExternalInput").ap()
    b2 = nc.dram_tensor("b2", [D], F32, kind="ExternalInput").ap()
    eye32 = nc.dram_tensor("eye32", [128, 128], F32, kind="ExternalInput").ap()
    eye16 = nc.dram_tensor("eye16", [128, 128], F32, kind="ExternalInput").ap()
    ones_in = nc.dram_tensor("ones_in", [128, 2], F32, kind="ExternalInput").ap()
    e2_in = nc.dram_tensor("e2_in", [2, 128], F32, kind="ExternalInput").ap()
    zsel_in = nc.dram_tensor("zsel_in", [2, 128], F32, kind="ExternalInput").ap()

    if debug_out is None:
        y = nc.dram_tensor("y", [NTOK, D], F32, kind="ExternalOutput").ap()
    else:
        y = nc.dram_tensor("y", [128, DB, NTOK], F32, kind="ExternalOutput").ap()
    tap_aps = {}
    if taps:
        for nm, shape in (("t_Xf", [128, DB * T]), ("t_xnh", [128, DB * T]),
                          ("t_xnl", [128, DB * T]), ("t_rows", [1, 4 * T]),
                          ("t_a1", [128, HF]), ("t_b1t", [128, HF]),
                          ("t_sc2", [128, DB]), ("t_bb2", [128, DB]),
                          ("t_s1", [128, 1024]), ("t_s2", [128, 1024])):
            tap_aps[nm] = nc.dram_tensor(nm, shape, F32, kind="ExternalOutput").ap()
        for nm in ("t_hh", "t_hl"):
            tap_aps[nm] = nc.dram_tensor(nm, [128, HF * T], F16,
                                         kind="ExternalOutput").ap()
        tap_aps["t_kv"] = nc.dram_tensor("t_kv", [128, HEADS * FEAT], F32,
                                         kind="ExternalOutput").ap()
        for nm, wd in (("t_kv65h", HEADS * 2 * 65), ("t_kv65l", HEADS * 2 * 65),
                       ("t_ath", DB * T), ("t_atl", DB * T),
                       ("t_qsh", DB * T), ("t_qsl", DB * T)):
            tap_aps[nm] = nc.dram_tensor(nm, [128, wd], F16,
                                         kind="ExternalOutput").ap()
        for nm, wd in (("t_kh", DB * T), ("t_kl", DB * T), ("t_vh", 4 * 1040),
                       ("t_vl", 4 * 1040), ("t_qh2", DB * T), ("t_ql2", DB * T),
                       ("t_keh", 4 * FEAT), ("t_kel", 4 * FEAT)):
            tap_aps[nm] = nc.dram_tensor(nm, [128, wd], F16,
                                         kind="ExternalOutput").ap()

    Xa = nc.dram_tensor("Xa", [128, DB, NTOK], F32).ap()
    Xb = nc.dram_tensor("Xb", [128, DB, NTOK], F32).ap()
    s1T = nc.dram_tensor("s1T", [128, HF, DB, 128], F16).ap()
    s2T = nc.dram_tensor("s2T", [128, DB, HF, 128], F16).ap()
    wqh = nc.dram_tensor("wqh", [128, M_QKV, DB, 128], F16).ap()
    wql = nc.dram_tensor("wql", [128, M_QKV, DB, 128], F16).ap()
    woh = nc.dram_tensor("woh", [128, DB, DB, 128], F16).ap()
    wol = nc.dram_tensor("wol", [128, DB, DB, 128], F16).ap()
    projhl = nc.dram_tensor("projhl", [128, 2, HEADS * FEAT], F16).ap()
    qhh = nc.dram_tensor("qhh", [128, QC, NTOK], F16).ap()
    qhl = nc.dram_tensor("qhl", [128, QC, NTOK], F16).ap()

    with ExitStack() as st:
        def sb(name, shape, dt=F32):
            return st.enter_context(nc.sbuf_tensor(name, shape, dt))

        p = Prog(nc, st)
        p.sem("ld", "wr", "sg", "tr", "ev", "vx", "sx", "gx", "mm",
              "row", "hs", "s2l", "gel", "epi",
              *[f"ldx{c}" for c in range(8)],
              *[f"wrx{c}" for c in range(8)],
              *[f"qwr{c}" for c in range(8)],
              "s2a", "s2b", "wqa", "wqb", "s1a", "s1b", "wpl", "wpv", "wpw")

        eyeS = sb("eyeS", [128, 128], F32)
        eyeH = sb("eyeH", [128, 128], F16)
        ones1 = sb("ones1", [128, 2], F32)
        onesr = sb("onesr", [2, 128], F32)
        a1 = sb("a1", [128, HF], F32)
        b1t = sb("b1t", [128, HF], F32)
        sc2 = sb("sc2", [128, DB], F32)
        bb2 = sb("bb2", [128, DB], F32)
        bo = sb("bo", [128, DB], F32)
        zselS = sb("zselS", [2, 128], F32)
        ones16S = sb("ones16S", [128, 1], F16)
        rows = sb("rows", [1, T * 4], F32)
        psum = st.enter_context(nc.psum_tensor("psum", [128, 6, 512], F32))
        psumh = st.enter_context(nc.psum_tensor("psumh", [128, 2, 1024], F16))

        def pbank(i, p0=0, p1=128, f0=0, f1=512):
            return psum[p0:p1, i, f0:f1]

        def phbank(i, p0=0, p1=128, f0=0, f1=128):
            return psumh[p0:p1, i, f0:f1]

        p.do("sync", "dma_start", sem="ld", amt=16, out=eyeS[:], in_=eye32)
        p.do("sync", "dma_start", sem="ld", amt=16, out=ones1[:], in_=ones_in)
        p.do("sync", "dma_start", sem="ld", amt=16, out=onesr[:], in_=e2_in)
        p.do("gpsimd", "dma_start", sem="wr", amt=16, out=eyeH[:], in_=eye16)
        p.do("sync", "dma_start", sem="ld", amt=16, out=zselS[:], in_=zsel_in)
        p.do("gpsimd", "dma_start", sem="wr", amt=16, out=ones16S[:],
             in_=ones_in[:, 0:1], allow_slow_non_contiguous=True)

        with ExitStack() as st0:
            def sb0(name, shape, dt=F32):
                return st0.enter_context(nc.sbuf_tensor(name, shape, dt))

            wld = [sb0(f"wld{i}", [128, H], F32) for i in range(2)]
            sgn = [sb0(f"sgn{i}", [128, H], F16) for i in range(2)]
            stg = sb0("stg", [128, HF * 128], F16)
            hi_t = sb0("hi_t", [128, 4 * D], F16)
            lo_t = sb0("lo_t", [128, 4 * D], F16)
            araw = sb0("araw", [128, HF], F32)
            brow = sb0("brow", [HF, 128], F32)

            p.wait("vector", "ld")
            p.wait("scalar", "ld")
            p.wait("tensor", "ld")
            p.wait("tensor", "wr")

            def prep_sign(n_jb, n_db, w_ap, scr_ap, a_tile):
                cons = {}
                bank_ev = {}
                wr_marks = {}
                for jb in range(n_jb):
                    buf = jb % 2
                    if jb >= 2:
                        vxm, sgm, trm = cons[jb - 2]
                        p.wait("sync", "vx", vxm)
                        p.wait("sync", "sg", sgm)
                    else:
                        p.wait("sync", "vx")
                        p.wait("sync", "sg")
                    ldc = p.do("sync", "dma_start", sem="ld", amt=16,
                               out=wld[buf][:, 0:n_db * 128],
                               in_=w_ap[jb * 128:(jb + 1) * 128, :])
                    p.wait("vector", "ld", ldc)
                    vxm = p.do("vector", "reduce_sum", a_tile[:, jb:jb + 1],
                               wld[buf][:, 0:n_db * 128], axis=AX.X,
                               apply_absolute_value=True, sem="vx")
                    p.wait("scalar", "ld", ldc)
                    if jb >= 2:
                        p.wait("scalar", "tr", cons[jb - 2][2])
                    if jb >= 1:
                        p.wait("scalar", "wr", wr_marks[jb - 1])
                    sgm = p.do("scalar", "sign", sgn[buf][:, 0:n_db * 128],
                               wld[buf][:, 0:n_db * 128], sem="sg")
                    p.wait("tensor", "sg", sgm)
                    for db in range(n_db):
                        bk = db % 2
                        if bk in bank_ev:
                            p.wait("tensor", "ev", bank_ev[bk])
                        p.do("tensor", "transpose", phbank(bk),
                             sgn[buf][:, db * 128:(db + 1) * 128], eyeH[:],
                             sem="tr")
                        trm = p.mark("tr")
                        p.wait("scalar", "tr", trm)
                        bank_ev[bk] = p.do("scalar", "copy",
                                           stg[:, db * 128:(db + 1) * 128],
                                           phbank(bk), sem="ev")
                    p.wait("gpsimd", "ev", p.mark("ev"))
                    wr_marks[jb] = p.do("gpsimd", "dma_start", sem="wr", amt=16,
                                        out=scr_ap[:, jb, :, :].rearrange("p a b -> p (a b)"),
                                        in_=stg[:, 0:n_db * 128])
                    cons[jb] = (vxm, sgm, trm)

            prep_sign(HF, DB, w1, s1T, araw)
            p.wait("vector", "vx")
            p.do("vector", "tensor_scalar_mul", a1[:], araw[:, 0:HF], 1.0 / D)
            prep_sign(DB, HF, w2, s2T, sc2)
            p.wait("vector", "vx")
            p.do("vector", "tensor_scalar_mul", sc2[:], sc2[:, 0:DB], 0.5 / H,
                 sem="vx")

            def prep_bias(b_ap, n, out_tile, scale):
                p.wait("sync", "tr")
                ldc = p.do("sync", "dma_start", sem="ld", amt=16,
                           out=brow[0:n, :],
                           in_=b_ap.rearrange("(a b) -> a b", b=128))
                p.wait("tensor", "ld", ldc)
                p.wait("tensor", "ev")
                p.do("tensor", "transpose", pbank(4, 0, 128, 0, n),
                     brow[0:n, :], eyeS[0:n, 0:n], sem="tr")
                p.wait("vector", "tr", p.mark("tr"))
                p.do("vector", "tensor_scalar_mul", out_tile[:, 0:n],
                     pbank(4, 0, 128, 0, n), scale, sem="vx")

            prep_bias(b1, HF, b1t, 1.0)
            prep_bias(b2, DB, bb2, 0.5)
            prep_bias(b_out, DB, bo, 1.0)

            def prep_split(w_ap, n_db, n_out, hi_scr, lo_scr):
                for db in range(n_db):
                    buf = db % 2
                    p.wait("sync", "vx")
                    p.wait("sync", "sg")
                    ldc = p.do("sync", "dma_start", sem="ld", amt=16,
                               out=wld[buf][:, 0:n_out],
                               in_=w_ap[db * 128:(db + 1) * 128, :])
                    p.wait("vector", "ld", ldc)
                    p.wait("vector", "wr")
                    p.do("vector", "tensor_copy", hi_t[:, 0:n_out],
                         wld[buf][:, 0:n_out])
                    vc = p.do("vector", "tensor_sub", lo_t[:, 0:n_out],
                              wld[buf][:, 0:n_out], hi_t[:, 0:n_out], sem="vx")
                    p.wait("gpsimd", "vx", vc)
                    p.do("gpsimd", "dma_start", sem="wr", amt=16,
                         out=hi_scr[:, :, db, :],
                         in_=hi_t[:, 0:n_out].rearrange("p (a b) -> p a b", b=128))
                    p.do("gpsimd", "dma_start", sem="wr", amt=16,
                         out=lo_scr[:, :, db, :],
                         in_=lo_t[:, 0:n_out].rearrange("p (a b) -> p a b", b=128))

            p.wait("gpsimd", "wr")
            p.wait("sync", "wr")
            p.wait("sync", "ld")
            p.flush_block("prep")

        def ffn_phase(src, dst, name):
            with ExitStack() as stf:
                def sbf(nm, shape, dt=F32):
                    return stf.enter_context(nc.sbuf_tensor(name + nm, shape, dt))

                s1c = [sbf(f"s1c{i}", [128, DB * 128], F16) for i in range(2)]
                Xf2 = [sbf(f"Xf{i}", [128, DB * T], F32) for i in range(2)]
                xtm = sbf("xtm", [128, D], F32)
                xnh2 = [sbf(f"xnh{i}", [128, DB * T], F16) for i in range(2)]
                xnl2 = [sbf(f"xnl{i}", [128, DB * T], F16) for i in range(2)]
                hh = sbf("hh", [128, HF * T], F16)
                hl = sbf("hl", [128, HF * T], F16)
                gbuf = [sbf(f"g{i}", [128, T], F32) for i in range(2)]
                sqd = [sbf(f"sq{i}", [128, T], F32) for i in range(2)]
                tbuf = sbf("tb", [128, T], F32)
                ubuf = sbf("ub", [128, T], F32)
                mS = sbf("mS", [128, T], F32)
                rS = sbf("rS", [128, T], F32)
                s2c = [sbf(f"s2c{i}", [128, HF * 128], F16) for i in range(2)]
                tmpb = sbf("tmpb", [128, T], F32)
                ytm = xtm
                if name == "ffn1":
                    wp_ld = sbf("wpld", [128, 3 * D], F32)
                    wp_hi = sbf("wphi", [128, 3 * D], F16)
                    wp_lo = sbf("wplo", [128, 3 * D], F16)
                    wp_items = ([("wq", i) for i in range(DB)]
                                + [("wo", i) for i in range(DB)]
                                + [("pj", 0), ("pj", 1)])

                def emit_wp_item(kind, db):
                    p.wait("sync", "wpv")
                    if kind == "wq":
                        n_out = 3 * D
                        lw = p.do("sync", "dma_start", sem="wpl", amt=16,
                                  out=wp_ld[:, 0:n_out],
                                  in_=w_qkv[db * 128:(db + 1) * 128, :])
                    elif kind == "wo":
                        n_out = D
                        lw = p.do("sync", "dma_start", sem="wpl", amt=16,
                                  out=wp_ld[:, 0:n_out],
                                  in_=w_out[db * 128:(db + 1) * 128, :])
                    else:
                        n_out = HEADS * FEAT // 2
                        hs8 = slice(db * 8, db * 8 + 8)
                        p.do("sync", "dma_start", sem="wpl", amt=16,
                             out=wp_ld[0:64, 0:n_out].rearrange(
                                 "d (h f) -> d h f", f=FEAT),
                             in_=proj[hs8].rearrange("h d f -> d h f"))
                        lw = p.do("sync", "dma_start", sem="wpl", amt=16,
                                  out=wp_ld[64:128, 0:n_out].rearrange(
                                      "d (h f) -> d h f", f=FEAT),
                                  in_=proj[hs8].rearrange("h d f -> d h f"))
                    p.wait("vector", "wpl", lw)
                    p.wait("vector", "wpw")
                    p.do("vector", "tensor_copy", wp_hi[:, 0:n_out],
                         wp_ld[:, 0:n_out])
                    vc = p.do("vector", "tensor_sub", wp_lo[:, 0:n_out],
                              wp_ld[:, 0:n_out], wp_hi[:, 0:n_out], sem="wpv")
                    p.wait("gpsimd", "wpv", vc)
                    if kind == "wq":
                        p.do("gpsimd", "dma_start", sem="wpw", amt=16,
                             out=wqh[:, :, db, :],
                             in_=wp_hi[:, 0:n_out].rearrange(
                                 "p (a b) -> p a b", b=128))
                        p.do("gpsimd", "dma_start", sem="wpw", amt=16,
                             out=wql[:, :, db, :],
                             in_=wp_lo[:, 0:n_out].rearrange(
                                 "p (a b) -> p a b", b=128))
                    elif kind == "wo":
                        p.do("gpsimd", "dma_start", sem="wpw", amt=16,
                             out=woh[:, :, db, :],
                             in_=wp_hi[:, 0:n_out].rearrange(
                                 "p (a b) -> p a b", b=128))
                        p.do("gpsimd", "dma_start", sem="wpw", amt=16,
                             out=wol[:, :, db, :],
                             in_=wp_lo[:, 0:n_out].rearrange(
                                 "p (a b) -> p a b", b=128))
                    else:
                        o8 = db * (HEADS * FEAT // 2)
                        p.do("gpsimd", "dma_start", sem="wpw", amt=16,
                             out=projhl[:, 0, o8:o8 + n_out],
                             in_=wp_hi[:, 0:n_out])
                        p.do("gpsimd", "dma_start", sem="wpw", amt=16,
                             out=projhl[:, 1, o8:o8 + n_out],
                             in_=wp_lo[:, 0:n_out])

                sumsq_mm = {}
                mmA_m = {}
                ec_last = 0
                bc_last = 0
                wrm = {}
                ytr = {0: 0, 1: 0}
                yev = {0: 0, 1: 0}
                gel_marks = {}
                hs_marks = {}
                epi_marks = {}
                mmB_marks = {}
                bank_ev = {}
                nt_eff = NT if not (taps and not taps_full) else 1

                def emit_loadln(tt):
                    nonlocal bc_last
                    t0 = tt * T
                    Xf = Xf2[tt % 2]
                    xnh, xnl = xnh2[tt % 2], xnl2[tt % 2]
                    if src == "xin":
                        if tt > 1:
                            p.wait("scalar", "epi")
                            p.wait("scalar", "wr")
                        p.wait("tensor", "sx", bc_last)
                        for ts in range(4):
                            if ts > 0:
                                p.wait("sync", "tr")
                            ldc = p.do("sync", "dma_start", sem="ld", amt=16,
                                       out=xtm[:],
                                       in_=x_in[t0 + ts * 128:t0 + (ts + 1) * 128, :])
                            p.wait("tensor", "ld", ldc)
                            for c in range(DB):
                                bk = c % 2
                                if bk in bank_ev:
                                    p.wait("tensor", "ev", bank_ev[bk])
                                p.do("tensor", "transpose",
                                     pbank(bk, 0, 128, 0, 128),
                                     xtm[:, c * 128:(c + 1) * 128], eyeS[:],
                                     sem="tr")
                                p.wait("scalar", "tr", p.mark("tr"))
                                bank_ev[bk] = p.do(
                                    "scalar", "copy",
                                    Xf[:, c * T + ts * 128:c * T + (ts + 1) * 128],
                                    pbank(bk, 0, 128, 0, 128), sem="ev")
                        xf_sem, xf_ready = "ev", p.mark("ev")
                        ldsems = {c: "ev" for c in range(DB)}
                        ldcs = {c: xf_ready for c in range(DB)}
                    else:
                        _, src_ap = src
                        ldcs = {}
                        ldsems = {}
                        for c in range(DB):
                            if tt > 1:
                                if dst != "yout":
                                    p.wait("sync", f"wrx{c}", wrm[(tt % 2, c)])
                                else:
                                    p.wait("sync", "tr", ytr[tt % 2])
                            ldsems[c] = f"ldx{c}"
                            ldcs[c] = p.do("sync", "dma_start", sem=ldsems[c],
                                           amt=16,
                                           out=Xf[:, c * T:(c + 1) * T],
                                           in_=src_ap[:, c, t0:t0 + T])
                        xf_sem, xf_ready = ldsems[DB - 1], ldcs[DB - 1]

                    for c in range(DB):
                        p.wait("scalar", ldsems[c], ldcs[c])
                        if c >= 2:
                            p.wait("scalar", "mm", sumsq_mm[c - 2])
                        elif tt > 0:
                            p.wait("scalar", "mm", sumsq_mm[DB - 2 + c])
                        sqc = p.do("scalar", "activation", sqd[c % 2][:],
                                   Xf[:, c * T:(c + 1) * T], AF.Square, sem="sx")
                        if c == 0 and src != "xin":
                            p.wait("tensor", "sx", bc_last)
                            if dst == "yout" and tt > 1:
                                p.wait("tensor", "ev", yev[tt % 2])
                        p.wait("tensor", ldsems[c], ldcs[c])
                        p.do("tensor", "matmul", pbank(0, 0, 1), ones1[:, 0:1],
                             Xf[:, c * T:(c + 1) * T],
                             start=(c == 0), stop=(c == DB - 1))
                        p.wait("tensor", "sx", sqc)
                        sumsq_mm[c] = p.do("tensor", "matmul", pbank(1, 0, 1),
                                           ones1[:, 0:1], sqd[c % 2][:],
                                           start=(c == 0), stop=(c == DB - 1),
                                           sem="mm")
                    stats_mm = p.mark("mm")
                    R = rows
                    p.wait("vector", "mm", stats_mm)
                    p.do("vector", "tensor_scalar_mul", R[:, 0:T], pbank(0, 0, 1),
                         1.0 / D)
                    p.do("vector", "tensor_scalar_mul", R[:, T:2 * T],
                         pbank(1, 0, 1), 1.0 / D, sem="row")
                    p.wait("scalar", "row", p.mark("row"))
                    p.do("scalar", "activation", R[:, 2 * T:3 * T], R[:, 0:T],
                         AF.Square, sem="row")
                    p.wait("vector", "row", p.mark("row"))
                    p.do("vector", "tensor_sub", R[:, 3 * T:4 * T], R[:, T:2 * T],
                         R[:, 2 * T:3 * T])
                    p.do("vector", "tensor_scalar_add", R[:, 3 * T:4 * T],
                         R[:, 3 * T:4 * T], EPS_LN, sem="row")
                    p.wait("scalar", "row", p.mark("row"))
                    p.do("scalar", "activation", R[:, 2 * T:3 * T],
                         R[:, 3 * T:4 * T], AF.Sqrt, sem="row")
                    p.wait("vector", "row", p.mark("row"))
                    rowc = p.do("vector", "reciprocal", R[:, T:2 * T],
                                R[:, 2 * T:3 * T], sem="row")
                    p.wait("tensor", "row", rowc)
                    p.do("tensor", "matmul", pbank(0), onesr[0:1, :], R[:, 0:T],
                         start=True, stop=True)
                    bc_mm = p.do("tensor", "matmul", pbank(1), onesr[0:1, :],
                                 R[:, T:2 * T], start=True, stop=True,
                                 sem="mm")
                    p.wait("scalar", "mm", bc_mm)
                    p.do("scalar", "copy", mS[:], pbank(0))
                    bc_cp = p.do("scalar", "copy", rS[:], pbank(1), sem="sx")
                    bc_last = bc_cp
                    for c in range(DB):
                        cs = slice(c * T, (c + 1) * T)
                        if c == 0:
                            p.wait("vector", "sx", bc_cp)
                        p.do("vector", "tensor_sub", tbuf[:], Xf[:, cs], mS[:])
                        p.do("vector", "tensor_mul", ubuf[:], tbuf[:], rS[:])
                        p.do("vector", "tensor_copy", xnh[:, cs], ubuf[:])
                        p.do("vector", "tensor_sub", xnl[:, cs], ubuf[:],
                             xnh[:, cs], sem="vx")
                    xn_ready = p.mark("vx")
                    if taps and tt == 0 and name == "ffn1":
                        p.wait("gpsimd", xf_sem, xf_ready)
                        p.do("gpsimd", "dma_start", sem="wr", amt=16,
                             out=tap_aps["t_Xf"], in_=Xf[:])
                        p.wait("gpsimd", "vx", xn_ready)
                        p.do("gpsimd", "dma_start", sem="wr", amt=16,
                             out=tap_aps["t_xnh"], in_=xnh[:])
                        p.do("gpsimd", "dma_start", sem="wr", amt=16,
                             out=tap_aps["t_xnl"], in_=xnl[:])
                        p.wait("gpsimd", "row", rowc)
                        p.do("gpsimd", "dma_start", sem="wr", amt=16,
                             out=tap_aps["t_rows"], in_=rows[:])
                        p.do("gpsimd", "dma_start", sem="wr", amt=16,
                             out=tap_aps["t_a1"], in_=a1[:])
                        p.do("gpsimd", "dma_start", sem="wr", amt=16,
                             out=tap_aps["t_b1t"], in_=b1t[:])
                        p.do("gpsimd", "dma_start", sem="wr", amt=16,
                             out=tap_aps["t_sc2"], in_=sc2[:])
                        p.do("gpsimd", "dma_start", sem="wr", amt=16,
                             out=tap_aps["t_bb2"], in_=bb2[:])
                        p.do("gpsimd", "dma_start", sem="wr", amt=16,
                             out=tap_aps["t_s1"], in_=s1c[0][:, 0:1024])
                    return xn_ready

                xn_ready_by = {0: emit_loadln(0)}

                for tt in range(nt_eff):
                    t0 = tt * T
                    Xf = Xf2[tt % 2]
                    xnh, xnl = xnh2[tt % 2], xnl2[tt % 2]
                    xn_ready = xn_ready_by[tt]

                    p.wait("tensor", "vx", xn_ready)
                    for hf in range(HF):
                        ga = tt * HF + hf
                        s1sem = "s1a" if hf % 2 == 0 else "s1b"
                        if ga >= 2:
                            p.wait("sync", "mm", mmA_m[ga - 2])
                        s1l = p.do("sync", "dma_start", sem=s1sem, amt=16,
                                   out=s1c[hf % 2][:],
                                   in_=s1T[:, hf, :, :].rearrange(
                                       "p a b -> p (a b)"))
                        bk = 4 + (hf % 2)
                        if hf >= 2:
                            p.wait("tensor", "gel", gel_marks[hf - 2])
                        p.wait("tensor", s1sem, s1l)
                        for pi, xn in enumerate((xnh, xnl)):
                            for db in range(DB):
                                last = (pi == 1 and db == DB - 1)
                                p.do("tensor", "matmul", pbank(bk),
                                     s1c[hf % 2][:, db * 128:(db + 1) * 128],
                                     xn[:, db * T:(db + 1) * T],
                                     start=(pi == 0 and db == 0), stop=last,
                                     sem="mm" if last else None)
                        mmA_m[ga] = mmA = p.mark("mm")
                        p.wait("scalar", "mm", mmA)
                        if hf >= 2:
                            p.wait("scalar", "hs", hs_marks[hf - 2])
                        gel_marks[hf] = p.do("scalar", "activation",
                                             gbuf[hf % 2][:], pbank(bk), AF.Gelu,
                                             bias=b1t[:, hf:hf + 1],
                                             scale=a1[:, hf:hf + 1], sem="gel")
                        p.wait("vector", "gel", gel_marks[hf])
                        hsl = slice(hf * T, (hf + 1) * T)
                        p.do("vector", "tensor_copy", hh[:, hsl], gbuf[hf % 2][:])
                        hs_marks[hf] = p.do("vector", "tensor_sub", hl[:, hsl],
                                            gbuf[hf % 2][:], hh[:, hsl], sem="hs")
                    h_ready = p.mark("hs")

                    if tt + 1 < nt_eff:
                        xn_ready_by[tt + 1] = emit_loadln(tt + 1)

                    for c in range(DB):
                        key = c % 2
                        if (tt * DB + c) >= 2:
                            prev = tt * DB + c - 2
                            p.wait("sync", "mm", mmB_marks[prev])
                        s2sem = "s2a" if key == 0 else "s2b"
                        lds2 = p.do("sync", "dma_start", sem=s2sem, amt=16,
                                    out=s2c[key][:],
                                    in_=s2T[:, c, :, :].rearrange("p a b -> p (a b)"))
                        bk = 2 + key
                        p.wait("tensor", s2sem, lds2)
                        if c == 0:
                            p.wait("tensor", "hs", h_ready)
                            if tt > 0:
                                p.wait("tensor", "sx", ec_last)
                        if (tt * DB + c) >= 2:
                            p.wait("tensor", "epi", epi_marks[tt * DB + c - 2])
                        for pi, hx in enumerate((hh, hl)):
                            for jb in range(HF):
                                last = (pi == 1 and jb == HF - 1)
                                p.do("tensor", "matmul", pbank(bk),
                                     s2c[key][:, jb * 128:(jb + 1) * 128],
                                     hx[:, jb * T:(jb + 1) * T],
                                     start=(pi == 0 and jb == 0), stop=last,
                                     sem="mm" if last else None)
                        mmB_marks[tt * DB + c] = p.mark("mm")
                        p.wait("scalar", "mm", mmB_marks[tt * DB + c])
                        if (tt * DB + c) >= 1:
                            p.wait("scalar", "epi", epi_marks[tt * DB + c - 1])
                        ec = p.do("scalar", "activation", tmpb[:], pbank(bk),
                                  AF.Identity, bias=bb2[:, c:c + 1],
                                  scale=sc2[:, c:c + 1], sem="sx")
                        ec_last = ec
                        p.wait("vector", "sx", ec)
                        cs = slice(c * T, (c + 1) * T)
                        epi_marks[tt * DB + c] = p.do("vector", "tensor_add",
                                                      Xf[:, cs], Xf[:, cs],
                                                      tmpb[:], sem="epi")
                        if dst != "yout":
                            _, dst_ap = dst
                            p.wait("gpsimd", "epi", epi_marks[tt * DB + c])
                            wrm[(tt % 2, c)] = p.do("gpsimd", "dma_start",
                                                    sem=f"wrx{c}", amt=16,
                                                    out=dst_ap[:, c, t0:t0 + T],
                                                    in_=Xf[:, cs])

                    if name == "ffn1":
                        for _ in range(3):
                            if wp_items:
                                emit_wp_item(*wp_items.pop(0))
                    if taps and tt == 0 and name == "ffn1":
                        p.wait("gpsimd", "s2a")
                        p.wait("gpsimd", "s2b")
                        p.do("gpsimd", "dma_start", sem="wr", amt=16,
                             out=tap_aps["t_s2"], in_=s2c[1][:, 0:1024])
                    if dst == "yout":
                        x_done = p.mark("epi")
                        p.wait("tensor", "epi", x_done)
                        p.wait("tensor", "sx", bc_last)
                        for ts in range(4):
                            if ts > 0:
                                p.wait("scalar", "wr")
                            for c in range(DB):
                                bk = c % 2
                                if bk in bank_ev:
                                    p.wait("tensor", "ev", bank_ev[bk])
                                p.do("tensor", "transpose",
                                     pbank(bk, 0, 128, 0, 128),
                                     Xf[:, c * T + ts * 128:c * T + (ts + 1) * 128],
                                     eyeS[:], sem="tr")
                                p.wait("scalar", "tr", p.mark("tr"))
                                bank_ev[bk] = p.do("scalar", "copy",
                                                   ytm[:, c * 128:(c + 1) * 128],
                                                   pbank(bk, 0, 128, 0, 128),
                                                   sem="ev")
                            p.wait("gpsimd", "ev", p.mark("ev"))
                            p.do("gpsimd", "dma_start", sem="wr", amt=16,
                                 out=y[t0 + ts * 128:t0 + (ts + 1) * 128, :],
                                 in_=ytm[:])
                        ytr[tt % 2] = p.mark("tr")
                        yev[tt % 2] = p.mark("ev")

                p.wait("gpsimd", "wr")
                p.wait("gpsimd", "s2a")
                p.wait("gpsimd", "s2b")
                p.wait("gpsimd", "ld")
                p.wait("gpsimd", "wpl")
                p.wait("gpsimd", "wpw")
                p.flush_block(name)

        def attn_phase():
            with ExitStack() as sta:
                def sba(nm, shape, dt=F32):
                    return sta.enter_context(nc.sbuf_tensor("at" + nm, shape, dt))

                kv_acc = sba("kvacc", [128, HEADS * FEAT], F32)
                kv65h = sba("kv65h", [128, HEADS * 2 * 65], F16)
                kv65l = sba("kv65l", [128, HEADS * 2 * 65], F16)
                projS_h = sba("pjh", [128, HEADS * FEAT], F16)
                projS_l = sba("pjl", [128, HEADS * FEAT], F16)
                p.do("sync", "dma_start", sem="ld", amt=16, out=projS_h[:],
                     in_=projhl[:, 0, :])
                pjld = p.do("sync", "dma_start", sem="ld", amt=16,
                            out=projS_l[:], in_=projhl[:, 1, :])
                p.wait("tensor", "ld", pjld)

                with ExitStack() as st1:
                    def sb1(nm, shape, dt=F32):
                        return st1.enter_context(nc.sbuf_tensor("a1" + nm, shape, dt))

                    Xf = sb1("Xf", [128, DB * T], F32)
                    xnh2 = [sb1(f"xnh{i}", [128, DB * T], F16) for i in range(2)]
                    xnl2 = [sb1(f"xnl{i}", [128, DB * T], F16) for i in range(2)]
                    k_hi = sb1("kh", [128, DB * T], F16)
                    k_lo = sb1("kl", [128, DB * T], F16)
                    qs_h = sb1("qsh", [128, DB * T], F16)
                    qs_l = sb1("qsl", [128, DB * T], F16)
                    vst = [sb1(f"vst{i}", [128, T], F16) for i in range(2)]
                    vsl = [sb1(f"vsl{i}", [128, T], F16) for i in range(2)]
                    vtmh = sb1("vtmh", [128, 4 * 1040], F16)
                    vtml = sb1("vtml", [128, 4 * 1040], F16)
                    wqs_h = [sb1(f"wqh{i}", [128, DB * 128], F16) for i in range(2)]
                    wqs_l = [sb1(f"wql{i}", [128, DB * 128], F16) for i in range(2)]
                    ke_h = [sb1(f"keh{i}", [128, 4 * FEAT], F16) for i in range(2)]
                    ke_l = [sb1(f"kel{i}", [128, 4 * FEAT], F16) for i in range(2)]
                    el_m = sb1("elm", [128, 4 * FEAT], F32)
                    el_f = sb1("elf", [128, 4 * FEAT], F32)
                    sqb1 = [sb1(f"sq{i}", [128, T], F32) for i in range(2)]
                    tbuf = [sb1(f"tb{i}", [128, T], F32) for i in range(2)]
                    ubuf = [sb1(f"ub{i}", [128, T], F32) for i in range(2)]
                    mS = sb1("mS", [128, T], F32)
                    rS = sb1("rS", [128, T], F32)

                    zer = p.do("gpsimd", "memset", kv_acc[:], 0.0)
                    p.do("gpsimd", "memset", vtmh[:], 1.0)
                    zer = p.do("gpsimd", "memset", vtml[:], 0.0, sem="gx")

                    sumsq_mm = {}
                    kel_m = {}
                    kvm_m = {}
                    mmQ = {}
                    evq = {}
                    kva = {}
                    elu_ps = {}
                    vtr_ev = {}
                    q_wr = {}

                    lnlo = {}
                    hic_last = 0
                    bc1_last = 0
                    nt1 = NT if not (taps and not taps_full) else 1

                    def p1_loadln(tt):
                        nonlocal bc1_last
                        t0 = tt * T
                        xnh, xnl = xnh2[tt % 2], xnl2[tt % 2]
                        ldcs1 = {}
                        for c in range(DB):
                            if tt > 0:
                                p.wait("sync", "vx", lnlo[c])
                            ldcs1[c] = p.do("sync", "dma_start",
                                            sem=f"ldx{c}", amt=16,
                                            out=Xf[:, c * T:(c + 1) * T],
                                            in_=Xa[:, c, t0:t0 + T])
                        for c in range(DB):
                            p.wait("gpsimd", f"ldx{c}", ldcs1[c])
                            if c >= 2:
                                p.wait("gpsimd", "mm", sumsq_mm[c - 2])
                            elif tt > 0:
                                p.wait("gpsimd", "mm", sumsq_mm[DB - 2 + c])
                            sqc = p.do("gpsimd", "tensor_mul", sqb1[c % 2][:],
                                       Xf[:, c * T:(c + 1) * T],
                                       Xf[:, c * T:(c + 1) * T], sem="gx")
                            if c == 0 and tt > 0:
                                p.wait("tensor", "sx", hic_last)
                                p.wait("tensor", "vx",
                                       evq[tt * M_QKV - 1])
                            p.wait("tensor", f"ldx{c}", ldcs1[c])
                            p.do("tensor", "matmul", pbank(2, 0, 1),
                                 ones1[:, 0:1], Xf[:, c * T:(c + 1) * T],
                                 start=(c == 0), stop=(c == DB - 1))
                            p.wait("tensor", "gx", sqc)
                            sumsq_mm[c] = p.do("tensor", "matmul",
                                               pbank(3, 0, 1), ones1[:, 0:1],
                                               sqb1[c % 2][:],
                                               start=(c == 0),
                                               stop=(c == DB - 1), sem="mm")
                        stats_mm = p.mark("mm")
                        R = rows
                        p.wait("vector", "mm", stats_mm)
                        p.do("vector", "tensor_scalar_mul", R[:, 0:T],
                             pbank(2, 0, 1), 1.0 / D)
                        p.do("vector", "tensor_scalar_mul", R[:, T:2 * T],
                             pbank(3, 0, 1), 1.0 / D, sem="row")
                        p.wait("scalar", "row", p.mark("row"))
                        p.do("scalar", "activation", R[:, 2 * T:3 * T],
                             R[:, 0:T], AF.Square, sem="row")
                        p.wait("vector", "row", p.mark("row"))
                        p.do("vector", "tensor_sub", R[:, 3 * T:4 * T],
                             R[:, T:2 * T], R[:, 2 * T:3 * T])
                        p.do("vector", "tensor_scalar_add", R[:, 3 * T:4 * T],
                             R[:, 3 * T:4 * T], EPS_LN, sem="row")
                        p.wait("scalar", "row", p.mark("row"))
                        p.do("scalar", "activation", R[:, 2 * T:3 * T],
                             R[:, 3 * T:4 * T], AF.Sqrt, sem="row")
                        p.wait("vector", "row", p.mark("row"))
                        rowc = p.do("vector", "reciprocal", R[:, T:2 * T],
                                    R[:, 2 * T:3 * T], sem="row")
                        p.wait("tensor", "row", rowc)
                        p.do("tensor", "matmul", pbank(2), onesr[0:1, :],
                             R[:, 0:T], start=True, stop=True)
                        bc_mm = p.do("tensor", "matmul", pbank(3),
                                     onesr[0:1, :], R[:, T:2 * T],
                                     start=True, stop=True, sem="mm")
                        p.wait("scalar", "mm", bc_mm)
                        p.do("scalar", "copy", mS[:], pbank(2))
                        bc_cp = p.do("scalar", "copy", rS[:], pbank(3),
                                     sem="sx")
                        bc1_last = bc_cp

                        def apply_chunk(c):
                            cs = slice(c * T, (c + 1) * T)
                            if c == 0:
                                p.wait("vector", "sx", bc_cp)
                            p.do("vector", "tensor_sub", tbuf[0][:],
                                 Xf[:, cs], mS[:])
                            p.do("vector", "tensor_mul", ubuf[0][:],
                                 tbuf[0][:], rS[:])
                            p.do("vector", "tensor_copy", xnh[:, cs],
                                 ubuf[0][:])
                            lnlo[c] = p.do("vector", "tensor_sub", xnl[:, cs],
                                           ubuf[0][:], xnh[:, cs], sem="vx")
                            return lnlo[c]

                        return apply_chunk

                    def p1_apply_all(tt):
                        ap = p1_loadln(tt)
                        for c in range(DB):
                            ap(c)
                        return p.mark("vx")

                    xnr1 = {0: p1_apply_all(0)}
                    for tt in range(nt1):
                        t0 = tt * T
                        xnh, xnl = xnh2[tt % 2], xnl2[tt % 2]
                        xn_ready = xnr1[tt]

                        p.wait("tensor", "vx", xn_ready)
                        for m in range(M_QKV):
                            key = m % 2
                            gstep = tt * M_QKV + m
                            if gstep >= 2:
                                p.wait("sync", "mm", mmQ[gstep - 2])
                            wqsem = "wqa" if key == 0 else "wqb"
                            lw1 = p.do("sync", "dma_start", sem=wqsem, amt=16,
                                       out=wqs_h[key][:],
                                       in_=wqh[:, m, :, :].rearrange(
                                           "p a b -> p (a b)"))
                            lw2 = p.do("sync", "dma_start", sem=wqsem, amt=16,
                                       out=wqs_l[key][:],
                                       in_=wql[:, m, :, :].rearrange(
                                           "p a b -> p (a b)"))
                            bk = 2 + (m % 4)
                            p.wait("tensor", wqsem, lw2)
                            if gstep >= 4:
                                p.wait("tensor", "vx", evq[gstep - 4])
                            first = True
                            for wop, xop in ((wqs_h[key], xnh), (wqs_l[key], xnh),
                                             (wqs_h[key], xnl)):
                                for db in range(DB):
                                    last = (wop is wqs_h[key] and xop is xnl
                                            and db == DB - 1)
                                    p.do("tensor", "matmul", pbank(bk),
                                         wop[:, db * 128:(db + 1) * 128],
                                         xop[:, db * T:(db + 1) * T],
                                         start=first, stop=last,
                                         sem="mm" if last else None)
                                    first = False
                            mmQ[gstep] = p.mark("mm")
                            p.wait("scalar", "mm", mmQ[gstep])
                            if m < DB:
                                qc = m
                                hidst = qs_h[:, qc * T:(qc + 1) * T]
                                lodst = qs_l[:, qc * T:(qc + 1) * T]
                                if tt > 0:
                                    p.wait("scalar", f"qwr{qc}",
                                           q_wr.get(qc, 0))
                            elif m < 2 * DB:
                                kc = m - DB
                                hidst = k_hi[:, kc * T:(kc + 1) * T]
                                lodst = k_lo[:, kc * T:(kc + 1) * T]
                            else:
                                vc = m - 2 * DB
                                hidst = vst[m % 2][:]
                                lodst = vsl[m % 2][:]
                                if m - 2 >= 2 * DB:
                                    p.wait("scalar", "tr",
                                           evq[("vtr", gstep - 2)])
                            hicp = p.do("scalar", "copy", hidst, pbank(bk),
                                        sem="sx")
                            hic_last = hicp
                            p.wait("vector", "sx", hicp)
                            evq[gstep] = p.do("vector", "tensor_sub", lodst,
                                              pbank(bk), hidst, sem="vx")
                            if m >= 2 * DB:
                                vc = m - 2 * DB
                                p.wait("tensor", "vx", evq[gstep])
                                for sub in range(4):
                                    for part, src in ((0, hidst), (1, lodst)):
                                        hb = sub % 2
                                        if (hb in vtr_ev):
                                            p.wait("tensor", "ev", vtr_ev[hb])
                                        p.do("tensor", "transpose", phbank(hb),
                                             src[:, sub * 128:(sub + 1) * 128],
                                             eyeH[:], sem="tr")
                                        p.wait("scalar", "tr", p.mark("tr"))
                                        dstt = (vtmh if part == 0 else vtml)
                                        o0 = sub * 1040 + (2 * vc) * 65
                                        o1 = sub * 1040 + (2 * vc + 1) * 65
                                        p.do("scalar", "copy",
                                             dstt[:, o0:o0 + 64],
                                             phbank(hb, 0, 128, 0, 64))
                                        vtr_ev[hb] = p.do(
                                            "scalar", "copy",
                                            dstt[:, o1:o1 + 64],
                                            phbank(hb, 0, 128, 64, 128),
                                            sem="ev")
                                evq[("vtr", gstep)] = p.mark("tr")
                        for qc in range(DB):
                            p.wait("gpsimd", "vx", evq[tt * M_QKV + qc])
                            p.do("gpsimd", "dma_start", sem=f"qwr{qc}", amt=16,
                                 out=qhh[:, qc, t0:t0 + T],
                                 in_=qs_h[:, qc * T:(qc + 1) * T])
                            q_wr[qc] = p.do("gpsimd", "dma_start",
                                            sem=f"qwr{qc}", amt=16,
                                            out=qhl[:, qc, t0:t0 + T],
                                            in_=qs_l[:, qc * T:(qc + 1) * T])

                        pend_apply = None
                        if tt + 1 < nt1:
                            pend_apply = p1_loadln(tt + 1)

                        vtm_done = p.mark("ev")
                        if tt == 0:
                            p.wait("vector", "gx", zer)
                        def emit_kv(h):
                            kvb = h % 2
                            if h >= 2:
                                p.wait("tensor", "epi", kva[tt * HEADS + h - 2])
                            elif tt > 0:
                                p.wait("tensor", "epi",
                                       kva[(tt - 1) * HEADS + HEADS - 2 + h])
                            p.wait("tensor", "vx", kel_m[h])
                            keh_b, kel_b = ke_h[h % 2], ke_l[h % 2]
                            for ts in range(4):
                                vh = vtmh[:, ts * 1040 + h * 65:
                                          ts * 1040 + (h + 1) * 65]
                                vl = vtml[:, ts * 1040 + h * 65:
                                          ts * 1040 + (h + 1) * 65]
                                keh_s = keh_b[:, ts * FEAT:(ts + 1) * FEAT]
                                kel_s = kel_b[:, ts * FEAT:(ts + 1) * FEAT]
                                p.do("tensor", "matmul",
                                     pbank(kvb, 0, 65, 0, FEAT), vh, keh_s,
                                     start=(ts == 0), stop=False,
                                     skip_group_check=True)
                                p.do("tensor", "matmul",
                                     pbank(kvb, 0, 65, 0, FEAT), vl, keh_s,
                                     start=False, stop=False,
                                     skip_group_check=True)
                                p.do("tensor", "matmul",
                                     pbank(kvb, 0, 65, 0, FEAT), vh, kel_s,
                                     start=False, stop=(ts == 3),
                                     sem="mm" if ts == 3 else None,
                                     skip_group_check=True)
                            kvm_m[tt * HEADS + h] = p.mark("mm")
                            p.wait("vector", "mm", kvm_m[tt * HEADS + h])
                            kva[tt * HEADS + h] = p.do(
                                "vector", "tensor_add",
                                kv_acc[0:65, h * FEAT:(h + 1) * FEAT],
                                pbank(kvb, 0, 65, 0, FEAT),
                                kv_acc[0:65, h * FEAT:(h + 1) * FEAT],
                                sem="epi")

                        for h in range(HEADS):
                            hb2 = (h % 2) * 64
                            kc = h // 2
                            hkey = tt * HEADS + h
                            if h == 0:
                                p.wait("tensor", "vx",
                                       evq[tt * M_QKV + M_QKV - 1])
                                p.wait("tensor", "ev", vtm_done)
                                p.wait("tensor", "sx", bc1_last)
                            if hkey >= 1:
                                p.wait("tensor", "vx", elu_ps[hkey - 1])
                            ph = projS_h[hb2:hb2 + 64, h * FEAT:(h + 1) * FEAT]
                            pl = projS_l[hb2:hb2 + 64, h * FEAT:(h + 1) * FEAT]
                            for ts in range(4):
                                pb = 2 + ts
                                ksl_h = k_hi[hb2:hb2 + 64,
                                             kc * T + ts * 128:kc * T + (ts + 1) * 128]
                                ksl_l = k_lo[hb2:hb2 + 64,
                                             kc * T + ts * 128:kc * T + (ts + 1) * 128]
                                p.do("tensor", "matmul",
                                     pbank(pb, 0, 128, 0, FEAT),
                                     ksl_h, ph, start=True, stop=False)
                                p.do("tensor", "matmul",
                                     pbank(pb, 0, 128, 0, FEAT),
                                     ksl_l, ph, start=False, stop=False)
                                p.do("tensor", "matmul",
                                     pbank(pb, 0, 128, 0, FEAT),
                                     ksl_h, pl, start=False, stop=(ts == 3),
                                     sem="mm" if ts == 3 else None)
                            kpm = p.mark("mm")
                            if h >= 1:
                                emit_kv(h - 1)
                            psall = psum[0:128, 2:6, 0:FEAT]
                            keh_b, kel_b = ke_h[h % 2], ke_l[h % 2]
                            elm3 = el_m[:].rearrange("p (a b) -> p a b", b=FEAT)
                            elf3 = el_f[:].rearrange("p (a b) -> p a b", b=FEAT)
                            p.wait("scalar", "mm", kpm)
                            if hkey >= 1:
                                p.wait("scalar", "vx", elu_ps[hkey - 1])
                            exm = p.do("scalar", "activation", elf3, psall,
                                       AF.Exp, sem="sx")
                            p.wait("vector", "sx", exm)
                            if hkey >= 2:
                                p.wait("vector", "mm", kvm_m[hkey - 2])
                            p.do("vector", "tensor_scalar_min", el_f[:],
                                 el_f[:], 1.0, sem="vx")
                            p.do("vector", "scalar_tensor_tensor", elm3,
                                 psall, 0.0, elf3, OP.max, OP.add)
                            elu_ps[hkey] = p.mark("vx") + 1
                            p.do("vector", "tensor_copy", keh_b[:], el_m[:],
                                 sem="vx")
                            kel_m[h] = p.do("vector", "tensor_sub", kel_b[:],
                                            el_m[:], keh_b[:], sem="vx")
                            if pend_apply is not None and h >= 8:
                                pend_apply(h - 8)
                                if h == HEADS - 1:
                                    xnr1[tt + 1] = p.mark("vx")
                        emit_kv(HEADS - 1)
                    p.wait("gpsimd", "wr")
                    p.wait("gpsimd", "epi")
                    if taps:
                        p.wait("gpsimd", "epi")
                        p.wait("gpsimd", "vx")
                        p.wait("gpsimd", "sx")
                        p.do("gpsimd", "dma_start", sem="wr", amt=16,
                             out=tap_aps["t_kv"], in_=kv_acc[:])
                        if not taps_full:
                            for nm, tile in (("t_kh", k_hi), ("t_kl", k_lo),
                                             ("t_vh", vtmh), ("t_vl", vtml),
                                             ("t_qh2", qs_h), ("t_ql2", qs_l),
                                             ("t_keh", ke_h[1]), ("t_kel", ke_l[1])):
                                p.do("gpsimd", "dma_start", sem="wr", amt=16,
                                     out=tap_aps[nm], in_=tile[:])
                        p.wait("gpsimd", "wr")

                p.flush_block("attn1")
                if not taps or taps_full:
                    with ExitStack() as st2:
                        def sb2(nm, shape, dt=F32):
                            return st2.enter_context(nc.sbuf_tensor("a2" + nm, shape, dt))

                        Xf = sb2("Xf", [128, DB * T], F32)
                        qs_h = sb2("qsh", [128, DB * T], F16)
                        qs_l = sb2("qsl", [128, DB * T], F16)
                        wosb_h = sb2("woh", [128, DB * DB * 128], F16)
                        wosb_l = sb2("wol", [128, DB * DB * 128], F16)
                        qe_h = [sb2(f"qeh{i}", [128, 4 * T], F16)
                                for i in range(2)]
                        qe_l = [sb2(f"qel{i}", [128, 4 * T], F16)
                                for i in range(2)]
                        el2_m = sb2("elm", [128, 4 * T], F32)
                        el2_f = sb2("elf", [128, 4 * T], F32)
                        at_h = sb2("ath", [128, DB * T], F16)
                        at_l = sb2("atl", [128, DB * T], F16)
                        aof = [sb2(f"aof{i}", [128, T], F32) for i in range(2)]
                        ub2 = sb2("ub2", [128, T], F32)
                        zst2 = sb2("zst2", [64, T], F32)

                        onesP = sb2("onesP", [33, 128], F32)
                        p.do("gpsimd", "memset", onesP[:], 1.0)
                        zmem = p.do("gpsimd", "memset", zst2[:], 1.0, sem="gx")
                        p.do("sync", "dma_start", sem="ld", amt=16, out=wosb_h[:],
                             in_=woh.rearrange("p a b c -> p (a b c)"))
                        wold = p.do("sync", "dma_start", sem="ld", amt=16,
                                    out=wosb_l[:],
                                    in_=wol.rearrange("p a b c -> p (a b c)"))

                        p.wait("tensor", "epi")
                        kv_ev = {}
                        for h in range(HEADS):
                            for fc in range(2):
                                bk = 4 + fc
                                if bk in kv_ev:
                                    p.wait("tensor", "vx", kv_ev[bk])
                                p.do("tensor", "transpose", pbank(bk, 0, 128, 0, 65),
                                     kv_acc[0:65, h * FEAT + fc * 128:
                                            h * FEAT + (fc + 1) * 128],
                                     eyeS[0:65, 0:65], sem="tr")
                                trm = p.mark("tr")
                                p.wait("scalar", "tr", trm)
                                o65 = (h * 2 + fc) * 65
                                hcp = p.do("scalar", "copy",
                                           kv65h[:, o65:o65 + 65],
                                           pbank(bk, 0, 128, 0, 65), sem="sx")
                                p.wait("vector", "sx", hcp)
                                kv_ev[bk] = p.do("vector", "tensor_sub",
                                                 kv65l[:, o65:o65 + 65],
                                                 pbank(bk, 0, 128, 0, 65),
                                                 kv65h[:, o65:o65 + 65], sem="vx")
                        kv65_done = p.mark("vx")
                        p.wait("tensor", "ld", wold)

                        atg = {}
                        stt2 = {}
                        amm_pair = {}
                        zad_m = {}
                        wo_epi = {}
                        zb_rd = {}
                        aof_rd = {}

                        wrm2 = {}
                        last_amm = 0
                        for tt in range(NT if not taps else 1):
                            t0 = tt * T
                            for c in range(DB):
                                if tt > 0:
                                    p.wait("sync", "wr", wrm2[c])
                                    if c == 0:
                                        p.wait("sync", "mm", last_amm)
                                p.do("sync", "dma_start", sem="ld", amt=16,
                                     out=Xf[:, c * T:(c + 1) * T],
                                     in_=Xa[:, c, t0:t0 + T])
                                p.do("sync", "dma_start", sem="ld", amt=16,
                                     out=qs_h[:, c * T:(c + 1) * T],
                                     in_=qhh[:, c, t0:t0 + T])
                                ldc = p.do("sync", "dma_start", sem="ld", amt=16,
                                           out=qs_l[:, c * T:(c + 1) * T],
                                           in_=qhl[:, c, t0:t0 + T])
                            p.wait("tensor", "ld", ldc)
                            p.wait("tensor", "epi")
                            if tt == 0:
                                p.wait("tensor", "vx", kv65_done)
                            def emit_zbc_norm(j):
                                oc = j
                                ob = j % 2
                                p.wait("tensor", "vx", zad_m[j])
                                p.wait("tensor", "sx", atg[2 * j])
                                p.do("tensor", "matmul", pbank(4, 0, 64),
                                     onesr[0:1, 0:64], zst2[0:1, :],
                                     start=True, stop=True, skip_group_check=True)
                                zbm = p.do("tensor", "matmul", pbank(4, 64, 128),
                                           onesP[32:33, 0:64], zst2[32:33, :],
                                           start=True, stop=True, sem="mm",
                                           skip_group_check=True)
                                ocs = slice(oc * T, (oc + 1) * T)
                                p.wait("vector", "mm", zbm)
                                p.do("vector", "tensor_mul", at_h[:, ocs],
                                     aof[ob][:], pbank(4))
                                p.do("vector", "tensor_mul", ub2[:],
                                     aof[ob][:], pbank(4), sem="vx")
                                zb_rd[j] = p.mark("vx")
                                aof_rd[ob] = zb_rd[j]
                                p.do("vector", "tensor_sub", at_l[:, ocs],
                                     ub2[:], at_h[:, ocs], sem="hs")

                            for mpr in range(8):
                                h0, h1 = 2 * mpr, 2 * mpr + 1
                                qc = mpr
                                pkey = tt * 8 + mpr
                                if pkey >= 1:
                                    p.wait("tensor", "vx", stt2[pkey - 1])
                                qpm = 0
                                qpm_half = {}
                                for hp, hbk in enumerate(((h0, 0), (h1, 2))):
                                    h, bk0 = hbk
                                    hb2 = hp * 64
                                    qsl_h = qs_h[hb2:hb2 + 64, qc * T:(qc + 1) * T]
                                    qsl_l = qs_l[hb2:hb2 + 64, qc * T:(qc + 1) * T]
                                    for fc in range(2):
                                        ph = projS_h[hb2:hb2 + 64,
                                                     h * FEAT + fc * 128:
                                                     h * FEAT + (fc + 1) * 128]
                                        pl = projS_l[hb2:hb2 + 64,
                                                     h * FEAT + fc * 128:
                                                     h * FEAT + (fc + 1) * 128]
                                        pb = bk0 + fc
                                        p.do("tensor", "matmul", pbank(pb), ph,
                                             qsl_h, start=True, stop=False)
                                        p.do("tensor", "matmul", pbank(pb), ph,
                                             qsl_l, start=False, stop=False)
                                        qpm = p.do("tensor", "matmul", pbank(pb),
                                                   pl, qsl_h, start=False,
                                                   stop=True,
                                                   sem="mm" if fc == 1 else None)
                                        if fc == 1:
                                            qpm_half[hp] = p.mark("mm")
                                            qpm = qpm_half[hp]
                                if mpr >= 1:
                                    emit_zbc_norm(mpr - 1)
                                qeh_b, qel_b = qe_h[mpr % 2], qe_l[mpr % 2]
                                qeld_half = {}
                                for hp in range(2):
                                    ps2h = psum[0:128, 2 * hp:2 * hp + 2, 0:512]
                                    fsl = slice(hp * 2 * T, (hp + 1) * 2 * T)
                                    em3 = el2_m[:, fsl].rearrange(
                                        "p (a b) -> p a b", b=512)
                                    ef3 = el2_f[:, fsl].rearrange(
                                        "p (a b) -> p a b", b=512)
                                    if hp == 0:
                                        p.wait("scalar", "mm", qpm_half[0])
                                    else:
                                        p.wait("scalar", "mm", qpm)
                                    if pkey >= 1:
                                        p.wait("scalar", "vx", stt2[pkey - 1])
                                    exm = p.do("scalar", "activation", ef3,
                                               ps2h, AF.Exp, sem="sx")
                                    p.wait("vector", "sx", exm)
                                    if hp == 0 and pkey >= 2:
                                        p.wait("vector", "mm",
                                               amm_pair[pkey - 2])
                                    p.do("vector", "tensor_scalar_min",
                                         el2_f[:, fsl], el2_f[:, fsl], 1.0,
                                         sem="vx")
                                    p.do("vector", "scalar_tensor_tensor",
                                         em3, ps2h, 0.0, ef3, OP.max, OP.add)
                                    if hp == 1:
                                        stt2[pkey] = p.mark("vx") + 1
                                    p.do("vector", "tensor_copy",
                                         qeh_b[:, fsl], el2_m[:, fsl],
                                         sem="vx")
                                    qeld_half[hp] = p.do(
                                        "vector", "tensor_sub",
                                        qel_b[:, fsl], el2_m[:, fsl],
                                        qeh_b[:, fsl], sem="vx")
                                qeld = qeld_half[1]
                                if mpr >= 1:
                                    p.wait("tensor", "vx", zb_rd.get(mpr - 1, 0))
                                    p.wait("tensor", "sx", atg.get(h1 - 2, 0))
                                for hp, hab in enumerate(((h0, 4), (h1, 5))):
                                    h, ab = hab
                                    p.wait("tensor", "vx", qeld_half[hp])
                                    first = True
                                    for fc in range(2):
                                        o65 = (h * 2 + fc) * 65
                                        qoff = (hp * 2 + fc) * T
                                        for wop, xop in ((kv65h, qeh_b),
                                                         (kv65l, qeh_b),
                                                         (kv65h, qel_b)):
                                            last = (fc == 1 and xop is qel_b)
                                            p.do("tensor", "matmul",
                                                 pbank(ab, 0, 65),
                                                 wop[:, o65:o65 + 65],
                                                 xop[:, qoff:qoff + T],
                                                 start=first, stop=last,
                                                 sem="mm" if last else None,
                                                 skip_group_check=True)
                                            first = False
                                    amm_pair[pkey] = p.mark("mm")
                                    last_amm = amm_pair[pkey]
                                    p.wait("scalar", "mm", amm_pair[pkey])
                                    if pkey == 0:
                                        p.wait("scalar", "gx", zmem)
                                    zrow = 0 if hp == 0 else 32
                                    p.do("scalar", "copy", zst2[zrow:zrow + 1, :],
                                         pbank(ab, 64, 65), sem="sx")
                                    ob = mpr % 2
                                    if hp == 0 and mpr >= 2:
                                        p.wait("scalar", "vx", aof_rd.get(ob, 0))
                                    atg[h] = p.do("scalar", "copy",
                                                  aof[ob][hp * 64:hp * 64 + 64, :],
                                                  pbank(ab, 0, 64), sem="sx")
                                p.wait("vector", "sx", atg[h1])
                                p.do("vector", "tensor_scalar_add",
                                     zst2[0:33, :], zst2[0:33, :], EPS_Z)
                                zad_m[mpr] = p.do("vector", "reciprocal",
                                                  zst2[0:33, :], zst2[0:33, :],
                                                  sem="vx")
                            emit_zbc_norm(7)

                            at_done = p.mark("hs")
                            for co in range(DB):
                                wb = 2 + (co % 2)
                                if co == 0:
                                    p.wait("tensor", "hs", at_done)
                                if co >= 2:
                                    p.wait("tensor", "epi",
                                           wo_epi[tt * DB + co - 2])
                                first = True
                                for wop, xop in ((wosb_h, at_h), (wosb_l, at_h),
                                                 (wosb_h, at_l)):
                                    for ci in range(DB):
                                        last = (wop is wosb_h and xop is at_l
                                                and ci == DB - 1)
                                        p.do("tensor", "matmul", pbank(wb),
                                             wop[:, (co * DB + ci) * 128:
                                                 (co * DB + ci + 1) * 128],
                                             xop[:, ci * T:(ci + 1) * T],
                                             start=first, stop=last,
                                             sem="mm" if last else None)
                                        first = False
                                wmm = p.mark("mm")
                                p.wait("vector", "mm", wmm)
                                cs = slice(co * T, (co + 1) * T)
                                wo_epi[tt * DB + co] = p.do(
                                    "vector", "scalar_tensor_tensor", Xf[:, cs],
                                    pbank(wb), bo[:, co:co + 1], Xf[:, cs],
                                    OP.add, OP.add, sem="epi")
                                p.wait("gpsimd", "epi", wo_epi[tt * DB + co])
                                wrm2[co] = p.do("gpsimd", "dma_start", sem="wr",
                                                amt=16,
                                                out=Xb[:, co, t0:t0 + T],
                                                in_=Xf[:, cs])
                        p.wait("gpsimd", "wr")
                p.flush_block("attn")

        if "ffn1" in phases:
            ffn_phase("xin", ("fm", Xa), "ffn1")
        if "attn" in phases:
            attn_phase()
        if "ffn2" in phases:
            src_ap = Xb if "attn" in phases else Xa
            ffn_phase(("fm", src_ap), "yout", "ffn2")

        if debug_out is not None:
            src_ap = Xa if debug_out == "Xa" else Xb
            with ExitStack() as std:
                dbg = std.enter_context(nc.sbuf_tensor("dbg", [128, NTOK], F32))
                for c in range(DB):
                    ldc = p.do("sync", "dma_start", sem="ld", amt=16,
                               out=dbg[:], in_=src_ap[:, c, :])
                    p.wait("gpsimd", "ld", ldc)
                    wc = p.do("gpsimd", "dma_start", sem="wr", amt=16,
                              out=y[:, c, :], in_=dbg[:])
                    p.wait("sync", "wr", wc)
                p.wait("gpsimd", "wr")
                p.flush_block("dbg")

    return nc


_NC_CACHE = {}


def _get_nc(phases, debug_out, taps=False, taps_full=False):
    key = (tuple(phases), debug_out, taps, taps_full)
    if key not in _NC_CACHE:
        _NC_CACHE[key] = _build_nc(phases, debug_out, taps, taps_full)
    return _NC_CACHE[key]


def _make_in_maps(inputs):
    x = np.ascontiguousarray(np.asarray(inputs["L0_attn_x"], dtype=np.float32))
    eye = np.eye(128, dtype=np.float32)
    ones_in = np.zeros((128, 2), dtype=np.float32)
    ones_in[:, 0] = 1.0
    e2 = np.zeros((2, 128), dtype=np.float32)
    e2[0, :] = 1.0
    zsel = np.zeros((2, 128), dtype=np.float32)
    zsel[0, 0:64] = 1.0
    zsel[1, 64:128] = 1.0
    common = {
        "w_qkv": np.ascontiguousarray(inputs["L0_attn_w_qkv"], dtype=np.float32),
        "proj": np.ascontiguousarray(inputs["L0_attn_proj_matrix"], dtype=np.float32),
        "w_out": np.ascontiguousarray(inputs["L0_attn_w_out"], dtype=np.float32),
        "b_out": np.ascontiguousarray(inputs["L0_attn_b_out"], dtype=np.float32),
        "w1": np.ascontiguousarray(inputs["L0_ffn_w1"], dtype=np.float32),
        "b1": np.ascontiguousarray(inputs["L0_ffn_b1"], dtype=np.float32),
        "w2": np.ascontiguousarray(inputs["L0_ffn_w2"], dtype=np.float32),
        "b2": np.ascontiguousarray(inputs["L0_ffn_b2"], dtype=np.float32),
        "eye32": eye, "eye16": eye, "ones_in": ones_in, "e2_in": e2,
        "zsel_in": zsel,
    }
    return [dict(common, x=np.ascontiguousarray(x[i])) for i in range(8)]


def kernel(**inputs):
    nc = _get_nc(ALL_PHASES, None)
    in_maps = _make_in_maps(inputs)
    res = run_bass_kernel_spmd(nc, in_maps, core_ids=list(range(8)))
    return np.stack([np.asarray(r["y"], dtype=np.float32) for r in res.results])
